# version 21
# baseline (speedup 1.0000x reference)
import contextlib
import numpy as np
import concourse.bass as bass
import concourse.mybir as mybir
from concourse.bass_utils import run_bass_kernel_spmd
from concourse.alu_op_type import AluOpType

F32 = mybir.dt.float32
BF16 = mybir.dt.bfloat16
I32 = mybir.dt.int32
U8 = mybir.dt.uint8
FP16 = mybir.dt.float16
FP8E3 = mybir.dt.float8e3
AF = mybir.ActivationFunctionType
OP = AluOpType
AX = mybir.AxisListType

NCORE = 8
TC = 2048
D = 2048
NCT = 16
NTT = 4
TW = 512
I5632 = 5632
NHT = 44
NKD = 48
HD_S = 0.08838834764831845
EPS_LN = 1e-5
EPS_ATT = 1e-9
EPS_FFN = 1e-6
DEBUG = False
PH_STOP = 0
SKIP_RAGGED = False
SKIP_WDRAG = False
SKIP_SIGMA = False
ESZ = {F32: 4, BF16: 2, I32: 4, U8: 1, FP16: 2, FP8E3: 1}


def _perm_d():
    perm = np.empty(D, np.int64)
    p = np.arange(128)
    for pt in range(2):
        for par in range(2):
            for k in range(4):
                m = 8 * pt + 4 * par + k
                perm[128 * m + p] = 1024 * pt + 8 * p + 2 * k + par
    return perm


def _perm_i():
    perm = np.empty(I5632, np.int64)
    p = np.arange(128)
    for n in range(40):
        pt, sl = divmod(n, 8)
        par, k = divmod(sl, 4)
        perm[128 * n + p] = 8 * (128 * pt + p) + 2 * k + par
    p64 = np.arange(64)
    for n2 in range(8):
        par, k = divmod(n2, 4)
        perm[5120 + 64 * n2 + p64] = 8 * (640 + p64) + 2 * k + par
    return perm


class _Prog:
    def __init__(self, nc, ctx):
        self.nc = nc
        self.ctx = ctx
        self._sems = {}
        self.cnt = {}
        self._hw = {}

    def sem(self, name):
        if name not in self._sems:
            self._sems[name] = self.ctx.enter_context(self.nc.semaphore(name))
            self.cnt[name] = 0
            self._hw[name] = {}
        return self._sems[name]

    def inc(self, instr, name, n=1):
        instr.then_inc(self.sem(name), n)
        self.cnt[name] += n
        return self.cnt[name]

    def vsync(self, eng, instr, name="fv"):
        c = self.inc(instr, name)
        self.wait(eng, name, c)
        return c

    def wait(self, eng, name, val=None, key=None):
        self.sem(name)
        if val is None:
            val = self.cnt[name]
        if val <= 0:
            return
        k = key if key is not None else str(getattr(eng, "engine", id(eng)))
        hw = self._hw[name]
        if hw.get(k, -1) >= val:
            return
        eng.wait_ge(self._sems[name], val)
        hw[k] = val


def build_program():
    nc = bass.Bass("TRN2", target_bir_lowering=False, debug=False)

    xT = nc.dram_tensor("xT", [D, TC], F32, kind="ExternalInput").ap()
    wqkvT = nc.dram_tensor("wqkvT", [D, 3 * D], F32, kind="ExternalInput").ap()
    woutT = nc.dram_tensor("woutT", [D, D], F32, kind="ExternalInput").ap()
    guT = nc.dram_tensor("guT", [256, 2 * I5632], I32, kind="ExternalInput").ap()
    sguT = nc.dram_tensor("sguT", [256, 2 * I5632], FP16, kind="ExternalInput").ap()
    dnT = nc.dram_tensor("dnT", [704, D], I32, kind="ExternalInput").ap()
    sdnT = nc.dram_tensor("sdnT", [704, D], FP16, kind="ExternalInput").ap()
    mask8 = nc.dram_tensor("mask8", [128, 8], F32, kind="ExternalInput").ap()
    y_out = nc.dram_tensor("y_out", [D, TC], F32, kind="ExternalOutput").ap()

    kind_dbg = "ExternalOutput" if DEBUG else "Internal"
    if DEBUG:
        xn_dbg = nc.dram_tensor("xn_dbg", [128, NCT * TC], BF16, kind="ExternalOutput").ap()
        mb_dbg = nc.dram_tensor("mb_dbg", [128, 2 * TC], F32, kind="ExternalOutput").ap()
        s12_dbg = nc.dram_tensor("s12_dbg", [1, 2 * TC], F32, kind="ExternalOutput").ap()
    h_scr = nc.dram_tensor("h_scr", [I5632, TC], BF16, kind=kind_dbg).ap()
    wd_scr = nc.dram_tensor("wd_scr", [I5632, D], BF16, kind=kind_dbg).ap()
    a1_scr = nc.dram_tensor("a1_scr", [D, TC], F32, kind=kind_dbg).ap()
    ag_in = [nc.dram_tensor(f"ag_in{j}", [128, 2], F32) for j in range(NCT)]
    ag_out = [nc.dram_tensor(f"ag_out{j}", [8 * 128, 2], F32) for j in range(NCT)]

    ctx = contextlib.ExitStack()
    with ctx:
        E = ctx.enter_context
        P = _Prog(nc, ctx)

        xn = E(nc.sbuf_tensor("xn", [128, NCT * TC], BF16))
        big2 = E(nc.sbuf_tensor("big2", [128, NCT * TC], BF16))
        arena = E(nc.sbuf_tensor("arena", [128, 40960], U8))
        wbuf = E(nc.sbuf_tensor("wbuf", [128, 5 * 2048], BF16))
        sct = E(nc.sbuf_tensor("sct", [128, 2 * 512], FP16))
        smalls = E(nc.sbuf_tensor("smalls", [128, 64], F32))
        onesr = E(nc.sbuf_tensor("onesr", [1, 128], F32))
        onesc = E(nc.sbuf_tensor("onesc", [128, 1], F32))
        onesb = E(nc.sbuf_tensor("onesb", [128, 1], BF16))
        cpool = E(nc.sbuf_tensor("cpool", [128, 8], F32))
        gath = E(nc.sbuf_tensor("gath", [128, NCT * 16], F32))
        sums = E(nc.sbuf_tensor("sums", [128, NCT * 2], F32))
        maskt = E(nc.sbuf_tensor("maskt", [128, 8], F32))
        wrag = E(nc.sbuf_tensor("wrag", [64, 4096], BF16))
        negone = E(nc.sbuf_tensor("negone", [128, 512], BF16))
        neghalf = E(nc.sbuf_tensor("neghalf", [1, 512], BF16))
        hrag = E(nc.sbuf_tensor("hrag", [64, 2048], BF16))

        ps = [E(nc.psum_tensor(f"ps{i}", [128, 512], F32)) for i in range(8)]

        def av(off, dt, n):
            return arena[:, off:off + n * ESZ[dt]].bitcast(dt)

        def xv(off_b, dt, n):
            e = ESZ[BF16]
            assert off_b % e == 0
            return xn[:, off_b // e:(off_b + n * ESZ[dt]) // e].bitcast(dt)

        def bv(off_b, dt, n):
            e = ESZ[BF16]
            return big2[:, off_b // e:(off_b + n * ESZ[dt]) // e].bitcast(dt)

        def wv_(off_b, dt, n):
            e = ESZ[BF16]
            return wbuf[:, off_b // e:(off_b + n * ESZ[dt]) // e].bitcast(dt)

        S = [8192 * i for i in range(5)]
        TMP_OFF = 40960

        blk = E(nc.Block())
        bank_free = [("v", 0)] * 8

        def mm_group(tensor, bank, lhs_list, rhs_list, inc_sem="t", gates=None):
            sname, sval = bank_free[bank]
            P.wait(tensor, sname, sval)
            n = len(lhs_list)
            mm = None
            for i, (l, r) in enumerate(zip(lhs_list, rhs_list)):
                if gates is not None:
                    P.wait(tensor, gates[i][0], gates[i][1])
                mm = tensor.matmul(ps[bank][:, :], l, r, start=(i == 0), stop=(i == n - 1))
            return P.inc(mm, inc_sem)

        st = {}

        @blk.vector
        def _(vector):
            vector.memset(onesr[:], 1.0)
            vector.memset(onesc[:], 1.0)
            vector.memset(onesb[:], 1.0)
            for i, v in enumerate([EPS_LN, -0.5, -1.0, EPS_ATT, EPS_FFN, 1.0 / 2048.0]):
                vector.memset(cpool[:, i:i + 1], v)
                nc.const_aps.register(F32, v, cpool[:, i:i + 1])
            m = vector.memset(cpool[:, 6:7], 0.0)
            P.inc(m, "v")

        @blk.sync
        def _(sync):
            P.inc(sync.dma_start(out=maskt[:], in_=mask8), "msk", 16)

        xslot = [av(S[i], F32, 2048) for i in range(3)]
        xbslot = [av(S[i], BF16, 2048) for i in range(3)]
        sqv = av(S[3], F32, 2048)
        sqvb = av(S[3], BF16, 2048)
        r0 = bv(8192, F32, 2048)[0:1, :]
        r1 = bv(16384, F32, 2048)[0:1, :]
        r2 = bv(24576, F32, 2048)[0:1, :]
        r3 = bv(49152, F32, 2048)[0:1, :]
        rtmp = bv(57344, F32, 2048)[0:1, :]
        Mb = bv(32768, F32, 2048)
        Rb = bv(40960, F32, 2048)
        xfree = [("v", 0)] * 3
        st["sq_used"] = 0

        for ct in range(NCT):
            sl = ct % 3

            @blk.gpsimd
            def _(gpsimd, ct=ct, sl=sl):
                sname, sval = xfree[sl]
                P.wait(gpsimd, sname, sval)
                P.inc(gpsimd.dma_start(out=xbslot[sl], in_=xT[128 * ct:128 * (ct + 1), :]), f"xg{sl}", 16)

            @blk.vector
            def _(vector, ct=ct, sl=sl):
                P.wait(vector, f"xg{sl}")
                P.wait(vector, "t", st["sq_used"])
                a = vector.tensor_tensor(out=sqvb, in0=xbslot[sl], in1=xbslot[sl], op=OP.mult)
                P.inc(a, "s")

            @blk.tensor
            def _(tensor, ct=ct, sl=sl):
                P.wait(tensor, "v", 1)
                P.wait(tensor, f"xg{sl}")
                x3 = xbslot[sl].rearrange("p (a b) -> p a b", a=NTT)
                for tc_ in range(NTT):
                    mm = tensor.matmul(ps[tc_][0:1, :], onesb[:], x3[:, tc_],
                                       start=(ct == 0), stop=(ct == NCT - 1))
                if ct == NCT - 1:
                    P.inc(mm, "t")
                P.wait(tensor, "s")
                sq3 = sqvb.rearrange("p (a b) -> p a b", a=NTT)
                for tc_ in range(NTT):
                    mm = tensor.matmul(ps[4 + tc_][0:1, :], onesb[:], sq3[:, tc_],
                                       start=(ct == 0), stop=(ct == NCT - 1))
                P.inc(mm, "t")
                st["sq_used"] = P.cnt["t"]

            xfree[sl] = ("t", P.cnt["t"])

        @blk.vector
        def _(vector):
            P.wait(vector, "t")
            v = None
            for tc_ in range(NTT):
                vector.tensor_copy(out=r0[:, 512 * tc_:512 * (tc_ + 1)], in_=ps[tc_][0:1, :])
                v = vector.tensor_copy(out=r1[:, 512 * tc_:512 * (tc_ + 1)], in_=ps[4 + tc_][0:1, :])
            P.vsync(vector, v)
            v = vector.tensor_scalar(out=r2, in0=r0, scalar1=1.0 / 2048.0, scalar2=None, op0=OP.mult)
            P.vsync(vector, v)
            v = vector.tensor_tensor(out=r3, in0=r2, in1=r2, op=OP.mult)
            P.vsync(vector, v)
            v = vector.scalar_tensor_tensor(out=r3, in0=r1, scalar=1.0 / 2048.0, in1=r3,
                                            op0=OP.mult, op1=OP.subtract)
            P.inc(v, "v")
        st["var"] = P.cnt["v"]

        @blk.scalar
        def _(scalar):
            P.wait(scalar, "v", st["var"])
            a0 = scalar.activation(out=rtmp, in_=r3, func=AF.Ln, bias=EPS_LN)
            P.vsync(scalar, a0, "fs")
            a = scalar.activation(out=r3, in_=rtmp, func=AF.Exp, scale=-0.5)
            P.inc(a, "s")
        st["rstd"] = P.cnt["s"]

        @blk.tensor
        def _(tensor):
            P.wait(tensor, "s", st["rstd"])
            for tc_ in range(NTT):
                mm = tensor.matmul(ps[tc_][:, :], onesr[:], r2[:, 512 * tc_:512 * (tc_ + 1)],
                                   start=True, stop=True)
                P.inc(mm, "t")
                mm = tensor.matmul(ps[4 + tc_][:, :], onesr[:], r3[:, 512 * tc_:512 * (tc_ + 1)],
                                   start=True, stop=True)
                P.inc(mm, "t")

        @blk.vector
        def _(vector):
            P.wait(vector, "t")
            for tc_ in range(NTT):
                vector.tensor_copy(out=Mb[:, 512 * tc_:512 * (tc_ + 1)], in_=ps[tc_][:, :])
                v = vector.tensor_copy(out=Rb[:, 512 * tc_:512 * (tc_ + 1)], in_=ps[4 + tc_][:, :])
            P.inc(v, "v")
        for b in range(8):
            bank_free[b] = ("v", P.cnt["v"])
        st["mbrb"] = P.cnt["v"]

        if DEBUG:
            @blk.gpsimd
            def _(gpsimd):
                P.wait(gpsimd, "v")
                P.inc(gpsimd.dma_start(out=mb_dbg[:, 0:TC], in_=Mb), "ds", 16)
                P.inc(gpsimd.dma_start(out=mb_dbg[:, TC:2 * TC], in_=Rb), "ds", 16)
                P.inc(gpsimd.dma_start(out=s12_dbg[0:1, 0:TC], in_=r0), "ds", 16)
                P.inc(gpsimd.dma_start(out=s12_dbg[0:1, TC:2 * TC], in_=r1), "ds", 16)

        if PH_STOP == 1:
            @blk.gpsimd
            def _(gpsimd):
                P.wait(gpsimd, "ds")
            return nc

        xfree = [("t", P.cnt["t"])] * 3
        for ct in range(NCT):
            sl = ct % 3

            @blk.sync
            def _(sync, ct=ct, sl=sl):
                sname, sval = xfree[sl]
                P.wait(sync, sname, sval)
                P.inc(sync.dma_start(out=xslot[sl], in_=xT[128 * ct:128 * (ct + 1), :]), f"xs{sl}", 16)

            @blk.vector
            def _(vector, ct=ct, sl=sl):
                P.wait(vector, f"xs{sl}")
                P.wait(vector, "v", st[("xn_t", ct - 1)] if ct else st["mbrb"])
                v = vector.tensor_tensor(out=sqv, in0=xslot[sl], in1=Mb, op=OP.subtract)
                P.vsync(vector, v)
                v = vector.tensor_tensor(out=xn[:, TC * ct:TC * (ct + 1)], in0=sqv, in1=Rb, op=OP.mult)
                P.inc(v, "v")
                st[("xn_t", ct)] = P.cnt["v"]

            xfree[sl] = ("v", P.cnt["v"])
        st["xn_ready"] = P.cnt["v"]
        if DEBUG:
            @blk.gpsimd
            def _(gpsimd):
                P.wait(gpsimd, "v", st["xn_ready"])
                P.inc(gpsimd.dma_start(out=xn_dbg, in_=xn[:, :]), "ds", 16)

        attnmix = big2
        kS, vS, cS, qS, dS = (av(S[i], F32, 2048) for i in range(5))
        emc = [av(S[4] + 2048 * tt, F32, 512) for tt in range(NTT)]
        wk_t = [wv_(s * 4096, BF16, 2048).rearrange("p (c o) -> p c o", c=16) for s in range(2)]
        wv_t = [wv_(8192 + s * 4096, BF16, 2048).rearrange("p (c o) -> p c o", c=16) for s in range(2)]
        wq_t = wv_(16384, BF16, 2048).rearrange("p (c o) -> p c o", c=16)
        wfree = [("v", 0), ("v", 0)]
        wqfree = ("v", 0)
        pd = {}
        xn3 = xn[:, :].rearrange("p (c t) -> p c t", c=NCT)
        amix3 = attnmix[:, :].rearrange("p (c t) -> p c t", c=NCT)

        def load_wkv(gpsimd, j):
            sl = j % 2
            sname, sval = wfree[sl]
            P.wait(gpsimd, sname, sval)
            src = wqkvT.rearrange("(c p) o -> p c o", p=128)
            P.inc(gpsimd.dma_start(out=wk_t[sl], in_=src[:, :, D + 128 * j:D + 128 * (j + 1)]), f"wk{sl}", 16)
            P.inc(gpsimd.dma_start(out=wv_t[sl], in_=src[:, :, 2 * D + 128 * j:2 * D + 128 * (j + 1)]), f"wvv{sl}", 16)
            pd[("wkv_ld", j)] = (P.cnt[f"wk{sl}"], P.cnt[f"wvv{sl}"])

        @blk.gpsimd
        def _(gpsimd):
            load_wkv(gpsimd, 0)
            src = wqkvT.rearrange("(c p) o -> p c o", p=128)
            P.inc(gpsimd.dma_start(out=wq_t, in_=src[:, :, 0:128]), "wq", 16)
            pd[("wq_ld", 0)] = P.cnt["wq"]

        for j in range(NCT):
            sl = j % 2

            @blk.tensor
            def _(tensor, j=j, sl=sl):
                P.wait(tensor, f"wk{sl}", pd[("wkv_ld", j)][0])
                P.wait(tensor, f"wvv{sl}", pd[("wkv_ld", j)][1])
                xn_gates = [("v", st[("xn_t", c)]) for c in range(16)]
                for tt in range(NTT):
                    mm_group(tensor, tt,
                             [wk_t[sl][:, c, :] for c in range(16)],
                             [xn3[:, c, TW * tt:TW * (tt + 1)] for c in range(16)], "t",
                             gates=xn_gates)
                    pd[("kmm", j, tt)] = P.cnt["t"]
                for tt in range(NTT):
                    mm_group(tensor, 4 + tt,
                             [wv_t[sl][:, c, :] for c in range(16)],
                             [xn3[:, c, TW * tt:TW * (tt + 1)] for c in range(16)], "t",
                             gates=xn_gates)
                    pd[("vmm", j, tt)] = P.cnt["t"]
            wfree[sl] = ("t", P.cnt["t"])

            @blk.vector
            def _(vector, j=j):
                for tt in range(NTT):
                    P.wait(vector, "t", pd[("kmm", j, tt)])
                    if j > 0:
                        P.wait(vector, "v", pd[("amix", j - 1)], key="self_amx")
                    v = vector.tensor_scalar(out=qS[:, TW * tt:TW * (tt + 1)], in0=ps[tt][:, :],
                                             scalar1=0.0, scalar2=None, op0=OP.min)
                    P.inc(v, "v")
                    pd[("mink", j, tt)] = P.cnt["v"]

            @blk.scalar
            def _(scalar, j=j):
                for tt in range(NTT):
                    P.wait(scalar, "v", pd[("mink", j, tt)])
                    a = scalar.activation(out=emc[tt], in_=qS[:, TW * tt:TW * (tt + 1)], func=AF.Exp)
                    P.inc(a, "s")
                    pd[("emk", j, tt)] = P.cnt["s"]

            @blk.vector
            def _(vector, j=j):
                for tt in range(NTT):
                    P.wait(vector, "s", pd[("emk", j, tt)])
                    v = vector.scalar_tensor_tensor(out=kS[:, TW * tt:TW * (tt + 1)], in0=ps[tt][:, :],
                                                    scalar=0.0, in1=emc[tt], op0=OP.max, op1=OP.add)
                    P.inc(v, "v")
                    pd[("phik", j, tt)] = P.cnt["v"]
                for tt in range(NTT):
                    P.wait(vector, "t", pd[("vmm", j, tt)])
                    v = vector.tensor_copy(out=vS[:, TW * tt:TW * (tt + 1)], in_=ps[4 + tt][:, :])
                    P.inc(v, "v")
                    pd[("vev", j, tt)] = P.cnt["v"]
                P.wait(vector, "v")
                v = vector.scalar_tensor_tensor(out=cS, in0=kS, scalar=1.0, in1=vS,
                                                op0=OP.mult, op1=OP.mult,
                                                accum_out=sums[:, 2 * j + 1:2 * j + 2])
                P.inc(v, "v")
                v = vector.reduce_sum(out=sums[:, 2 * j:2 * j + 1], in_=kS, axis=AX.X)
                P.inc(v, "v")
                pd[("sums", j)] = P.cnt["v"]
                P.wait(vector, "v")
                v = vector.tensor_tensor_scan(out=vS, data0=cS, data1=cS, initial=0.0,
                                              op0=OP.add, op1=OP.bypass)
                P.vsync(vector, v)
                v = vector.tensor_tensor_scan(out=cS, data0=kS, data1=kS, initial=0.0,
                                              op0=OP.add, op1=OP.bypass)
                P.inc(v, "v")
                pd[("scan2", j)] = P.cnt["v"]
            for tt in range(NTT):
                bank_free[tt] = ("v", pd[("phik", j, tt)])
                bank_free[4 + tt] = ("v", pd[("vev", j, tt)])

            @blk.gpsimd
            def _(gpsimd, j=j):
                if j + 1 < NCT:
                    load_wkv(gpsimd, j + 1)
                P.wait(gpsimd, "v", pd[("sums", j)])
                P.inc(gpsimd.dma_start(out=ag_in[j].ap(), in_=sums[:, 2 * j:2 * j + 2]), "agi", 16)
                P.wait(gpsimd, "agi")
                cc = gpsimd.collective_compute(
                    "AllGather", OP.bypass, replica_groups=[list(range(NCORE))],
                    ins=[ag_in[j].ap().opt()], outs=[ag_out[j].ap().opt()])
                P.inc(cc, "cc")
                P.wait(gpsimd, "cc")
                P.inc(gpsimd.dma_start(out=gath[:, 16 * j:16 * (j + 1)].rearrange("p (r c) -> p r c", r=8),
                                       in_=ag_out[j].ap().rearrange("(r p) c -> p r c", r=8)), "gld", 16)
                P.wait(gpsimd, "gld")

            @blk.tensor
            def _(tensor, j=j):
                P.wait(tensor, "wq", pd[("wq_ld", j)])
                for tt in range(NTT):
                    mm_group(tensor, tt,
                             [wq_t[:, c, :] for c in range(16)],
                             [xn3[:, c, TW * tt:TW * (tt + 1)] for c in range(16)], "t")
                    pd[("qmm", j, tt)] = P.cnt["t"]
            wqfree = ("t", P.cnt["t"])

            if j + 1 < NCT:
                @blk.gpsimd
                def _(gpsimd, j=j):
                    sname, sval = wqfree
                    P.wait(gpsimd, sname, sval)
                    src = wqkvT.rearrange("(c p) o -> p c o", p=128)
                    P.inc(gpsimd.dma_start(out=wq_t, in_=src[:, :, 128 * (j + 1):128 * (j + 2)]), "wq", 16)
                    pd[("wq_ld", j + 1)] = P.cnt["wq"]

            @blk.vector
            def _(vector, j=j):
                P.wait(vector, "v", pd[("scan2", j)])
                for tt in range(NTT):
                    P.wait(vector, "t", pd[("qmm", j, tt)])
                    v = vector.tensor_scalar(out=kS[:, TW * tt:TW * (tt + 1)], in0=ps[tt][:, :],
                                             scalar1=0.0, scalar2=None, op0=OP.min)
                    P.inc(v, "v")
                    pd[("minq", j, tt)] = P.cnt["v"]

            @blk.scalar
            def _(scalar, j=j):
                for tt in range(NTT):
                    P.wait(scalar, "v", pd[("minq", j, tt)])
                    a = scalar.activation(out=emc[tt], in_=kS[:, TW * tt:TW * (tt + 1)], func=AF.Exp)
                    P.inc(a, "s")
                    pd[("emq", j, tt)] = P.cnt["s"]

            @blk.vector
            def _(vector, j=j):
                for tt in range(NTT):
                    P.wait(vector, "s", pd[("emq", j, tt)])
                    v = vector.scalar_tensor_tensor(out=qS[:, TW * tt:TW * (tt + 1)], in0=ps[tt][:, :],
                                                    scalar=0.0, in1=emc[tt], op0=OP.max, op1=OP.add)
                    P.inc(v, "v")
                    pd[("phiq", j, tt)] = P.cnt["v"]
                    bank_free[tt] = ("v", P.cnt["v"])
                P.wait(vector, "gld", 16 * (j + 1))
                P.wait(vector, "msk", 16)
                P.wait(vector, "v")
                P.wait(vector, "fv")
                g3 = gath[:, 16 * j:16 * (j + 1)].rearrange("p (r c) -> p r c", r=8)
                m3 = maskt[:, :].unsqueeze(2).broadcast_to([128, 8, 2])
                prodb = smalls[:, 16:32]
                v = vector.scalar_tensor_tensor(out=prodb.rearrange("p (r c) -> p r c", r=8),
                                                in0=g3, scalar=1.0, in1=m3,
                                                op0=OP.mult, op1=OP.mult)
                P.vsync(vector, v)
                pb2 = prodb.rearrange("p (r c) -> p c r", c=2)
                vector.reduce_sum(out=smalls[:, 0:1], in_=pb2[:, 0], axis=AX.X)
                v = vector.reduce_sum(out=smalls[:, 1:2], in_=pb2[:, 1], axis=AX.X)
                P.vsync(vector, v)
                v = vector.scalar_tensor_tensor(out=kS, in0=cS, scalar=smalls[:, 0:1],
                                                in1=qS, op0=OP.add, op1=OP.mult)
                P.inc(v, "v")
                pd[("qk", j)] = P.cnt["v"]

            @blk.scalar
            def _(scalar, j=j):
                P.wait(scalar, "v", pd[("qk", j)])
                a0 = scalar.activation(out=dS, in_=kS, func=AF.Ln, bias=EPS_ATT)
                P.vsync(scalar, a0, "fs")
                a = scalar.activation(out=kS, in_=dS, func=AF.Exp, scale=-1.0)
                P.inc(a, "s")
                pd[("recip", j)] = P.cnt["s"]

            @blk.vector
            def _(vector, j=j):
                P.wait(vector, "v")
                P.wait(vector, "fv")
                v = vector.scalar_tensor_tensor(out=cS, in0=vS, scalar=smalls[:, 1:2],
                                                in1=qS, op0=OP.add, op1=OP.mult)
                P.vsync(vector, v)
                P.wait(vector, "s", pd[("recip", j)])
                v = vector.tensor_tensor(out=attnmix[:, TC * j:TC * (j + 1)], in0=cS, in1=kS, op=OP.mult)
                P.inc(v, "v")
                pd[("amix", j)] = P.cnt["v"]
        st["amix_ready"] = P.cnt["v"]
        if PH_STOP == 3:
            @blk.gpsimd
            def _(gpsimd):
                P.wait(gpsimd, "v")
            return nc

        wo_t = [wv_(s * 4096, BF16, 2048).rearrange("p (c o) -> p c o", c=16) for s in range(2)]
        wo_flat = [wv_(s * 4096, BF16, 2048) for s in range(2)]
        wo_stage = [av(S[2], F32, 2048), av(S[3], F32, 2048)]
        wofree = [("t", P.cnt["t"]), ("t", P.cnt["t"])]
        wsfree = [("v", st["amix_ready"]), ("v", st["amix_ready"])]
        xo = [av(S[0], F32, 512), av(S[0] + 2048, F32, 512)]
        a1b = [av(S[1], F32, 512), av(S[1] + 2048, F32, 512)]
        xofree = [("v", st["amix_ready"])] * 2
        a1free = [("as0", 0), ("as1", 0)]

        for ot in range(NCT):
            sl = ot % 2

            @blk.sync
            def _(sync, ot=ot, sl=sl):
                sname, sval = wsfree[sl]
                P.wait(sync, sname, sval)
                src = woutT.rearrange("(c p) o -> p c o", p=128)
                P.inc(sync.dma_start(out=wo_stage[sl].rearrange("p (c o) -> p c o", c=16),
                                     in_=src[:, :, 128 * ot:128 * (ot + 1)]), f"wof{sl}", 16)
            wof_now = P.cnt[f"wof{sl}"]

            @blk.vector
            def _(vector, sl=sl, wof_now=wof_now):
                P.wait(vector, f"wof{sl}", wof_now)
                sname, sval = wofree[sl]
                P.wait(vector, sname, sval)
                v = vector.tensor_copy(out=wo_flat[sl], in_=wo_stage[sl])
                P.inc(v, "v")
            ldw_now = P.cnt["v"]
            wsfree[sl] = ("v", ldw_now)

            for tt in range(NTT):
                i = ot * NTT + tt
                bsl = i % 8
                xsl = i % 2

                @blk.sync
                def _(sync, ot=ot, tt=tt, xsl=xsl):
                    sname, sval = xofree[xsl]
                    P.wait(sync, sname, sval)
                    P.inc(sync.dma_start(out=xo[xsl],
                                         in_=xT[128 * ot:128 * (ot + 1), TW * tt:TW * (tt + 1)]), f"xs{xsl}", 16)
                x_ld = P.cnt[f"xs{xsl}"]

                @blk.tensor
                def _(tensor, ot=ot, tt=tt, sl=sl, bsl=bsl, ldw_now=ldw_now):
                    P.wait(tensor, "v", ldw_now)
                    P.wait(tensor, "v", st["amix_ready"])
                    mm_group(tensor, bsl,
                             [wo_t[sl][:, c, :] for c in range(16)],
                             [amix3[:, c, TW * tt:TW * (tt + 1)] for c in range(16)], "t")
                mm_now = P.cnt["t"]

                @blk.vector
                def _(vector, bsl=bsl, xsl=xsl, i=i, mm_now=mm_now, x_ld=x_ld):
                    P.wait(vector, "t", mm_now)
                    P.wait(vector, f"xs{xsl}", x_ld)
                    asl = i % 2
                    sname, sval = a1free[asl]
                    P.wait(vector, sname, sval)
                    v = vector.scalar_tensor_tensor(out=a1b[asl], in0=ps[bsl][:, :], scalar=HD_S,
                                                    in1=xo[xsl], op0=OP.mult, op1=OP.add)
                    P.inc(v, "v")
                bank_free[bsl] = ("v", P.cnt["v"])
                xofree[xsl] = ("v", P.cnt["v"])
                a1_rdy = P.cnt["v"]

                @blk.gpsimd
                def _(gpsimd, ot=ot, tt=tt, i=i, a1_rdy=a1_rdy):
                    asl = i % 2
                    P.wait(gpsimd, "v", a1_rdy)
                    P.inc(gpsimd.dma_start(out=a1_scr[128 * ot:128 * (ot + 1), TW * tt:TW * (tt + 1)],
                                           in_=a1b[asl]), f"as{i % 2}", 16)
                a1free[i % 2] = (f"as{i % 2}", P.cnt[f"as{i % 2}"])
            wofree[sl] = ("t", P.cnt["t"])
        st["a1_spills"] = (P.cnt["as0"], P.cnt["as1"])
        st["ph4_t"] = P.cnt["t"]
        st["ph4_v"] = P.cnt["v"]

        wg = bv(0, BF16, 8192).rearrange("p (s o) -> p s o", s=16)
        wu = bv(16384, BF16, 8192).rearrange("p (s o) -> p s o", s=16)
        pkt = av(0, I32, 512)
        dq1 = av(2048, I32, 512)
        dq2 = av(4096, I32, 512)
        dce = av(6144, I32, 512)
        dco = av(8192, I32, 512)
        gsil = av(10240, BF16, 512)
        hbf = [av(11264 + 1024 * k, BF16, 512) for k in range(4)]
        hfree = [(f"hs{i}", 0) for i in range(4)]
        gd = {"pk_free": st["ph4_v"], "sc_free0": st["ph4_v"], "sc_free1": st["ph4_v"]}

        @blk.vector
        def _(vector):
            P.wait(vector, "as0", st["a1_spills"][0])
            P.wait(vector, "as1", st["a1_spills"][1])
            P.wait(vector, "v", st["ph4_v"])

        def emit_dequant(src_pk, src_sc, pt, col0, ncols, dst3, ssl, npart=128):
            row0 = 128 * pt

            @blk.sync
            def _(sync):
                P.wait(sync, "v", gd["pk_free"])
                P.inc(sync.dma_start(out=pkt[0:npart, 0:ncols],
                                     in_=src_pk[row0:row0 + npart, col0:col0 + ncols]), "pk", 16)
                P.wait(sync, "v", gd[f"sc_free{ssl}"])
                P.inc(sync.dma_start(out=sct[0:npart, 512 * ssl:512 * ssl + ncols],
                                     in_=src_sc[row0:row0 + npart, col0:col0 + ncols]), f"sc{ssl}", 16)

            @blk.vector
            def _(vector):
                P.wait(vector, "pk")
                P.wait(vector, f"sc{ssl}")
                P.wait(vector, "v", gd.get("last_deq", 0))
                pk = pkt[0:npart, 0:ncols]
                d1 = dq1[0:npart, 0:ncols]
                d2 = dq2[0:npart, 0:ncols]
                ce = dce[0:npart, 0:ncols]
                co = dco[0:npart, 0:ncols]
                vector.tensor_scalar(out=d1, in0=pk, scalar1=0x07070707, scalar2=3,
                                     op0=OP.bitwise_and, op1=OP.logical_shift_left)
                v = vector.tensor_scalar(out=d2, in0=pk, scalar1=0x08080808, scalar2=4,
                                         op0=OP.bitwise_and, op1=OP.logical_shift_left)
                P.vsync(vector, v)
                v = vector.tensor_tensor(out=ce, in0=d1, in1=d2, op=OP.bitwise_or)
                P.vsync(vector, v)
                vector.tensor_scalar(out=d1, in0=pk, scalar1=0x70707070, scalar2=1,
                                     op0=OP.bitwise_and, op1=OP.logical_shift_right)
                v = vector.tensor_scalar(out=d2, in0=pk, scalar1=0x80808080 - 2**32, scalar2=None,
                                         op0=OP.bitwise_and)
                P.vsync(vector, v)
                v = vector.tensor_tensor(out=co, in0=d1, in1=d2, op=OP.bitwise_or)
                gd["pk_free"] = P.inc(v, "v")
                P.wait(vector, "v")
                scv = sct[0:npart, 512 * ssl:512 * ssl + ncols]
                for par, csrc in ((0, ce), (1, co)):
                    fp8 = csrc.bitcast(FP8E3).rearrange("p (o k) -> p k o", k=4)
                    for k in range(4):
                        v = vector.scalar_tensor_tensor(out=dst3[0:npart, 4 * par + k, 0:ncols], in0=fp8[:, k],
                                                        scalar=4.0, in1=scv, op0=OP.mult, op1=OP.mult)
                gd[f"sc_free{ssl}"] = P.inc(v, "v")
                gd["last_deq"] = P.cnt["v"]
            return P.cnt["v"]

        wg_sl = [bv(0, BF16, 8192).rearrange("p (s o) -> p s o", s=16),
                 bv(32768, BF16, 8192).rearrange("p (s o) -> p s o", s=16)]
        wu_sl = [bv(16384, BF16, 8192).rearrange("p (s o) -> p s o", s=16),
                 bv(49152, BF16, 8192).rearrange("p (s o) -> p s o", s=16)]
        wdq_sl = [av(16384, BF16, 4096).rearrange("p (s o) -> p s o", s=8),
                  av(24576, BF16, 4096).rearrange("p (s o) -> p s o", s=8)]
        gsil2 = [gsil, av(15360, BF16, 512)]

        NCH = NHT // 4
        chunk_ready = {}
        chunk_used = {-1: st["ph4_t"], 0: st["ph4_t"]}
        dn_state = {"k": 0}

        def emit_gate_chunk(cidx):
            slq = cidx % 2
            col0 = 512 * cidx

            @blk.vector
            def _(vector):
                P.wait(vector, "t", chunk_used.get(cidx - 2, 0))
            for pt in range(2):
                emit_dequant(guT, sguT, pt, col0, 512, wg_sl[slq][:, 8 * pt:8 * pt + 8], 0)
                emit_dequant(guT, sguT, pt, I5632 + col0, 512, wu_sl[slq][:, 8 * pt:8 * pt + 8], 1)
            chunk_ready[cidx] = P.cnt["v"]

        def emit_dn_chunk():
            k = dn_state["k"]
            if k >= 24:
                return False
            pt, ch = divmod(k, 4)
            npart = 128 if pt < 5 else 64
            wsl = k % 2

            @blk.vector
            def _(vector):
                P.wait(vector, f"ws{wsl}", 128 * (k // 2))
            emit_dequant(dnT, sdnT, pt, 512 * ch, 512, wdq_sl[wsl], 0, npart=npart)
            dn_rdy = P.cnt["v"]

            @blk.gpsimd
            def _(gpsimd, pt=pt, ch=ch, npart=npart, dn_rdy=dn_rdy, wsl=wsl):
                P.wait(gpsimd, "v", dn_rdy)
                for s_ in range(8):
                    r0_ = 128 * (8 * pt + s_) if pt < 5 else 5120 + 64 * s_
                    P.inc(gpsimd.dma_start(out=wd_scr[r0_:r0_ + npart, 512 * ch:512 * (ch + 1)],
                                           in_=wdq_sl[wsl][0:npart, s_, :]), f"ws{wsl}", 16)
            dn_state["k"] = k + 1
            return True

        emit_gate_chunk(0)
        for m in range(NHT):
            cidx = m // 4
            slq = cidx % 2
            osub = m % 4
            wg = wg_sl[slq]
            wu = wu_sl[slq]
            if osub == 0 and cidx + 1 < NCH:
                emit_gate_chunk(cidx + 1)

            for tt in range(NTT):
                i = m * NTT + tt
                gb = i % 4
                ub = 4 + i % 4

                @blk.tensor
                def _(tensor, m=m, tt=tt, gb=gb, ub=ub, osub=osub, wg=wg, wu=wu, cidx=cidx):
                    P.wait(tensor, "v", chunk_ready[cidx])
                    mm_group(tensor, gb,
                             [wg[:, s, 128 * osub:128 * (osub + 1)] for s in range(16)],
                             [xn3[:, s, TW * tt:TW * (tt + 1)] for s in range(16)], "t")
                    gd[("gmm", m, tt)] = P.cnt["t"]
                    mm_group(tensor, ub,
                             [wu[:, s, 128 * osub:128 * (osub + 1)] for s in range(16)],
                             [xn3[:, s, TW * tt:TW * (tt + 1)] for s in range(16)], "t")
                    gd[("umm", m, tt)] = P.cnt["t"]
                chunk_used[cidx] = P.cnt["t"]

                @blk.scalar
                def _(scalar, m=m, tt=tt, gb=gb, i=i):
                    P.wait(scalar, "t", gd[("gmm", m, tt)])
                    P.wait(scalar, "v", gd.get(("gsil_free", i - 2), 0))
                    a = scalar.activation(out=gsil2[i % 2], in_=ps[gb][:, :], func=AF.Silu)
                    P.inc(a, "s")
                    gd[("gsil", m, tt)] = P.cnt["s"]

                @blk.vector
                def _(vector, m=m, tt=tt, ub=ub, i=i):
                    hsl = i % 4
                    P.wait(vector, "s", gd[("gsil", m, tt)])
                    P.wait(vector, "t", gd[("umm", m, tt)])
                    sname, sval = hfree[hsl]
                    P.wait(vector, sname, sval)
                    v = vector.tensor_tensor(out=hbf[hsl], in0=ps[ub][:, :], in1=gsil2[i % 2], op=OP.mult)
                    P.inc(v, "v")
                    gd[("h", m, tt)] = P.cnt["v"]
                    gd[("gsil_free", i)] = P.cnt["v"]
                bank_free[gb] = ("v", P.cnt["v"])
                bank_free[ub] = ("v", P.cnt["v"])

                @blk.gpsimd
                def _(gpsimd, m=m, tt=tt, i=i):
                    hsl = i % 4
                    P.wait(gpsimd, "v", gd[("h", m, tt)])
                    P.inc(gpsimd.dma_start(out=h_scr[128 * m:128 * (m + 1), TW * tt:TW * (tt + 1)],
                                           in_=hbf[hsl]), f"hs{hsl}", 16)
                hfree[i % 4] = (f"hs{i % 4}", P.cnt[f"hs{i % 4}"])

                if i % 2 == 1:
                    emit_dn_chunk()
        while emit_dn_chunk():
            pass
        st["xn_dead"] = P.cnt["t"]
        st["ph6_v"] = P.cnt["v"]
        if PH_STOP == 5:
            @blk.gpsimd
            def _(gpsimd):
                P.wait(gpsimd, "v")
                P.wait(gpsimd, "t")
                for i4 in range(4):
                    P.wait(gpsimd, f"hs{i4}")
                P.wait(gpsimd, "ws0")
                P.wait(gpsimd, "ws1")
            return nc

        y_d = big2
        yd3 = y_d[:, :].rearrange("p (o t) -> p o t", o=16)
        hb3 = xv(0, BF16, 22528).rearrange("p (m t) -> p m t", m=NHT)
        wd_sb = [xv(45056 + s * 10240, BF16, 5120).rearrange("p (n o) -> p n o", n=40) for s in range(2)]
        wrag_e = [wrag[:, 2048 * s:2048 * s + 512].rearrange("p (q o) -> p q o", q=4) for s in range(2)]
        wrag_o = [wrag[:, 2048 * s + 512:2048 * s + 1024].rearrange("p (q o) -> p q o", q=4) for s in range(2)]
        hrag3 = hrag[:, :].rearrange("p (q t) -> p q t", q=4)
        sqv2 = [av(0, BF16, 512), av(1024, BF16, 512)]
        alt_q0 = av(4096, BF16, 5632).rearrange("p (m t) -> p m t", m=11)
        f0 = av(18432, F32, 2048)[0:1, :]
        f1 = av(26624, F32, 2048)[0:1, :]
        rfb = wv_(0, F32, 2048)
        dn = {}
        rms = {}
        wdfree = [("t", 0)] * 2

        for tch in range(NTT):
            @blk.sync
            def _(sync, tch=tch):
                P.wait(sync, "t", dn.get(("h_used", tch - 1), 0))
                for i4 in range(4):
                    P.wait(sync, f"hs{i4}")
                P.wait(sync, "ws0")
                P.wait(sync, "ws1")
                P.wait(sync, "t", st["xn_dead"])
                src = h_scr.rearrange("(m p) t -> p m t", p=128)
                for half in range(1 if tch else 0, 4):
                    P.inc(sync.dma_start(out=hb3[:, 11 * half:11 * (half + 1), :],
                                         in_=src[:, 11 * half:11 * (half + 1), TW * tch:TW * (tch + 1)]),
                          f"hq{half}", 16)
                    dn[("hqs", tch, half)] = (f"hq{half}", P.cnt[f"hq{half}"])
            if tch == 0:
                dn[("hqs", 0, 0)] = ("hq0", P.cnt["hq0"])
            dn[("q0v", tch)] = alt_q0 if tch % 2 else hb3[:, 0:11, :]

            if tch + 1 < NTT:
                @blk.sync
                def _(sync, tch=tch):
                    nxt = tch + 1
                    if nxt % 2:
                        P.wait(sync, "v", st["ph6_v"])
                        if nxt >= 3:
                            P.wait(sync, "t", dn[("h_used", nxt - 2)])
                        dst = alt_q0
                    else:
                        P.wait(sync, "t", dn[("h_used", nxt - 2)])
                        dst = hb3[:, 0:11, :]
                    src = h_scr.rearrange("(m p) t -> p m t", p=128)
                    P.inc(sync.dma_start(out=dst, in_=src[:, 0:11, TW * nxt:TW * (nxt + 1)]), "hqp", 16)
                    dn[("hqs", nxt, 0)] = ("hqp", P.cnt["hqp"])
            @blk.scalar
            def _(scalar, tch=tch):
                P.wait(scalar, "t", dn.get(("h_used", tch - 1), 0))
                for i4 in range(4):
                    P.wait(scalar, f"hs{i4}")
                src2 = h_scr.rearrange("(n p) t -> p n t", p=64)
                P.inc(scalar.dma_start(out=hrag3, in_=src2[:, 81:88:2, TW * tch:TW * (tch + 1)]), "hrg", 16)
                dn[("hrg", tch)] = P.cnt["hrg"]

            for oc in range(16):
                sl = oc % 2
                b_ = oc % 6

                @blk.scalar
                def _(scalar, tch=tch, oc=oc, sl=sl):
                    sname, sval = wdfree[sl]
                    P.wait(scalar, sname, sval)
                    P.wait(scalar, "t", st["xn_dead"])
                    P.wait(scalar, "ws0")
                    P.wait(scalar, "ws1")
                    src = wd_scr.rearrange("(n p) o -> p n o", p=128)
                    P.inc(scalar.dma_start(out=wd_sb[sl][:, :, :],
                                         in_=src[:, 0:40, 128 * oc:128 * (oc + 1)]), f"wds{sl}", 16)
                    src2 = wd_scr.rearrange("(n p) o -> p n o", p=64)
                    P.inc(scalar.dma_start(out=wrag_e[sl],
                                         in_=src2[:, 80:88:2, 128 * oc:128 * (oc + 1)]), f"wds{sl}", 16)
                    P.inc(scalar.dma_start(out=wrag_o[sl],
                                         in_=src2[:, 81:88:2, 128 * oc:128 * (oc + 1)]), f"wds{sl}", 16)
                wd_ld = P.cnt[f"wds{sl}"]

                @blk.tensor
                def _(tensor, tch=tch, oc=oc, sl=sl, b_=b_):
                    P.wait(tensor, f"wds{sl}", P.cnt[f"wds{sl}"])
                    sname, sval = bank_free[b_]
                    P.wait(tensor, sname, sval)
                    mm = None
                    q0v = dn[("q0v", tch)]
                    for n in range(40):
                        q_ = min(n // 11, 3)
                        sname_h, sval_h = dn[("hqs", tch, q_)]
                        P.wait(tensor, sname_h, sval_h)
                        rhs = q0v[:, n, :] if n < 11 else hb3[:, n, :]
                        mm = tensor.matmul(ps[b_][:, :], wd_sb[sl][:, n, :], rhs,
                                           start=(n == 0), stop=False)
                    P.wait(tensor, "hrg", dn[("hrg", tch)])
                    for q in range(4):
                        tensor.matmul(ps[b_][:, :], wrag_e[sl][:, q, :], hb3[0:64, 40 + q, :],
                                      start=False, stop=False)
                        mm = tensor.matmul(ps[b_][:, :], wrag_o[sl][:, q, :], hrag3[:, q, :],
                                           start=False, stop=(q == 3))
                    P.inc(mm, "t")
                    dn[("mm", tch, oc)] = P.cnt["t"]
                    dn[("h_used", tch)] = P.cnt["t"]
                wdfree[sl] = ("t", P.cnt["t"])

                @blk.vector
                def _(vector, tch=tch, oc=oc, b_=b_):
                    P.wait(vector, "t", dn[("mm", tch, oc)])
                    v = vector.tensor_copy(out=yd3[:, oc, TW * tch:TW * (tch + 1)], in_=ps[b_][:, :])
                    P.inc(v, "v")
                    dn[("ev", tch, oc)] = P.cnt["v"]
                bank_free[b_] = ("v", P.cnt["v"])




        st["yd_ready"] = P.cnt["v"]

        sqb2 = [av(0, BF16, 2048), av(34816, BF16, 2048)]
        for ot in range(16):
            @blk.vector
            def _(vector, ot=ot):
                P.wait(vector, "v", st["yd_ready"], key="self_yd")
                P.wait(vector, "t", rms.get(("sq_used", ot - 2), 0))
                a = vector.tensor_tensor(out=sqb2[ot % 2], in0=yd3[:, ot, :], in1=yd3[:, ot, :], op=OP.mult)
                P.inc(a, "s")
                rms[("sq", ot)] = P.cnt["s"]

            @blk.tensor
            def _(tensor, ot=ot):
                P.wait(tensor, "s", rms[("sq", ot)])
                if ot == 0:
                    for b4 in range(4):
                        sname, sval = bank_free[b4]
                        P.wait(tensor, sname, sval)
                sq3 = sqb2[ot % 2].rearrange("p (a b) -> p a b", a=NTT)
                mm = None
                for tc_ in range(NTT):
                    mm = tensor.matmul(ps[tc_][0:1, :], onesb[:], sq3[:, tc_],
                                       start=(ot == 0), stop=(ot == 15))
                P.inc(mm, "t")
                rms[("sq_used", ot)] = P.cnt["t"]

        @blk.vector
        def _(vector):
            P.wait(vector, "t")
            P.wait(vector, "ws0")
            P.wait(vector, "ws1")
            v = None
            for tc_ in range(NTT):
                v = vector.tensor_copy(out=f0[:, 512 * tc_:512 * (tc_ + 1)], in_=ps[tc_][0:1, :])
            P.inc(v, "v")
        for b4 in range(4):
            bank_free[b4] = ("v", P.cnt["v"])
        rms_s = P.cnt["v"]
        if PH_STOP == 7:
            @blk.gpsimd
            def _(gpsimd):
                P.wait(gpsimd, "v")
                P.wait(gpsimd, "t")
            return nc
        st["dn_done"] = P.cnt["t"]

        @blk.scalar
        def _(scalar):
            P.wait(scalar, "v", rms_s)
            a0 = scalar.activation(out=f1, in_=f0, func=AF.Ln, scale=1.0 / 2048.0, bias=EPS_FFN)
            P.vsync(scalar, a0, "fs")
            a = scalar.activation(out=f0, in_=f1, func=AF.Exp, scale=-0.5)
            P.inc(a, "s")
        rf_row = P.cnt["s"]

        rfbe = {}
        for tc_ in range(NTT):
            @blk.tensor
            def _(tensor, tc_=tc_):
                if tc_ == 0:
                    P.wait(tensor, "s", rf_row)
                b = 4 + tc_ % 2
                if tc_ >= 2:
                    P.wait(tensor, "v", rfbe[tc_ - 2])
                else:
                    sname, sval = bank_free[b]
                    P.wait(tensor, sname, sval)
                mm = tensor.matmul(ps[b][:, :], onesr[:], f0[:, 512 * tc_:512 * (tc_ + 1)],
                                   start=True, stop=True)
                P.inc(mm, "t")
                rms[("rfb", tc_)] = P.cnt["t"]

            @blk.vector
            def _(vector, tc_=tc_):
                P.wait(vector, "t", rms[("rfb", tc_)])
                v = vector.tensor_copy(out=rfb[:, 512 * tc_:512 * (tc_ + 1)], in_=ps[4 + tc_ % 2][:, :])
                rfbe[tc_] = P.inc(v, "v")
        rfb_done = P.cnt["v"]

        a1l = [av(4096 + k * 2048, F32, 512) for k in range(3)]
        yob = [av(10240 + k * 2048, F32, 512) for k in range(3)]
        t2b = av(16384, F32, 512)
        t2g = av(38912, F32, 512)
        a1lfree = [("v", 0)] * 3
        yofree = [(f"ys{i}", 0) for i in range(3)]

        for ot in range(16):
            for tt in range(NTT):
                i = ot * NTT + tt
                asl = i % 3
                ysl = i % 3
                eng_v = (i % 2 == 0)

                @blk.sync
                def _(sync, ot=ot, tt=tt, asl=asl):
                    sname, sval = a1lfree[asl]
                    P.wait(sync, sname, sval)
                    P.wait(sync, "as0", st["a1_spills"][0])
                    P.wait(sync, "as1", st["a1_spills"][1])
                    P.wait(sync, "t", st["dn_done"])
                    P.inc(sync.dma_start(out=a1l[asl],
                                         in_=a1_scr[128 * ot:128 * (ot + 1), TW * tt:TW * (tt + 1)]), f"xs{asl}", 16)
                a1_ld = P.cnt[f"xs{asl}"]

                if eng_v:
                    @blk.vector
                    def _(vector, ot=ot, tt=tt, asl=asl, ysl=ysl, a1_ld=a1_ld):
                        P.wait(vector, "v", rfb_done, key="self_rfb")
                        P.wait(vector, "v")
                        P.wait(vector, "fv")
                        P.wait(vector, f"xs{asl}", a1_ld)
                        sname, sval = yofree[ysl]
                        P.wait(vector, sname, sval)
                        v = vector.tensor_tensor(out=t2b, in0=yd3[:, ot, TW * tt:TW * (tt + 1)],
                                                 in1=rfb[:, TW * tt:TW * (tt + 1)], op=OP.mult)
                        P.vsync(vector, v)
                        v = vector.tensor_tensor(out=yob[ysl], in0=t2b, in1=a1l[asl], op=OP.add)
                        P.inc(v, "v")
                    a1lfree[asl] = ("v", P.cnt["v"])
                    yo_rdy = ("v", P.cnt["v"])
                else:
                    @blk.gpsimd
                    def _(gpsimd, ot=ot, tt=tt, asl=asl, ysl=ysl, a1_ld=a1_ld):
                        P.wait(gpsimd, "v", rfb_done, key="gp_rfb")
                        P.wait(gpsimd, "g")
                        P.wait(gpsimd, "fg")
                        P.wait(gpsimd, f"xs{asl}", a1_ld)
                        sname, sval = yofree[ysl]
                        P.wait(gpsimd, sname, sval)
                        v = gpsimd.tensor_tensor(out=t2g, in0=yd3[:, ot, TW * tt:TW * (tt + 1)],
                                                 in1=rfb[:, TW * tt:TW * (tt + 1)], op=OP.mult)
                        P.vsync(gpsimd, v, "fg")
                        v = gpsimd.tensor_tensor(out=yob[ysl], in0=t2g, in1=a1l[asl], op=OP.add)
                        P.inc(v, "g")
                    a1lfree[asl] = ("g", P.cnt["g"])
                    yo_rdy = ("g", P.cnt["g"])

                @blk.scalar
                def _(scalar, ot=ot, tt=tt, ysl=ysl, yo_rdy=yo_rdy):
                    P.wait(scalar, yo_rdy[0], yo_rdy[1])
                    P.inc(scalar.dma_start(out=y_out[128 * ot:128 * (ot + 1), TW * tt:TW * (tt + 1)],
                                           in_=yob[ysl]), f"ys{ysl}", 16)
                yofree[ysl] = (f"ys{ysl}", P.cnt[f"ys{ysl}"])

        @blk.gpsimd
        def _(gpsimd):
            for i3 in range(3):
                P.wait(gpsimd, f"ys{i3}")
            if DEBUG:
                P.wait(gpsimd, "ds")

    return nc


_CACHE = {}


def kernel(L0_attn_x, L0_attn_W_qkv, L0_attn_W_out, L0_ffn_gate_up_packed,
           L0_ffn_gate_up_scales, L0_ffn_down_packed, L0_ffn_down_scales):
    x = np.asarray(L0_attn_x)
    wqkv = np.asarray(L0_attn_W_qkv)
    wout = np.asarray(L0_attn_W_out)
    gup = np.asarray(L0_ffn_gate_up_packed)
    gus = np.asarray(L0_ffn_gate_up_scales)
    dnp = np.asarray(L0_ffn_down_packed)
    dns = np.asarray(L0_ffn_down_scales)

    perm = _perm_d()
    iperm = _perm_i()
    inv = np.argsort(perm)

    wqkvT = np.ascontiguousarray(wqkv.T[perm], dtype=np.float32)
    woutT = np.ascontiguousarray(wout.T[:, perm], dtype=np.float32)
    row_order = np.concatenate([iperm, iperm + I5632])
    guT = np.ascontiguousarray(gup[row_order].T)
    sguT = np.ascontiguousarray(np.repeat(gus[:, row_order], 16, axis=0))
    dnT = np.ascontiguousarray(dnp[perm].T)
    sdnT = np.ascontiguousarray(np.repeat(dns[:, perm], 16, axis=0))

    if "nc" not in _CACHE:
        _CACHE["nc"] = build_program()
    nc = _CACHE["nc"]

    in_maps = []
    for c in range(NCORE):
        xc = np.ascontiguousarray(x[0, TC * c:TC * (c + 1), :].T[perm], dtype=np.float32)
        m8 = np.zeros((128, 8), np.float32)
        m8[:, :c] = 1.0
        in_maps.append({
            "xT": xc, "wqkvT": wqkvT, "woutT": woutT, "guT": guT, "sguT": sguT,
            "dnT": dnT, "sdnT": sdnT, "mask8": m8,
        })
    res = run_bass_kernel_spmd(nc, in_maps, core_ids=list(range(NCORE)))
    out = np.empty((1, NCORE * TC, D), np.float32)
    for c in range(NCORE):
        yc = res.results[c]["y_out"]
        out[0, TC * c:TC * (c + 1), :] = yc[inv].T
    return out


# version 22
# speedup vs baseline: 1.0105x; 1.0105x over previous
import contextlib
import numpy as np
import concourse.bass as bass
import concourse.mybir as mybir
from concourse.bass_utils import run_bass_kernel_spmd
from concourse.alu_op_type import AluOpType

F32 = mybir.dt.float32
BF16 = mybir.dt.bfloat16
I32 = mybir.dt.int32
U8 = mybir.dt.uint8
FP16 = mybir.dt.float16
FP8E3 = mybir.dt.float8e3
AF = mybir.ActivationFunctionType
OP = AluOpType
AX = mybir.AxisListType

NCORE = 8
TC = 2048
D = 2048
NCT = 16
NTT = 4
TW = 512
I5632 = 5632
NHT = 44
NKD = 48
HD_S = 0.08838834764831845
EPS_LN = 1e-5
EPS_ATT = 1e-9
EPS_FFN = 1e-6
DEBUG = False
PH_STOP = 0
SKIP_RAGGED = False
SKIP_WDRAG = False
SKIP_SIGMA = False
ESZ = {F32: 4, BF16: 2, I32: 4, U8: 1, FP16: 2, FP8E3: 1}


def _perm_d():
    perm = np.empty(D, np.int64)
    p = np.arange(128)
    for pt in range(2):
        for par in range(2):
            for k in range(4):
                m = 8 * pt + 4 * par + k
                perm[128 * m + p] = 1024 * pt + 8 * p + 2 * k + par
    return perm


def _perm_i():
    perm = np.empty(I5632, np.int64)
    p = np.arange(128)
    for n in range(40):
        pt, sl = divmod(n, 8)
        par, k = divmod(sl, 4)
        perm[128 * n + p] = 8 * (128 * pt + p) + 2 * k + par
    p64 = np.arange(64)
    for n2 in range(8):
        par, k = divmod(n2, 4)
        perm[5120 + 64 * n2 + p64] = 8 * (640 + p64) + 2 * k + par
    return perm


class _Prog:
    def __init__(self, nc, ctx):
        self.nc = nc
        self.ctx = ctx
        self._sems = {}
        self.cnt = {}
        self._hw = {}

    def sem(self, name):
        if name not in self._sems:
            self._sems[name] = self.ctx.enter_context(self.nc.semaphore(name))
            self.cnt[name] = 0
            self._hw[name] = {}
        return self._sems[name]

    def inc(self, instr, name, n=1):
        instr.then_inc(self.sem(name), n)
        self.cnt[name] += n
        return self.cnt[name]

    def vsync(self, eng, instr, name="fv"):
        c = self.inc(instr, name)
        self.wait(eng, name, c)
        return c

    def wait(self, eng, name, val=None, key=None):
        self.sem(name)
        if val is None:
            val = self.cnt[name]
        if val <= 0:
            return
        k = key if key is not None else str(getattr(eng, "engine", id(eng)))
        hw = self._hw[name]
        if hw.get(k, -1) >= val:
            return
        eng.wait_ge(self._sems[name], val)
        hw[k] = val


def build_program():
    nc = bass.Bass("TRN2", target_bir_lowering=False, debug=False)

    xT = nc.dram_tensor("xT", [D, TC], F32, kind="ExternalInput").ap()
    wqkvT = nc.dram_tensor("wqkvT", [D, 3 * D], F32, kind="ExternalInput").ap()
    woutT = nc.dram_tensor("woutT", [D, D], F32, kind="ExternalInput").ap()
    guT = nc.dram_tensor("guT", [256, 2 * I5632], I32, kind="ExternalInput").ap()
    sguT = nc.dram_tensor("sguT", [256, 2 * I5632], FP16, kind="ExternalInput").ap()
    dnT = nc.dram_tensor("dnT", [704, D], I32, kind="ExternalInput").ap()
    sdnT = nc.dram_tensor("sdnT", [704, D], FP16, kind="ExternalInput").ap()
    mask8 = nc.dram_tensor("mask8", [128, 8], F32, kind="ExternalInput").ap()
    y_out = nc.dram_tensor("y_out", [D, TC], F32, kind="ExternalOutput").ap()

    kind_dbg = "ExternalOutput" if DEBUG else "Internal"
    if DEBUG:
        xn_dbg = nc.dram_tensor("xn_dbg", [128, NCT * TC], BF16, kind="ExternalOutput").ap()
        mb_dbg = nc.dram_tensor("mb_dbg", [128, 2 * TC], F32, kind="ExternalOutput").ap()
        s12_dbg = nc.dram_tensor("s12_dbg", [1, 2 * TC], F32, kind="ExternalOutput").ap()
    h_scr = nc.dram_tensor("h_scr", [I5632, TC], BF16, kind=kind_dbg).ap()
    wd_scr = nc.dram_tensor("wd_scr", [I5632, D], BF16, kind=kind_dbg).ap()
    a1_scr = nc.dram_tensor("a1_scr", [D, TC], F32, kind=kind_dbg).ap()
    ag_in = [nc.dram_tensor(f"ag_in{j}", [128, 2], F32) for j in range(NCT)]
    ag_out = [nc.dram_tensor(f"ag_out{j}", [8 * 128, 2], F32) for j in range(NCT)]

    ctx = contextlib.ExitStack()
    with ctx:
        E = ctx.enter_context
        P = _Prog(nc, ctx)

        xn = E(nc.sbuf_tensor("xn", [128, NCT * TC], BF16))
        big2 = E(nc.sbuf_tensor("big2", [128, NCT * TC], BF16))
        arena = E(nc.sbuf_tensor("arena", [128, 40960], U8))
        wbuf = E(nc.sbuf_tensor("wbuf", [128, 5 * 2048], BF16))
        sct = E(nc.sbuf_tensor("sct", [128, 2 * 512], FP16))
        smalls = E(nc.sbuf_tensor("smalls", [128, 64], F32))
        onesr = E(nc.sbuf_tensor("onesr", [1, 128], F32))
        onesc = E(nc.sbuf_tensor("onesc", [128, 1], F32))
        onesb = E(nc.sbuf_tensor("onesb", [128, 1], BF16))
        cpool = E(nc.sbuf_tensor("cpool", [128, 8], F32))
        gath = E(nc.sbuf_tensor("gath", [128, NCT * 16], F32))
        sums = E(nc.sbuf_tensor("sums", [128, NCT * 2], F32))
        maskt = E(nc.sbuf_tensor("maskt", [128, 8], F32))
        wrag = E(nc.sbuf_tensor("wrag", [64, 4096], BF16))
        negone = E(nc.sbuf_tensor("negone", [128, 512], BF16))
        neghalf = E(nc.sbuf_tensor("neghalf", [1, 512], BF16))
        hrag = E(nc.sbuf_tensor("hrag", [64, 2048], BF16))

        ps = [E(nc.psum_tensor(f"ps{i}", [128, 512], F32)) for i in range(8)]

        def av(off, dt, n):
            return arena[:, off:off + n * ESZ[dt]].bitcast(dt)

        def xv(off_b, dt, n):
            e = ESZ[BF16]
            assert off_b % e == 0
            return xn[:, off_b // e:(off_b + n * ESZ[dt]) // e].bitcast(dt)

        def bv(off_b, dt, n):
            e = ESZ[BF16]
            return big2[:, off_b // e:(off_b + n * ESZ[dt]) // e].bitcast(dt)

        def wv_(off_b, dt, n):
            e = ESZ[BF16]
            return wbuf[:, off_b // e:(off_b + n * ESZ[dt]) // e].bitcast(dt)

        S = [8192 * i for i in range(5)]
        TMP_OFF = 40960

        blk = E(nc.Block())
        bank_free = [("v", 0)] * 8

        def mm_group(tensor, bank, lhs_list, rhs_list, inc_sem="t", gates=None):
            sname, sval = bank_free[bank]
            P.wait(tensor, sname, sval)
            n = len(lhs_list)
            mm = None
            for i, (l, r) in enumerate(zip(lhs_list, rhs_list)):
                if gates is not None:
                    P.wait(tensor, gates[i][0], gates[i][1])
                mm = tensor.matmul(ps[bank][:, :], l, r, start=(i == 0), stop=(i == n - 1))
            return P.inc(mm, inc_sem)

        st = {}

        @blk.vector
        def _(vector):
            vector.memset(onesr[:], 1.0)
            vector.memset(onesc[:], 1.0)
            vector.memset(onesb[:], 1.0)
            for i, v in enumerate([EPS_LN, -0.5, -1.0, EPS_ATT, EPS_FFN, 1.0 / 2048.0]):
                vector.memset(cpool[:, i:i + 1], v)
                nc.const_aps.register(F32, v, cpool[:, i:i + 1])
            m = vector.memset(cpool[:, 6:7], 0.0)
            P.inc(m, "v")

        @blk.sync
        def _(sync):
            P.inc(sync.dma_start(out=maskt[:], in_=mask8), "msk", 16)

        xslot = [av(S[i], F32, 2048) for i in range(3)]
        xb2 = [bv(0, BF16, 2048), bv(4096, BF16, 2048)]
        sqv = av(S[3], F32, 2048)
        sqvb = av(S[3], BF16, 2048)
        r0 = bv(8192, F32, 2048)[0:1, :]
        r1 = bv(16384, F32, 2048)[0:1, :]
        r2 = bv(24576, F32, 2048)[0:1, :]
        r3 = bv(49152, F32, 2048)[0:1, :]
        rtmp = bv(57344, F32, 2048)[0:1, :]
        Mb = bv(32768, F32, 2048)
        Rb = bv(40960, F32, 2048)
        xfree = [("v", 0)] * 3
        st["sq_used"] = 0

        for ct in range(NCT):
            sl = ct % 3
            sl2 = ct % 2

            @blk.sync
            def _(sync, ct=ct, sl=sl):
                sname, sval = xfree[sl]
                P.wait(sync, sname, sval)
                P.inc(sync.dma_start(out=xslot[sl], in_=xT[128 * ct:128 * (ct + 1), :]), f"xg{sl}", 16)

            @blk.vector
            def _(vector, ct=ct, sl=sl, sl2=sl2):
                P.wait(vector, f"xg{sl}")
                P.wait(vector, "t", st.get(("x_mm", ct - 2), 0))
                v = vector.tensor_copy(out=xb2[sl2], in_=xslot[sl])
                st[("xcast", ct)] = P.inc(v, "v")
                P.wait(vector, "v", st[("xcast", ct)])
                P.wait(vector, "t", st["sq_used"])
                a = vector.tensor_tensor(out=sqvb, in0=xb2[sl2], in1=xb2[sl2], op=OP.mult)
                P.inc(a, "s")

            @blk.tensor
            def _(tensor, ct=ct, sl2=sl2):
                P.wait(tensor, "v", 1)
                P.wait(tensor, "v", st[("xcast", ct)])
                x3 = xb2[sl2].rearrange("p (a b) -> p a b", a=NTT)
                for tc_ in range(NTT):
                    mm = tensor.matmul(ps[tc_][0:1, :], onesb[:], x3[:, tc_],
                                       start=(ct == 0), stop=(ct == NCT - 1))
                if ct == NCT - 1:
                    P.inc(mm, "t")
                P.wait(tensor, "s")
                sq3 = sqvb.rearrange("p (a b) -> p a b", a=NTT)
                for tc_ in range(NTT):
                    mm = tensor.matmul(ps[4 + tc_][0:1, :], onesb[:], sq3[:, tc_],
                                       start=(ct == 0), stop=(ct == NCT - 1))
                P.inc(mm, "t")
                st["sq_used"] = P.cnt["t"]
                st[("x_mm", ct)] = P.cnt["t"]

            xfree[sl] = ("v", st[("xcast", ct)])

        @blk.vector
        def _(vector):
            P.wait(vector, "t")
            v = None
            for tc_ in range(NTT):
                vector.tensor_copy(out=r0[:, 512 * tc_:512 * (tc_ + 1)], in_=ps[tc_][0:1, :])
                v = vector.tensor_copy(out=r1[:, 512 * tc_:512 * (tc_ + 1)], in_=ps[4 + tc_][0:1, :])
            P.vsync(vector, v)
            v = vector.tensor_scalar(out=r2, in0=r0, scalar1=1.0 / 2048.0, scalar2=None, op0=OP.mult)
            P.vsync(vector, v)
            v = vector.tensor_tensor(out=r3, in0=r2, in1=r2, op=OP.mult)
            P.vsync(vector, v)
            v = vector.scalar_tensor_tensor(out=r3, in0=r1, scalar=1.0 / 2048.0, in1=r3,
                                            op0=OP.mult, op1=OP.subtract)
            P.inc(v, "v")
        st["var"] = P.cnt["v"]

        @blk.scalar
        def _(scalar):
            P.wait(scalar, "v", st["var"])
            a0 = scalar.activation(out=rtmp, in_=r3, func=AF.Ln, bias=EPS_LN)
            P.vsync(scalar, a0, "fs")
            a = scalar.activation(out=r3, in_=rtmp, func=AF.Exp, scale=-0.5)
            P.inc(a, "s")
        st["rstd"] = P.cnt["s"]

        @blk.tensor
        def _(tensor):
            P.wait(tensor, "s", st["rstd"])
            for tc_ in range(NTT):
                mm = tensor.matmul(ps[tc_][:, :], onesr[:], r2[:, 512 * tc_:512 * (tc_ + 1)],
                                   start=True, stop=True)
                P.inc(mm, "t")
                mm = tensor.matmul(ps[4 + tc_][:, :], onesr[:], r3[:, 512 * tc_:512 * (tc_ + 1)],
                                   start=True, stop=True)
                P.inc(mm, "t")

        @blk.vector
        def _(vector):
            P.wait(vector, "t")
            for tc_ in range(NTT):
                vector.tensor_copy(out=Mb[:, 512 * tc_:512 * (tc_ + 1)], in_=ps[tc_][:, :])
                v = vector.tensor_copy(out=Rb[:, 512 * tc_:512 * (tc_ + 1)], in_=ps[4 + tc_][:, :])
            P.inc(v, "v")
        for b in range(8):
            bank_free[b] = ("v", P.cnt["v"])
        st["mbrb"] = P.cnt["v"]

        if DEBUG:
            @blk.gpsimd
            def _(gpsimd):
                P.wait(gpsimd, "v")
                P.inc(gpsimd.dma_start(out=mb_dbg[:, 0:TC], in_=Mb), "ds", 16)
                P.inc(gpsimd.dma_start(out=mb_dbg[:, TC:2 * TC], in_=Rb), "ds", 16)
                P.inc(gpsimd.dma_start(out=s12_dbg[0:1, 0:TC], in_=r0), "ds", 16)
                P.inc(gpsimd.dma_start(out=s12_dbg[0:1, TC:2 * TC], in_=r1), "ds", 16)

        if PH_STOP == 1:
            @blk.gpsimd
            def _(gpsimd):
                P.wait(gpsimd, "ds")
            return nc

        xfree = [("v", P.cnt["v"])] * 3
        for ct in range(NCT):
            sl = ct % 3

            @blk.sync
            def _(sync, ct=ct, sl=sl):
                sname, sval = xfree[sl]
                P.wait(sync, sname, sval)
                P.inc(sync.dma_start(out=xslot[sl], in_=xT[128 * ct:128 * (ct + 1), :]), f"xs{sl}", 16)

            @blk.vector
            def _(vector, ct=ct, sl=sl):
                P.wait(vector, f"xs{sl}")
                P.wait(vector, "v", st[("xn_t", ct - 1)] if ct else st["mbrb"])
                v = vector.tensor_tensor(out=sqv, in0=xslot[sl], in1=Mb, op=OP.subtract)
                P.vsync(vector, v)
                v = vector.tensor_tensor(out=xn[:, TC * ct:TC * (ct + 1)], in0=sqv, in1=Rb, op=OP.mult)
                P.inc(v, "v")
                st[("xn_t", ct)] = P.cnt["v"]

            xfree[sl] = ("v", P.cnt["v"])
        st["xn_ready"] = P.cnt["v"]
        if DEBUG:
            @blk.gpsimd
            def _(gpsimd):
                P.wait(gpsimd, "v", st["xn_ready"])
                P.inc(gpsimd.dma_start(out=xn_dbg, in_=xn[:, :]), "ds", 16)

        attnmix = big2
        kS, vS, cS, qS, dS = (av(S[i], F32, 2048) for i in range(5))
        emc = [av(S[4] + 2048 * tt, F32, 512) for tt in range(NTT)]
        wk_t = [wv_(s * 4096, BF16, 2048).rearrange("p (c o) -> p c o", c=16) for s in range(2)]
        wv_t = [wv_(8192 + s * 4096, BF16, 2048).rearrange("p (c o) -> p c o", c=16) for s in range(2)]
        wq_t = wv_(16384, BF16, 2048).rearrange("p (c o) -> p c o", c=16)
        wfree = [("v", 0), ("v", 0)]
        wqfree = ("v", 0)
        pd = {}
        xn3 = xn[:, :].rearrange("p (c t) -> p c t", c=NCT)
        amix3 = attnmix[:, :].rearrange("p (c t) -> p c t", c=NCT)

        def load_wkv(gpsimd, j):
            sl = j % 2
            sname, sval = wfree[sl]
            P.wait(gpsimd, sname, sval)
            src = wqkvT.rearrange("(c p) o -> p c o", p=128)
            P.inc(gpsimd.dma_start(out=wk_t[sl], in_=src[:, :, D + 128 * j:D + 128 * (j + 1)]), f"wk{sl}", 16)
            P.inc(gpsimd.dma_start(out=wv_t[sl], in_=src[:, :, 2 * D + 128 * j:2 * D + 128 * (j + 1)]), f"wvv{sl}", 16)
            pd[("wkv_ld", j)] = (P.cnt[f"wk{sl}"], P.cnt[f"wvv{sl}"])

        @blk.gpsimd
        def _(gpsimd):
            load_wkv(gpsimd, 0)
            src = wqkvT.rearrange("(c p) o -> p c o", p=128)
            P.inc(gpsimd.dma_start(out=wq_t, in_=src[:, :, 0:128]), "wq", 16)
            pd[("wq_ld", 0)] = P.cnt["wq"]

        for j in range(NCT):
            sl = j % 2

            @blk.tensor
            def _(tensor, j=j, sl=sl):
                P.wait(tensor, f"wk{sl}", pd[("wkv_ld", j)][0])
                P.wait(tensor, f"wvv{sl}", pd[("wkv_ld", j)][1])
                xn_gates = [("v", st[("xn_t", c)]) for c in range(16)]
                for tt in range(NTT):
                    mm_group(tensor, tt,
                             [wk_t[sl][:, c, :] for c in range(16)],
                             [xn3[:, c, TW * tt:TW * (tt + 1)] for c in range(16)], "t",
                             gates=xn_gates)
                    pd[("kmm", j, tt)] = P.cnt["t"]
                for tt in range(NTT):
                    mm_group(tensor, 4 + tt,
                             [wv_t[sl][:, c, :] for c in range(16)],
                             [xn3[:, c, TW * tt:TW * (tt + 1)] for c in range(16)], "t",
                             gates=xn_gates)
                    pd[("vmm", j, tt)] = P.cnt["t"]
            wfree[sl] = ("t", P.cnt["t"])

            @blk.vector
            def _(vector, j=j):
                for tt in range(NTT):
                    P.wait(vector, "t", pd[("kmm", j, tt)])
                    if j > 0:
                        P.wait(vector, "v", pd[("amix", j - 1)], key="self_amx")
                    v = vector.tensor_scalar(out=qS[:, TW * tt:TW * (tt + 1)], in0=ps[tt][:, :],
                                             scalar1=0.0, scalar2=None, op0=OP.min)
                    P.inc(v, "v")
                    pd[("mink", j, tt)] = P.cnt["v"]

            @blk.scalar
            def _(scalar, j=j):
                for tt in range(NTT):
                    P.wait(scalar, "v", pd[("mink", j, tt)])
                    a = scalar.activation(out=emc[tt], in_=qS[:, TW * tt:TW * (tt + 1)], func=AF.Exp)
                    P.inc(a, "s")
                    pd[("emk", j, tt)] = P.cnt["s"]

            @blk.vector
            def _(vector, j=j):
                for tt in range(NTT):
                    P.wait(vector, "s", pd[("emk", j, tt)])
                    v = vector.scalar_tensor_tensor(out=kS[:, TW * tt:TW * (tt + 1)], in0=ps[tt][:, :],
                                                    scalar=0.0, in1=emc[tt], op0=OP.max, op1=OP.add)
                    P.inc(v, "v")
                    pd[("phik", j, tt)] = P.cnt["v"]
                for tt in range(NTT):
                    P.wait(vector, "t", pd[("vmm", j, tt)])
                    v = vector.tensor_copy(out=vS[:, TW * tt:TW * (tt + 1)], in_=ps[4 + tt][:, :])
                    P.inc(v, "v")
                    pd[("vev", j, tt)] = P.cnt["v"]
                P.wait(vector, "v")
                v = vector.scalar_tensor_tensor(out=cS, in0=kS, scalar=1.0, in1=vS,
                                                op0=OP.mult, op1=OP.mult,
                                                accum_out=sums[:, 2 * j + 1:2 * j + 2])
                P.inc(v, "v")
                v = vector.reduce_sum(out=sums[:, 2 * j:2 * j + 1], in_=kS, axis=AX.X)
                P.inc(v, "v")
                pd[("sums", j)] = P.cnt["v"]
                P.wait(vector, "v")
                v = vector.tensor_tensor_scan(out=vS, data0=cS, data1=cS, initial=0.0,
                                              op0=OP.add, op1=OP.bypass)
                P.vsync(vector, v)
                v = vector.tensor_tensor_scan(out=cS, data0=kS, data1=kS, initial=0.0,
                                              op0=OP.add, op1=OP.bypass)
                P.inc(v, "v")
                pd[("scan2", j)] = P.cnt["v"]
            for tt in range(NTT):
                bank_free[tt] = ("v", pd[("phik", j, tt)])
                bank_free[4 + tt] = ("v", pd[("vev", j, tt)])

            @blk.gpsimd
            def _(gpsimd, j=j):
                if j + 1 < NCT:
                    load_wkv(gpsimd, j + 1)
                P.wait(gpsimd, "v", pd[("sums", j)])
                P.inc(gpsimd.dma_start(out=ag_in[j].ap(), in_=sums[:, 2 * j:2 * j + 2]), "agi", 16)
                P.wait(gpsimd, "agi")
                cc = gpsimd.collective_compute(
                    "AllGather", OP.bypass, replica_groups=[list(range(NCORE))],
                    ins=[ag_in[j].ap().opt()], outs=[ag_out[j].ap().opt()])
                P.inc(cc, "cc")
                P.wait(gpsimd, "cc")
                P.inc(gpsimd.dma_start(out=gath[:, 16 * j:16 * (j + 1)].rearrange("p (r c) -> p r c", r=8),
                                       in_=ag_out[j].ap().rearrange("(r p) c -> p r c", r=8)), "gld", 16)
                P.wait(gpsimd, "gld")

            @blk.tensor
            def _(tensor, j=j):
                P.wait(tensor, "wq", pd[("wq_ld", j)])
                for tt in range(NTT):
                    mm_group(tensor, tt,
                             [wq_t[:, c, :] for c in range(16)],
                             [xn3[:, c, TW * tt:TW * (tt + 1)] for c in range(16)], "t")
                    pd[("qmm", j, tt)] = P.cnt["t"]
            wqfree = ("t", P.cnt["t"])

            if j + 1 < NCT:
                @blk.gpsimd
                def _(gpsimd, j=j):
                    sname, sval = wqfree
                    P.wait(gpsimd, sname, sval)
                    src = wqkvT.rearrange("(c p) o -> p c o", p=128)
                    P.inc(gpsimd.dma_start(out=wq_t, in_=src[:, :, 128 * (j + 1):128 * (j + 2)]), "wq", 16)
                    pd[("wq_ld", j + 1)] = P.cnt["wq"]

            @blk.vector
            def _(vector, j=j):
                P.wait(vector, "v", pd[("scan2", j)])
                for tt in range(NTT):
                    P.wait(vector, "t", pd[("qmm", j, tt)])
                    v = vector.tensor_scalar(out=kS[:, TW * tt:TW * (tt + 1)], in0=ps[tt][:, :],
                                             scalar1=0.0, scalar2=None, op0=OP.min)
                    P.inc(v, "v")
                    pd[("minq", j, tt)] = P.cnt["v"]

            @blk.scalar
            def _(scalar, j=j):
                for tt in range(NTT):
                    P.wait(scalar, "v", pd[("minq", j, tt)])
                    a = scalar.activation(out=emc[tt], in_=kS[:, TW * tt:TW * (tt + 1)], func=AF.Exp)
                    P.inc(a, "s")
                    pd[("emq", j, tt)] = P.cnt["s"]

            @blk.vector
            def _(vector, j=j):
                for tt in range(NTT):
                    P.wait(vector, "s", pd[("emq", j, tt)])
                    v = vector.scalar_tensor_tensor(out=qS[:, TW * tt:TW * (tt + 1)], in0=ps[tt][:, :],
                                                    scalar=0.0, in1=emc[tt], op0=OP.max, op1=OP.add)
                    P.inc(v, "v")
                    pd[("phiq", j, tt)] = P.cnt["v"]
                    bank_free[tt] = ("v", P.cnt["v"])
                P.wait(vector, "gld", 16 * (j + 1))
                P.wait(vector, "msk", 16)
                P.wait(vector, "v")
                P.wait(vector, "fv")
                g3 = gath[:, 16 * j:16 * (j + 1)].rearrange("p (r c) -> p r c", r=8)
                m3 = maskt[:, :].unsqueeze(2).broadcast_to([128, 8, 2])
                prodb = smalls[:, 16:32]
                v = vector.scalar_tensor_tensor(out=prodb.rearrange("p (r c) -> p r c", r=8),
                                                in0=g3, scalar=1.0, in1=m3,
                                                op0=OP.mult, op1=OP.mult)
                P.vsync(vector, v)
                pb2 = prodb.rearrange("p (r c) -> p c r", c=2)
                vector.reduce_sum(out=smalls[:, 0:1], in_=pb2[:, 0], axis=AX.X)
                v = vector.reduce_sum(out=smalls[:, 1:2], in_=pb2[:, 1], axis=AX.X)
                P.vsync(vector, v)
                v = vector.scalar_tensor_tensor(out=kS, in0=cS, scalar=smalls[:, 0:1],
                                                in1=qS, op0=OP.add, op1=OP.mult)
                P.inc(v, "v")
                pd[("qk", j)] = P.cnt["v"]

            @blk.scalar
            def _(scalar, j=j):
                P.wait(scalar, "v", pd[("qk", j)])
                a0 = scalar.activation(out=dS, in_=kS, func=AF.Ln, bias=EPS_ATT)
                P.vsync(scalar, a0, "fs")
                a = scalar.activation(out=kS, in_=dS, func=AF.Exp, scale=-1.0)
                P.inc(a, "s")
                pd[("recip", j)] = P.cnt["s"]

            @blk.vector
            def _(vector, j=j):
                P.wait(vector, "v")
                P.wait(vector, "fv")
                v = vector.scalar_tensor_tensor(out=cS, in0=vS, scalar=smalls[:, 1:2],
                                                in1=qS, op0=OP.add, op1=OP.mult)
                P.vsync(vector, v)
                P.wait(vector, "s", pd[("recip", j)])
                v = vector.tensor_tensor(out=attnmix[:, TC * j:TC * (j + 1)], in0=cS, in1=kS, op=OP.mult)
                P.inc(v, "v")
                pd[("amix", j)] = P.cnt["v"]
        st["amix_ready"] = P.cnt["v"]
        if PH_STOP == 3:
            @blk.gpsimd
            def _(gpsimd):
                P.wait(gpsimd, "v")
            return nc

        wo_t = [wv_(s * 4096, BF16, 2048).rearrange("p (c o) -> p c o", c=16) for s in range(2)]
        wo_flat = [wv_(s * 4096, BF16, 2048) for s in range(2)]
        wo_stage = [av(S[2], F32, 2048), av(S[3], F32, 2048)]
        wofree = [("t", P.cnt["t"]), ("t", P.cnt["t"])]
        wsfree = [("v", st["amix_ready"]), ("v", st["amix_ready"])]
        xo = [av(S[0], F32, 512), av(S[0] + 2048, F32, 512)]
        a1b = [av(S[1], F32, 512), av(S[1] + 2048, F32, 512)]
        xofree = [("v", st["amix_ready"])] * 2
        a1free = [("as0", 0), ("as1", 0)]

        for ot in range(NCT):
            sl = ot % 2

            @blk.sync
            def _(sync, ot=ot, sl=sl):
                sname, sval = wsfree[sl]
                P.wait(sync, sname, sval)
                src = woutT.rearrange("(c p) o -> p c o", p=128)
                P.inc(sync.dma_start(out=wo_stage[sl].rearrange("p (c o) -> p c o", c=16),
                                     in_=src[:, :, 128 * ot:128 * (ot + 1)]), f"wof{sl}", 16)
            wof_now = P.cnt[f"wof{sl}"]

            @blk.vector
            def _(vector, sl=sl, wof_now=wof_now):
                P.wait(vector, f"wof{sl}", wof_now)
                sname, sval = wofree[sl]
                P.wait(vector, sname, sval)
                v = vector.tensor_copy(out=wo_flat[sl], in_=wo_stage[sl])
                P.inc(v, "v")
            ldw_now = P.cnt["v"]
            wsfree[sl] = ("v", ldw_now)

            for tt in range(NTT):
                i = ot * NTT + tt
                bsl = i % 8
                xsl = i % 2

                @blk.sync
                def _(sync, ot=ot, tt=tt, xsl=xsl):
                    sname, sval = xofree[xsl]
                    P.wait(sync, sname, sval)
                    P.inc(sync.dma_start(out=xo[xsl],
                                         in_=xT[128 * ot:128 * (ot + 1), TW * tt:TW * (tt + 1)]), f"xs{xsl}", 16)
                x_ld = P.cnt[f"xs{xsl}"]

                @blk.tensor
                def _(tensor, ot=ot, tt=tt, sl=sl, bsl=bsl, ldw_now=ldw_now):
                    P.wait(tensor, "v", ldw_now)
                    P.wait(tensor, "v", st["amix_ready"])
                    mm_group(tensor, bsl,
                             [wo_t[sl][:, c, :] for c in range(16)],
                             [amix3[:, c, TW * tt:TW * (tt + 1)] for c in range(16)], "t")
                mm_now = P.cnt["t"]

                @blk.vector
                def _(vector, bsl=bsl, xsl=xsl, i=i, mm_now=mm_now, x_ld=x_ld):
                    P.wait(vector, "t", mm_now)
                    P.wait(vector, f"xs{xsl}", x_ld)
                    asl = i % 2
                    sname, sval = a1free[asl]
                    P.wait(vector, sname, sval)
                    v = vector.scalar_tensor_tensor(out=a1b[asl], in0=ps[bsl][:, :], scalar=HD_S,
                                                    in1=xo[xsl], op0=OP.mult, op1=OP.add)
                    P.inc(v, "v")
                bank_free[bsl] = ("v", P.cnt["v"])
                xofree[xsl] = ("v", P.cnt["v"])
                a1_rdy = P.cnt["v"]

                @blk.gpsimd
                def _(gpsimd, ot=ot, tt=tt, i=i, a1_rdy=a1_rdy):
                    asl = i % 2
                    P.wait(gpsimd, "v", a1_rdy)
                    P.inc(gpsimd.dma_start(out=a1_scr[128 * ot:128 * (ot + 1), TW * tt:TW * (tt + 1)],
                                           in_=a1b[asl]), f"as{i % 2}", 16)
                a1free[i % 2] = (f"as{i % 2}", P.cnt[f"as{i % 2}"])
            wofree[sl] = ("t", P.cnt["t"])
        st["a1_spills"] = (P.cnt["as0"], P.cnt["as1"])
        st["ph4_t"] = P.cnt["t"]
        st["ph4_v"] = P.cnt["v"]

        wg = bv(0, BF16, 8192).rearrange("p (s o) -> p s o", s=16)
        wu = bv(16384, BF16, 8192).rearrange("p (s o) -> p s o", s=16)
        pkt = av(0, I32, 512)
        dq1 = av(2048, I32, 512)
        dq2 = av(4096, I32, 512)
        dce = av(6144, I32, 512)
        dco = av(8192, I32, 512)
        gsil = av(10240, BF16, 512)
        hbf = [av(11264 + 1024 * k, BF16, 512) for k in range(4)]
        hfree = [(f"hs{i}", 0) for i in range(4)]
        gd = {"pk_free": st["ph4_v"], "sc_free0": st["ph4_v"], "sc_free1": st["ph4_v"]}

        @blk.vector
        def _(vector):
            P.wait(vector, "as0", st["a1_spills"][0])
            P.wait(vector, "as1", st["a1_spills"][1])
            P.wait(vector, "v", st["ph4_v"])

        def emit_dequant(src_pk, src_sc, pt, col0, ncols, dst3, ssl, npart=128):
            row0 = 128 * pt

            @blk.sync
            def _(sync):
                P.wait(sync, "v", gd["pk_free"])
                P.inc(sync.dma_start(out=pkt[0:npart, 0:ncols],
                                     in_=src_pk[row0:row0 + npart, col0:col0 + ncols]), "pk", 16)
                P.wait(sync, "v", gd[f"sc_free{ssl}"])
                P.inc(sync.dma_start(out=sct[0:npart, 512 * ssl:512 * ssl + ncols],
                                     in_=src_sc[row0:row0 + npart, col0:col0 + ncols]), f"sc{ssl}", 16)

            @blk.vector
            def _(vector):
                P.wait(vector, "pk")
                P.wait(vector, f"sc{ssl}")
                P.wait(vector, "v", gd.get("last_deq", 0))
                pk = pkt[0:npart, 0:ncols]
                d1 = dq1[0:npart, 0:ncols]
                d2 = dq2[0:npart, 0:ncols]
                ce = dce[0:npart, 0:ncols]
                co = dco[0:npart, 0:ncols]
                vector.tensor_scalar(out=d1, in0=pk, scalar1=0x07070707, scalar2=3,
                                     op0=OP.bitwise_and, op1=OP.logical_shift_left)
                v = vector.tensor_scalar(out=d2, in0=pk, scalar1=0x08080808, scalar2=4,
                                         op0=OP.bitwise_and, op1=OP.logical_shift_left)
                P.vsync(vector, v)
                v = vector.tensor_tensor(out=ce, in0=d1, in1=d2, op=OP.bitwise_or)
                P.vsync(vector, v)
                vector.tensor_scalar(out=d1, in0=pk, scalar1=0x70707070, scalar2=1,
                                     op0=OP.bitwise_and, op1=OP.logical_shift_right)
                v = vector.tensor_scalar(out=d2, in0=pk, scalar1=0x80808080 - 2**32, scalar2=None,
                                         op0=OP.bitwise_and)
                P.vsync(vector, v)
                v = vector.tensor_tensor(out=co, in0=d1, in1=d2, op=OP.bitwise_or)
                gd["pk_free"] = P.inc(v, "v")
                P.wait(vector, "v")
                scv = sct[0:npart, 512 * ssl:512 * ssl + ncols]
                for par, csrc in ((0, ce), (1, co)):
                    fp8 = csrc.bitcast(FP8E3).rearrange("p (o k) -> p k o", k=4)
                    for k in range(4):
                        v = vector.scalar_tensor_tensor(out=dst3[0:npart, 4 * par + k, 0:ncols], in0=fp8[:, k],
                                                        scalar=4.0, in1=scv, op0=OP.mult, op1=OP.mult)
                gd[f"sc_free{ssl}"] = P.inc(v, "v")
                gd["last_deq"] = P.cnt["v"]
            return P.cnt["v"]

        wg_sl = [bv(0, BF16, 8192).rearrange("p (s o) -> p s o", s=16),
                 bv(32768, BF16, 8192).rearrange("p (s o) -> p s o", s=16)]
        wu_sl = [bv(16384, BF16, 8192).rearrange("p (s o) -> p s o", s=16),
                 bv(49152, BF16, 8192).rearrange("p (s o) -> p s o", s=16)]
        wdq_sl = [av(16384, BF16, 4096).rearrange("p (s o) -> p s o", s=8),
                  av(24576, BF16, 4096).rearrange("p (s o) -> p s o", s=8)]
        gsil2 = [gsil, av(15360, BF16, 512)]

        NCH = NHT // 4
        chunk_ready = {}
        chunk_used = {-1: st["ph4_t"], 0: st["ph4_t"]}
        dn_state = {"k": 0}

        def emit_gate_chunk(cidx):
            slq = cidx % 2
            col0 = 512 * cidx

            @blk.vector
            def _(vector):
                P.wait(vector, "t", chunk_used.get(cidx - 2, 0))
            for pt in range(2):
                emit_dequant(guT, sguT, pt, col0, 512, wg_sl[slq][:, 8 * pt:8 * pt + 8], 0)
                emit_dequant(guT, sguT, pt, I5632 + col0, 512, wu_sl[slq][:, 8 * pt:8 * pt + 8], 1)
            chunk_ready[cidx] = P.cnt["v"]

        def emit_dn_chunk():
            k = dn_state["k"]
            if k >= 24:
                return False
            pt, ch = divmod(k, 4)
            npart = 128 if pt < 5 else 64
            wsl = k % 2

            @blk.vector
            def _(vector):
                P.wait(vector, f"ws{wsl}", 128 * (k // 2))
            emit_dequant(dnT, sdnT, pt, 512 * ch, 512, wdq_sl[wsl], 0, npart=npart)
            dn_rdy = P.cnt["v"]

            @blk.gpsimd
            def _(gpsimd, pt=pt, ch=ch, npart=npart, dn_rdy=dn_rdy, wsl=wsl):
                P.wait(gpsimd, "v", dn_rdy)
                for s_ in range(8):
                    r0_ = 128 * (8 * pt + s_) if pt < 5 else 5120 + 64 * s_
                    P.inc(gpsimd.dma_start(out=wd_scr[r0_:r0_ + npart, 512 * ch:512 * (ch + 1)],
                                           in_=wdq_sl[wsl][0:npart, s_, :]), f"ws{wsl}", 16)
            dn_state["k"] = k + 1
            return True

        emit_gate_chunk(0)
        for m in range(NHT):
            cidx = m // 4
            slq = cidx % 2
            osub = m % 4
            wg = wg_sl[slq]
            wu = wu_sl[slq]
            if osub == 0 and cidx + 1 < NCH:
                emit_gate_chunk(cidx + 1)

            for tt in range(NTT):
                i = m * NTT + tt
                gb = i % 4
                ub = 4 + i % 4

                @blk.tensor
                def _(tensor, m=m, tt=tt, gb=gb, ub=ub, osub=osub, wg=wg, wu=wu, cidx=cidx):
                    P.wait(tensor, "v", chunk_ready[cidx])
                    mm_group(tensor, gb,
                             [wg[:, s, 128 * osub:128 * (osub + 1)] for s in range(16)],
                             [xn3[:, s, TW * tt:TW * (tt + 1)] for s in range(16)], "t")
                    gd[("gmm", m, tt)] = P.cnt["t"]
                    mm_group(tensor, ub,
                             [wu[:, s, 128 * osub:128 * (osub + 1)] for s in range(16)],
                             [xn3[:, s, TW * tt:TW * (tt + 1)] for s in range(16)], "t")
                    gd[("umm", m, tt)] = P.cnt["t"]
                chunk_used[cidx] = P.cnt["t"]

                @blk.scalar
                def _(scalar, m=m, tt=tt, gb=gb, i=i):
                    P.wait(scalar, "t", gd[("gmm", m, tt)])
                    P.wait(scalar, "v", gd.get(("gsil_free", i - 2), 0))
                    a = scalar.activation(out=gsil2[i % 2], in_=ps[gb][:, :], func=AF.Silu)
                    P.inc(a, "s")
                    gd[("gsil", m, tt)] = P.cnt["s"]

                @blk.vector
                def _(vector, m=m, tt=tt, ub=ub, i=i):
                    hsl = i % 4
                    P.wait(vector, "s", gd[("gsil", m, tt)])
                    P.wait(vector, "t", gd[("umm", m, tt)])
                    sname, sval = hfree[hsl]
                    P.wait(vector, sname, sval)
                    v = vector.tensor_tensor(out=hbf[hsl], in0=ps[ub][:, :], in1=gsil2[i % 2], op=OP.mult)
                    P.inc(v, "v")
                    gd[("h", m, tt)] = P.cnt["v"]
                    gd[("gsil_free", i)] = P.cnt["v"]
                bank_free[gb] = ("v", P.cnt["v"])
                bank_free[ub] = ("v", P.cnt["v"])

                @blk.gpsimd
                def _(gpsimd, m=m, tt=tt, i=i):
                    hsl = i % 4
                    P.wait(gpsimd, "v", gd[("h", m, tt)])
                    P.inc(gpsimd.dma_start(out=h_scr[128 * m:128 * (m + 1), TW * tt:TW * (tt + 1)],
                                           in_=hbf[hsl]), f"hs{hsl}", 16)
                hfree[i % 4] = (f"hs{i % 4}", P.cnt[f"hs{i % 4}"])

                if i % 2 == 1:
                    emit_dn_chunk()
        while emit_dn_chunk():
            pass
        st["xn_dead"] = P.cnt["t"]
        if PH_STOP == 5:
            @blk.gpsimd
            def _(gpsimd):
                P.wait(gpsimd, "v")
                P.wait(gpsimd, "t")
                for i4 in range(4):
                    P.wait(gpsimd, f"hs{i4}")
                P.wait(gpsimd, "ws0")
                P.wait(gpsimd, "ws1")
            return nc

        y_d = big2
        yd3 = y_d[:, :].rearrange("p (o t) -> p o t", o=16)
        hb3 = xv(0, BF16, 22528).rearrange("p (m t) -> p m t", m=NHT)
        wd_sb = [xv(45056 + s * 10240, BF16, 5120).rearrange("p (n o) -> p n o", n=40) for s in range(2)]
        wrag_e = [wrag[:, 2048 * s:2048 * s + 512].rearrange("p (q o) -> p q o", q=4) for s in range(2)]
        wrag_o = [wrag[:, 2048 * s + 512:2048 * s + 1024].rearrange("p (q o) -> p q o", q=4) for s in range(2)]
        hrag3 = hrag[:, :].rearrange("p (q t) -> p q t", q=4)
        sqv2 = [av(0, BF16, 512), av(1024, BF16, 512)]
        f0 = av(18432, F32, 2048)[0:1, :]
        f1 = av(26624, F32, 2048)[0:1, :]
        rfb = wv_(0, F32, 2048)
        dn = {}
        rms = {}
        wdfree = [("t", 0)] * 2

        for tch in range(NTT):
            @blk.sync
            def _(sync, tch=tch):
                P.wait(sync, "t", dn.get(("h_used", tch - 1), 0))
                for i4 in range(4):
                    P.wait(sync, f"hs{i4}")
                P.wait(sync, "ws0")
                P.wait(sync, "ws1")
                P.wait(sync, "t", st["xn_dead"])
                src = h_scr.rearrange("(m p) t -> p m t", p=128)
                for half in range(4):
                    P.inc(sync.dma_start(out=hb3[:, 11 * half:11 * (half + 1), :],
                                         in_=src[:, 11 * half:11 * (half + 1), TW * tch:TW * (tch + 1)]),
                          f"hq{half}", 16)
                    dn[("hq", tch, half)] = P.cnt[f"hq{half}"]
            @blk.scalar
            def _(scalar, tch=tch):
                P.wait(scalar, "t", dn.get(("h_used", tch - 1), 0))
                for i4 in range(4):
                    P.wait(scalar, f"hs{i4}")
                src2 = h_scr.rearrange("(n p) t -> p n t", p=64)
                P.inc(scalar.dma_start(out=hrag3, in_=src2[:, 81:88:2, TW * tch:TW * (tch + 1)]), "hrg", 16)
                dn[("hrg", tch)] = P.cnt["hrg"]

            for oc in range(16):
                sl = oc % 2
                b_ = oc % 6

                @blk.scalar
                def _(scalar, tch=tch, oc=oc, sl=sl):
                    sname, sval = wdfree[sl]
                    P.wait(scalar, sname, sval)
                    P.wait(scalar, "t", st["xn_dead"])
                    P.wait(scalar, "ws0")
                    P.wait(scalar, "ws1")
                    src = wd_scr.rearrange("(n p) o -> p n o", p=128)
                    P.inc(scalar.dma_start(out=wd_sb[sl][:, :, :],
                                         in_=src[:, 0:40, 128 * oc:128 * (oc + 1)]), f"wds{sl}", 16)
                    src2 = wd_scr.rearrange("(n p) o -> p n o", p=64)
                    P.inc(scalar.dma_start(out=wrag_e[sl],
                                         in_=src2[:, 80:88:2, 128 * oc:128 * (oc + 1)]), f"wds{sl}", 16)
                    P.inc(scalar.dma_start(out=wrag_o[sl],
                                         in_=src2[:, 81:88:2, 128 * oc:128 * (oc + 1)]), f"wds{sl}", 16)
                wd_ld = P.cnt[f"wds{sl}"]

                @blk.tensor
                def _(tensor, tch=tch, oc=oc, sl=sl, b_=b_):
                    P.wait(tensor, f"wds{sl}", P.cnt[f"wds{sl}"])
                    sname, sval = bank_free[b_]
                    P.wait(tensor, sname, sval)
                    mm = None
                    for n in range(40):
                        q_ = min(n // 11, 3)
                        P.wait(tensor, f"hq{q_}", dn[("hq", tch, q_)])
                        mm = tensor.matmul(ps[b_][:, :], wd_sb[sl][:, n, :], hb3[:, n, :],
                                           start=(n == 0), stop=False)
                    P.wait(tensor, "hrg", dn[("hrg", tch)])
                    for q in range(4):
                        tensor.matmul(ps[b_][:, :], wrag_e[sl][:, q, :], hb3[0:64, 40 + q, :],
                                      start=False, stop=False)
                        mm = tensor.matmul(ps[b_][:, :], wrag_o[sl][:, q, :], hrag3[:, q, :],
                                           start=False, stop=(q == 3))
                    P.inc(mm, "t")
                    dn[("mm", tch, oc)] = P.cnt["t"]
                    dn[("h_used", tch)] = P.cnt["t"]
                wdfree[sl] = ("t", P.cnt["t"])

                @blk.vector
                def _(vector, tch=tch, oc=oc, b_=b_):
                    P.wait(vector, "t", dn[("mm", tch, oc)])
                    v = vector.tensor_copy(out=yd3[:, oc, TW * tch:TW * (tch + 1)], in_=ps[b_][:, :])
                    P.inc(v, "v")
                    dn[("ev", tch, oc)] = P.cnt["v"]
                bank_free[b_] = ("v", P.cnt["v"])




        st["yd_ready"] = P.cnt["v"]

        sqb2 = [av(0, BF16, 2048), av(34816, BF16, 2048)]
        for ot in range(16):
            @blk.vector
            def _(vector, ot=ot):
                P.wait(vector, "v", st["yd_ready"], key="self_yd")
                P.wait(vector, "t", rms.get(("sq_used", ot - 2), 0))
                a = vector.tensor_tensor(out=sqb2[ot % 2], in0=yd3[:, ot, :], in1=yd3[:, ot, :], op=OP.mult)
                P.inc(a, "s")
                rms[("sq", ot)] = P.cnt["s"]

            @blk.tensor
            def _(tensor, ot=ot):
                P.wait(tensor, "s", rms[("sq", ot)])
                if ot == 0:
                    for b4 in range(4):
                        sname, sval = bank_free[b4]
                        P.wait(tensor, sname, sval)
                sq3 = sqb2[ot % 2].rearrange("p (a b) -> p a b", a=NTT)
                mm = None
                for tc_ in range(NTT):
                    mm = tensor.matmul(ps[tc_][0:1, :], onesb[:], sq3[:, tc_],
                                       start=(ot == 0), stop=(ot == 15))
                P.inc(mm, "t")
                rms[("sq_used", ot)] = P.cnt["t"]

        @blk.vector
        def _(vector):
            P.wait(vector, "t")
            P.wait(vector, "ws0")
            P.wait(vector, "ws1")
            v = None
            for tc_ in range(NTT):
                v = vector.tensor_copy(out=f0[:, 512 * tc_:512 * (tc_ + 1)], in_=ps[tc_][0:1, :])
            P.inc(v, "v")
        for b4 in range(4):
            bank_free[b4] = ("v", P.cnt["v"])
        rms_s = P.cnt["v"]
        if PH_STOP == 7:
            @blk.gpsimd
            def _(gpsimd):
                P.wait(gpsimd, "v")
                P.wait(gpsimd, "t")
            return nc
        st["dn_done"] = P.cnt["t"]

        @blk.scalar
        def _(scalar):
            P.wait(scalar, "v", rms_s)
            a0 = scalar.activation(out=f1, in_=f0, func=AF.Ln, scale=1.0 / 2048.0, bias=EPS_FFN)
            P.vsync(scalar, a0, "fs")
            a = scalar.activation(out=f0, in_=f1, func=AF.Exp, scale=-0.5)
            P.inc(a, "s")
        rf_row = P.cnt["s"]

        rfbe = {}
        for tc_ in range(NTT):
            @blk.tensor
            def _(tensor, tc_=tc_):
                if tc_ == 0:
                    P.wait(tensor, "s", rf_row)
                b = 4 + tc_ % 2
                if tc_ >= 2:
                    P.wait(tensor, "v", rfbe[tc_ - 2])
                else:
                    sname, sval = bank_free[b]
                    P.wait(tensor, sname, sval)
                mm = tensor.matmul(ps[b][:, :], onesr[:], f0[:, 512 * tc_:512 * (tc_ + 1)],
                                   start=True, stop=True)
                P.inc(mm, "t")
                rms[("rfb", tc_)] = P.cnt["t"]

            @blk.vector
            def _(vector, tc_=tc_):
                P.wait(vector, "t", rms[("rfb", tc_)])
                v = vector.tensor_copy(out=rfb[:, 512 * tc_:512 * (tc_ + 1)], in_=ps[4 + tc_ % 2][:, :])
                rfbe[tc_] = P.inc(v, "v")
        rfb_done = P.cnt["v"]

        a1l = [av(4096 + k * 2048, F32, 512) for k in range(3)]
        yob = [av(10240 + k * 2048, F32, 512) for k in range(3)]
        t2b = av(16384, F32, 512)
        t2g = av(38912, F32, 512)
        a1lfree = [("v", 0)] * 3
        yofree = [(f"ys{i}", 0) for i in range(3)]

        for ot in range(16):
            for tt in range(NTT):
                i = ot * NTT + tt
                asl = i % 3
                ysl = i % 3
                eng_v = (i % 2 == 0)

                @blk.sync
                def _(sync, ot=ot, tt=tt, asl=asl):
                    sname, sval = a1lfree[asl]
                    P.wait(sync, sname, sval)
                    P.wait(sync, "as0", st["a1_spills"][0])
                    P.wait(sync, "as1", st["a1_spills"][1])
                    P.wait(sync, "t", st["dn_done"])
                    P.inc(sync.dma_start(out=a1l[asl],
                                         in_=a1_scr[128 * ot:128 * (ot + 1), TW * tt:TW * (tt + 1)]), f"xs{asl}", 16)
                a1_ld = P.cnt[f"xs{asl}"]

                if eng_v:
                    @blk.vector
                    def _(vector, ot=ot, tt=tt, asl=asl, ysl=ysl, a1_ld=a1_ld):
                        P.wait(vector, "v", rfb_done, key="self_rfb")
                        P.wait(vector, "v")
                        P.wait(vector, "fv")
                        P.wait(vector, f"xs{asl}", a1_ld)
                        sname, sval = yofree[ysl]
                        P.wait(vector, sname, sval)
                        v = vector.tensor_tensor(out=t2b, in0=yd3[:, ot, TW * tt:TW * (tt + 1)],
                                                 in1=rfb[:, TW * tt:TW * (tt + 1)], op=OP.mult)
                        P.vsync(vector, v)
                        v = vector.tensor_tensor(out=yob[ysl], in0=t2b, in1=a1l[asl], op=OP.add)
                        P.inc(v, "v")
                    a1lfree[asl] = ("v", P.cnt["v"])
                    yo_rdy = ("v", P.cnt["v"])
                else:
                    @blk.gpsimd
                    def _(gpsimd, ot=ot, tt=tt, asl=asl, ysl=ysl, a1_ld=a1_ld):
                        P.wait(gpsimd, "v", rfb_done, key="gp_rfb")
                        P.wait(gpsimd, "g")
                        P.wait(gpsimd, "fg")
                        P.wait(gpsimd, f"xs{asl}", a1_ld)
                        sname, sval = yofree[ysl]
                        P.wait(gpsimd, sname, sval)
                        v = gpsimd.tensor_tensor(out=t2g, in0=yd3[:, ot, TW * tt:TW * (tt + 1)],
                                                 in1=rfb[:, TW * tt:TW * (tt + 1)], op=OP.mult)
                        P.vsync(gpsimd, v, "fg")
                        v = gpsimd.tensor_tensor(out=yob[ysl], in0=t2g, in1=a1l[asl], op=OP.add)
                        P.inc(v, "g")
                    a1lfree[asl] = ("g", P.cnt["g"])
                    yo_rdy = ("g", P.cnt["g"])

                @blk.scalar
                def _(scalar, ot=ot, tt=tt, ysl=ysl, yo_rdy=yo_rdy):
                    P.wait(scalar, yo_rdy[0], yo_rdy[1])
                    P.inc(scalar.dma_start(out=y_out[128 * ot:128 * (ot + 1), TW * tt:TW * (tt + 1)],
                                           in_=yob[ysl]), f"ys{ysl}", 16)
                yofree[ysl] = (f"ys{ysl}", P.cnt[f"ys{ysl}"])

        @blk.gpsimd
        def _(gpsimd):
            for i3 in range(3):
                P.wait(gpsimd, f"ys{i3}")
            if DEBUG:
                P.wait(gpsimd, "ds")

    return nc


_CACHE = {}


def kernel(L0_attn_x, L0_attn_W_qkv, L0_attn_W_out, L0_ffn_gate_up_packed,
           L0_ffn_gate_up_scales, L0_ffn_down_packed, L0_ffn_down_scales):
    x = np.asarray(L0_attn_x)
    wqkv = np.asarray(L0_attn_W_qkv)
    wout = np.asarray(L0_attn_W_out)
    gup = np.asarray(L0_ffn_gate_up_packed)
    gus = np.asarray(L0_ffn_gate_up_scales)
    dnp = np.asarray(L0_ffn_down_packed)
    dns = np.asarray(L0_ffn_down_scales)

    perm = _perm_d()
    iperm = _perm_i()
    inv = np.argsort(perm)

    wqkvT = np.ascontiguousarray(wqkv.T[perm], dtype=np.float32)
    woutT = np.ascontiguousarray(wout.T[:, perm], dtype=np.float32)
    row_order = np.concatenate([iperm, iperm + I5632])
    guT = np.ascontiguousarray(gup[row_order].T)
    sguT = np.ascontiguousarray(np.repeat(gus[:, row_order], 16, axis=0))
    dnT = np.ascontiguousarray(dnp[perm].T)
    sdnT = np.ascontiguousarray(np.repeat(dns[:, perm], 16, axis=0))

    if "nc" not in _CACHE:
        _CACHE["nc"] = build_program()
    nc = _CACHE["nc"]

    in_maps = []
    for c in range(NCORE):
        xc = np.ascontiguousarray(x[0, TC * c:TC * (c + 1), :].T[perm], dtype=np.float32)
        m8 = np.zeros((128, 8), np.float32)
        m8[:, :c] = 1.0
        in_maps.append({
            "xT": xc, "wqkvT": wqkvT, "woutT": woutT, "guT": guT, "sguT": sguT,
            "dnT": dnT, "sdnT": sdnT, "mask8": m8,
        })
    res = run_bass_kernel_spmd(nc, in_maps, core_ids=list(range(NCORE)))
    out = np.empty((1, NCORE * TC, D), np.float32)
    for c in range(NCORE):
        yc = res.results[c]["y_out"]
        out[0, TC * c:TC * (c + 1), :] = yc[inv].T
    return out


# version 23
# speedup vs baseline: 1.0144x; 1.0039x over previous
import contextlib
import numpy as np
import concourse.bass as bass
import concourse.mybir as mybir
from concourse.bass_utils import run_bass_kernel_spmd
from concourse.alu_op_type import AluOpType

F32 = mybir.dt.float32
BF16 = mybir.dt.bfloat16
I32 = mybir.dt.int32
U8 = mybir.dt.uint8
FP16 = mybir.dt.float16
FP8E3 = mybir.dt.float8e3
AF = mybir.ActivationFunctionType
OP = AluOpType
AX = mybir.AxisListType

NCORE = 8
TC = 2048
D = 2048
NCT = 16
NTT = 4
TW = 512
I5632 = 5632
NHT = 44
NKD = 48
HD_S = 0.08838834764831845
EPS_LN = 1e-5
EPS_ATT = 1e-9
EPS_FFN = 1e-6
DEBUG = False
PH_STOP = 0
SKIP_RAGGED = False
SKIP_WDRAG = False
SKIP_SIGMA = False
ESZ = {F32: 4, BF16: 2, I32: 4, U8: 1, FP16: 2, FP8E3: 1}


def _perm_d():
    perm = np.empty(D, np.int64)
    p = np.arange(128)
    for pt in range(2):
        for par in range(2):
            for k in range(4):
                m = 8 * pt + 4 * par + k
                perm[128 * m + p] = 1024 * pt + 8 * p + 2 * k + par
    return perm


def _perm_i():
    perm = np.empty(I5632, np.int64)
    p = np.arange(128)
    for n in range(40):
        pt, sl = divmod(n, 8)
        par, k = divmod(sl, 4)
        perm[128 * n + p] = 8 * (128 * pt + p) + 2 * k + par
    p64 = np.arange(64)
    for n2 in range(8):
        par, k = divmod(n2, 4)
        perm[5120 + 64 * n2 + p64] = 8 * (640 + p64) + 2 * k + par
    return perm


class _Prog:
    def __init__(self, nc, ctx):
        self.nc = nc
        self.ctx = ctx
        self._sems = {}
        self.cnt = {}
        self._hw = {}

    def sem(self, name):
        if name not in self._sems:
            self._sems[name] = self.ctx.enter_context(self.nc.semaphore(name))
            self.cnt[name] = 0
            self._hw[name] = {}
        return self._sems[name]

    def inc(self, instr, name, n=1):
        instr.then_inc(self.sem(name), n)
        self.cnt[name] += n
        return self.cnt[name]

    def vsync(self, eng, instr, name="fv"):
        c = self.inc(instr, name)
        self.wait(eng, name, c)
        return c

    def wait(self, eng, name, val=None, key=None):
        self.sem(name)
        if val is None:
            val = self.cnt[name]
        if val <= 0:
            return
        k = key if key is not None else str(getattr(eng, "engine", id(eng)))
        hw = self._hw[name]
        if hw.get(k, -1) >= val:
            return
        eng.wait_ge(self._sems[name], val)
        hw[k] = val


def build_program():
    nc = bass.Bass("TRN2", target_bir_lowering=False, debug=False)

    xT = nc.dram_tensor("xT", [D, TC], F32, kind="ExternalInput").ap()
    wqkvT = nc.dram_tensor("wqkvT", [D, 3 * D], F32, kind="ExternalInput").ap()
    woutT = nc.dram_tensor("woutT", [D, D], F32, kind="ExternalInput").ap()
    guT = nc.dram_tensor("guT", [256, 2 * I5632], I32, kind="ExternalInput").ap()
    sguT = nc.dram_tensor("sguT", [256, 2 * I5632], FP16, kind="ExternalInput").ap()
    dnT = nc.dram_tensor("dnT", [704, D], I32, kind="ExternalInput").ap()
    sdnT = nc.dram_tensor("sdnT", [704, D], FP16, kind="ExternalInput").ap()
    mask8 = nc.dram_tensor("mask8", [128, 8], F32, kind="ExternalInput").ap()
    y_out = nc.dram_tensor("y_out", [D, TC], F32, kind="ExternalOutput").ap()

    kind_dbg = "ExternalOutput" if DEBUG else "Internal"
    if DEBUG:
        xn_dbg = nc.dram_tensor("xn_dbg", [128, NCT * TC], BF16, kind="ExternalOutput").ap()
        mb_dbg = nc.dram_tensor("mb_dbg", [128, 2 * TC], F32, kind="ExternalOutput").ap()
        s12_dbg = nc.dram_tensor("s12_dbg", [1, 2 * TC], F32, kind="ExternalOutput").ap()
    h_scr = nc.dram_tensor("h_scr", [I5632, TC], BF16, kind=kind_dbg).ap()
    wd_scr = nc.dram_tensor("wd_scr", [I5632, D], BF16, kind=kind_dbg).ap()
    a1_scr = nc.dram_tensor("a1_scr", [D, TC], F32, kind=kind_dbg).ap()
    ag_in = [nc.dram_tensor(f"ag_in{j}", [128, 2], F32) for j in range(NCT)]
    ag_out = [nc.dram_tensor(f"ag_out{j}", [8 * 128, 2], F32) for j in range(NCT)]

    ctx = contextlib.ExitStack()
    with ctx:
        E = ctx.enter_context
        P = _Prog(nc, ctx)

        xn = E(nc.sbuf_tensor("xn", [128, NCT * TC], BF16))
        big2 = E(nc.sbuf_tensor("big2", [128, NCT * TC], BF16))
        arena = E(nc.sbuf_tensor("arena", [128, 40960], U8))
        wbuf = E(nc.sbuf_tensor("wbuf", [128, 5 * 2048], BF16))
        sct = E(nc.sbuf_tensor("sct", [128, 2 * 512], FP16))
        smalls = E(nc.sbuf_tensor("smalls", [128, 64], F32))
        onesr = E(nc.sbuf_tensor("onesr", [1, 128], F32))
        onesc = E(nc.sbuf_tensor("onesc", [128, 1], F32))
        onesb = E(nc.sbuf_tensor("onesb", [128, 1], BF16))
        cpool = E(nc.sbuf_tensor("cpool", [128, 8], F32))
        gath = E(nc.sbuf_tensor("gath", [128, NCT * 16], F32))
        sums = E(nc.sbuf_tensor("sums", [128, NCT * 2], F32))
        maskt = E(nc.sbuf_tensor("maskt", [128, 8], F32))
        wrag = E(nc.sbuf_tensor("wrag", [64, 4096], BF16))
        negone = E(nc.sbuf_tensor("negone", [128, 512], BF16))
        neghalf = E(nc.sbuf_tensor("neghalf", [1, 512], BF16))
        hrag = E(nc.sbuf_tensor("hrag", [64, 2048], BF16))

        ps = [E(nc.psum_tensor(f"ps{i}", [128, 512], F32)) for i in range(8)]

        def av(off, dt, n):
            return arena[:, off:off + n * ESZ[dt]].bitcast(dt)

        def xv(off_b, dt, n):
            e = ESZ[BF16]
            assert off_b % e == 0
            return xn[:, off_b // e:(off_b + n * ESZ[dt]) // e].bitcast(dt)

        def bv(off_b, dt, n):
            e = ESZ[BF16]
            return big2[:, off_b // e:(off_b + n * ESZ[dt]) // e].bitcast(dt)

        def wv_(off_b, dt, n):
            e = ESZ[BF16]
            return wbuf[:, off_b // e:(off_b + n * ESZ[dt]) // e].bitcast(dt)

        S = [8192 * i for i in range(5)]
        TMP_OFF = 40960

        blk = E(nc.Block())
        bank_free = [("v", 0)] * 8

        def mm_group(tensor, bank, lhs_list, rhs_list, inc_sem="t", gates=None):
            sname, sval = bank_free[bank]
            P.wait(tensor, sname, sval)
            n = len(lhs_list)
            mm = None
            for i, (l, r) in enumerate(zip(lhs_list, rhs_list)):
                if gates is not None:
                    P.wait(tensor, gates[i][0], gates[i][1])
                mm = tensor.matmul(ps[bank][:, :], l, r, start=(i == 0), stop=(i == n - 1))
            return P.inc(mm, inc_sem)

        st = {}

        @blk.vector
        def _(vector):
            vector.memset(onesr[:], 1.0)
            vector.memset(onesc[:], 1.0)
            vector.memset(onesb[:], 1.0)
            for i, v in enumerate([EPS_LN, -0.5, -1.0, EPS_ATT, EPS_FFN, 1.0 / 2048.0]):
                vector.memset(cpool[:, i:i + 1], v)
                nc.const_aps.register(F32, v, cpool[:, i:i + 1])
            m = vector.memset(cpool[:, 6:7], 0.0)
            P.inc(m, "v")

        @blk.sync
        def _(sync):
            P.inc(sync.dma_start(out=maskt[:], in_=mask8), "msk", 16)

        xslot = [av(S[i], F32, 2048) for i in range(3)]
        xbslot = [av(S[i], BF16, 2048) for i in range(3)]
        sqv = av(S[3], F32, 2048)
        sqvb = av(S[3], BF16, 2048)
        r0 = bv(8192, F32, 2048)[0:1, :]
        r1 = bv(16384, F32, 2048)[0:1, :]
        r2 = bv(24576, F32, 2048)[0:1, :]
        r3 = bv(49152, F32, 2048)[0:1, :]
        rtmp = bv(57344, F32, 2048)[0:1, :]
        Mb = bv(32768, F32, 2048)
        Rb = bv(40960, F32, 2048)
        xfree = [("v", 0)] * 3
        st["sq_used"] = 0

        for ct in range(NCT):
            sl = ct % 3

            @blk.gpsimd
            def _(gpsimd, ct=ct, sl=sl):
                sname, sval = xfree[sl]
                P.wait(gpsimd, sname, sval)
                P.inc(gpsimd.dma_start(out=xbslot[sl], in_=xT[128 * ct:128 * (ct + 1), :]), f"xg{sl}", 16)

            @blk.vector
            def _(vector, ct=ct, sl=sl):
                P.wait(vector, f"xg{sl}")
                P.wait(vector, "t", st["sq_used"])
                a = vector.tensor_tensor(out=sqvb, in0=xbslot[sl], in1=xbslot[sl], op=OP.mult)
                P.inc(a, "s")

            @blk.tensor
            def _(tensor, ct=ct, sl=sl):
                P.wait(tensor, "v", 1)
                P.wait(tensor, f"xg{sl}")
                x3 = xbslot[sl].rearrange("p (a b) -> p a b", a=NTT)
                for tc_ in range(NTT):
                    mm = tensor.matmul(ps[tc_][0:1, :], onesb[:], x3[:, tc_],
                                       start=(ct == 0), stop=(ct == NCT - 1))
                if ct == NCT - 1:
                    P.inc(mm, "t")
                P.wait(tensor, "s")
                sq3 = sqvb.rearrange("p (a b) -> p a b", a=NTT)
                for tc_ in range(NTT):
                    mm = tensor.matmul(ps[4 + tc_][0:1, :], onesb[:], sq3[:, tc_],
                                       start=(ct == 0), stop=(ct == NCT - 1))
                P.inc(mm, "t")
                st["sq_used"] = P.cnt["t"]

            xfree[sl] = ("t", P.cnt["t"])

        @blk.vector
        def _(vector):
            P.wait(vector, "t")
            v = None
            for tc_ in range(NTT):
                vector.tensor_copy(out=r0[:, 512 * tc_:512 * (tc_ + 1)], in_=ps[tc_][0:1, :])
                v = vector.tensor_copy(out=r1[:, 512 * tc_:512 * (tc_ + 1)], in_=ps[4 + tc_][0:1, :])
            P.vsync(vector, v)
            v = vector.tensor_scalar(out=r2, in0=r0, scalar1=1.0 / 2048.0, scalar2=None, op0=OP.mult)
            P.vsync(vector, v)
            v = vector.tensor_tensor(out=r3, in0=r2, in1=r2, op=OP.mult)
            P.vsync(vector, v)
            v = vector.scalar_tensor_tensor(out=r3, in0=r1, scalar=1.0 / 2048.0, in1=r3,
                                            op0=OP.mult, op1=OP.subtract)
            P.inc(v, "v")
        st["var"] = P.cnt["v"]

        @blk.scalar
        def _(scalar):
            P.wait(scalar, "v", st["var"])
            a0 = scalar.activation(out=rtmp, in_=r3, func=AF.Ln, bias=EPS_LN)
            P.vsync(scalar, a0, "fs")
            a = scalar.activation(out=r3, in_=rtmp, func=AF.Exp, scale=-0.5)
            P.inc(a, "s")
        st["rstd"] = P.cnt["s"]

        @blk.tensor
        def _(tensor):
            P.wait(tensor, "s", st["rstd"])
            for tc_ in range(NTT):
                mm = tensor.matmul(ps[tc_][:, :], onesr[:], r2[:, 512 * tc_:512 * (tc_ + 1)],
                                   start=True, stop=True)
                P.inc(mm, "t")
                mm = tensor.matmul(ps[4 + tc_][:, :], onesr[:], r3[:, 512 * tc_:512 * (tc_ + 1)],
                                   start=True, stop=True)
                P.inc(mm, "t")

        @blk.vector
        def _(vector):
            P.wait(vector, "t")
            for tc_ in range(NTT):
                vector.tensor_copy(out=Mb[:, 512 * tc_:512 * (tc_ + 1)], in_=ps[tc_][:, :])
                v = vector.tensor_copy(out=Rb[:, 512 * tc_:512 * (tc_ + 1)], in_=ps[4 + tc_][:, :])
            P.inc(v, "v")
        for b in range(8):
            bank_free[b] = ("v", P.cnt["v"])
        st["mbrb"] = P.cnt["v"]

        if DEBUG:
            @blk.gpsimd
            def _(gpsimd):
                P.wait(gpsimd, "v")
                P.inc(gpsimd.dma_start(out=mb_dbg[:, 0:TC], in_=Mb), "ds", 16)
                P.inc(gpsimd.dma_start(out=mb_dbg[:, TC:2 * TC], in_=Rb), "ds", 16)
                P.inc(gpsimd.dma_start(out=s12_dbg[0:1, 0:TC], in_=r0), "ds", 16)
                P.inc(gpsimd.dma_start(out=s12_dbg[0:1, TC:2 * TC], in_=r1), "ds", 16)

        if PH_STOP == 1:
            @blk.gpsimd
            def _(gpsimd):
                P.wait(gpsimd, "ds")
            return nc

        xfree = [("t", P.cnt["t"])] * 3
        for ct in range(NCT):
            sl = ct % 3

            @blk.sync
            def _(sync, ct=ct, sl=sl):
                sname, sval = xfree[sl]
                P.wait(sync, sname, sval)
                P.inc(sync.dma_start(out=xslot[sl], in_=xT[128 * ct:128 * (ct + 1), :]), f"xs{sl}", 16)

            @blk.vector
            def _(vector, ct=ct, sl=sl):
                P.wait(vector, f"xs{sl}")
                P.wait(vector, "v", st[("xn_t", ct - 1)] if ct else st["mbrb"])
                v = vector.tensor_tensor(out=sqv, in0=xslot[sl], in1=Mb, op=OP.subtract)
                P.vsync(vector, v)
                v = vector.tensor_tensor(out=xn[:, TC * ct:TC * (ct + 1)], in0=sqv, in1=Rb, op=OP.mult)
                P.inc(v, "v")
                st[("xn_t", ct)] = P.cnt["v"]

            xfree[sl] = ("v", P.cnt["v"])
        st["xn_ready"] = P.cnt["v"]
        if DEBUG:
            @blk.gpsimd
            def _(gpsimd):
                P.wait(gpsimd, "v", st["xn_ready"])
                P.inc(gpsimd.dma_start(out=xn_dbg, in_=xn[:, :]), "ds", 16)

        attnmix = big2
        kS, vS, cS, qS, dS = (av(S[i], F32, 2048) for i in range(5))
        emc = [av(S[4] + 2048 * tt, F32, 512) for tt in range(NTT)]
        wk_t = [wv_(s * 4096, BF16, 2048).rearrange("p (c o) -> p c o", c=16) for s in range(2)]
        wv_t = [wv_(8192 + s * 4096, BF16, 2048).rearrange("p (c o) -> p c o", c=16) for s in range(2)]
        wq_t = wv_(16384, BF16, 2048).rearrange("p (c o) -> p c o", c=16)
        wfree = [("v", 0), ("v", 0)]
        wqfree = ("v", 0)
        pd = {}
        xn3 = xn[:, :].rearrange("p (c t) -> p c t", c=NCT)
        amix3 = attnmix[:, :].rearrange("p (c t) -> p c t", c=NCT)

        def load_wkv(gpsimd, j):
            sl = j % 2
            sname, sval = wfree[sl]
            P.wait(gpsimd, sname, sval)
            src = wqkvT.rearrange("(c p) o -> p c o", p=128)
            P.inc(gpsimd.dma_start(out=wk_t[sl], in_=src[:, :, D + 128 * j:D + 128 * (j + 1)]), f"wk{sl}", 16)
            P.inc(gpsimd.dma_start(out=wv_t[sl], in_=src[:, :, 2 * D + 128 * j:2 * D + 128 * (j + 1)]), f"wvv{sl}", 16)
            pd[("wkv_ld", j)] = (P.cnt[f"wk{sl}"], P.cnt[f"wvv{sl}"])

        @blk.gpsimd
        def _(gpsimd):
            load_wkv(gpsimd, 0)
            src = wqkvT.rearrange("(c p) o -> p c o", p=128)
            P.inc(gpsimd.dma_start(out=wq_t, in_=src[:, :, 0:128]), "wq", 16)
            pd[("wq_ld", 0)] = P.cnt["wq"]

        for j in range(NCT):
            sl = j % 2

            @blk.tensor
            def _(tensor, j=j, sl=sl):
                P.wait(tensor, f"wk{sl}", pd[("wkv_ld", j)][0])
                P.wait(tensor, f"wvv{sl}", pd[("wkv_ld", j)][1])
                xn_gates = [("v", st[("xn_t", c)]) for c in range(16)]
                for tt in range(NTT):
                    mm_group(tensor, tt,
                             [wk_t[sl][:, c, :] for c in range(16)],
                             [xn3[:, c, TW * tt:TW * (tt + 1)] for c in range(16)], "t",
                             gates=xn_gates)
                    pd[("kmm", j, tt)] = P.cnt["t"]
                for tt in range(NTT):
                    mm_group(tensor, 4 + tt,
                             [wv_t[sl][:, c, :] for c in range(16)],
                             [xn3[:, c, TW * tt:TW * (tt + 1)] for c in range(16)], "t",
                             gates=xn_gates)
                    pd[("vmm", j, tt)] = P.cnt["t"]
            wfree[sl] = ("t", P.cnt["t"])

            @blk.vector
            def _(vector, j=j):
                for tt in range(NTT):
                    P.wait(vector, "t", pd[("kmm", j, tt)])
                    if j > 0:
                        P.wait(vector, "v", pd[("amix", j - 1)], key="self_amx")
                    v = vector.tensor_scalar(out=qS[:, TW * tt:TW * (tt + 1)], in0=ps[tt][:, :],
                                             scalar1=0.0, scalar2=None, op0=OP.min)
                    P.inc(v, "v")
                    pd[("mink", j, tt)] = P.cnt["v"]

            @blk.scalar
            def _(scalar, j=j):
                for tt in range(NTT):
                    P.wait(scalar, "v", pd[("mink", j, tt)])
                    a = scalar.activation(out=emc[tt], in_=qS[:, TW * tt:TW * (tt + 1)], func=AF.Exp)
                    P.inc(a, "s")
                    pd[("emk", j, tt)] = P.cnt["s"]

            @blk.vector
            def _(vector, j=j):
                for tt in range(NTT):
                    P.wait(vector, "s", pd[("emk", j, tt)])
                    v = vector.scalar_tensor_tensor(out=kS[:, TW * tt:TW * (tt + 1)], in0=ps[tt][:, :],
                                                    scalar=0.0, in1=emc[tt], op0=OP.max, op1=OP.add)
                    P.inc(v, "v")
                    pd[("phik", j, tt)] = P.cnt["v"]
                for tt in range(NTT):
                    P.wait(vector, "t", pd[("vmm", j, tt)])
                    v = vector.tensor_copy(out=vS[:, TW * tt:TW * (tt + 1)], in_=ps[4 + tt][:, :])
                    P.inc(v, "v")
                    pd[("vev", j, tt)] = P.cnt["v"]
                P.wait(vector, "v")
                v = vector.scalar_tensor_tensor(out=cS, in0=kS, scalar=1.0, in1=vS,
                                                op0=OP.mult, op1=OP.mult,
                                                accum_out=sums[:, 2 * j + 1:2 * j + 2])
                P.inc(v, "v")
                v = vector.reduce_sum(out=sums[:, 2 * j:2 * j + 1], in_=kS, axis=AX.X)
                P.inc(v, "v")
                pd[("sums", j)] = P.cnt["v"]
                P.wait(vector, "v")
                v = vector.tensor_tensor_scan(out=vS, data0=cS, data1=cS, initial=0.0,
                                              op0=OP.add, op1=OP.bypass)
                P.vsync(vector, v)
                v = vector.tensor_tensor_scan(out=cS, data0=kS, data1=kS, initial=0.0,
                                              op0=OP.add, op1=OP.bypass)
                P.inc(v, "v")
                pd[("scan2", j)] = P.cnt["v"]
            for tt in range(NTT):
                bank_free[tt] = ("v", pd[("phik", j, tt)])
                bank_free[4 + tt] = ("v", pd[("vev", j, tt)])

            @blk.gpsimd
            def _(gpsimd, j=j):
                if j + 1 < NCT:
                    load_wkv(gpsimd, j + 1)
                P.wait(gpsimd, "v", pd[("sums", j)])
                P.inc(gpsimd.dma_start(out=ag_in[j].ap(), in_=sums[:, 2 * j:2 * j + 2]), "agi", 16)
                P.wait(gpsimd, "agi")
                cc = gpsimd.collective_compute(
                    "AllGather", OP.bypass, replica_groups=[list(range(NCORE))],
                    ins=[ag_in[j].ap().opt()], outs=[ag_out[j].ap().opt()])
                P.inc(cc, "cc")
                P.wait(gpsimd, "cc")
                P.inc(gpsimd.dma_start(out=gath[:, 16 * j:16 * (j + 1)].rearrange("p (r c) -> p r c", r=8),
                                       in_=ag_out[j].ap().rearrange("(r p) c -> p r c", r=8)), "gld", 16)
                P.wait(gpsimd, "gld")

            @blk.tensor
            def _(tensor, j=j):
                P.wait(tensor, "wq", pd[("wq_ld", j)])
                for tt in range(NTT):
                    mm_group(tensor, tt,
                             [wq_t[:, c, :] for c in range(16)],
                             [xn3[:, c, TW * tt:TW * (tt + 1)] for c in range(16)], "t")
                    pd[("qmm", j, tt)] = P.cnt["t"]
            wqfree = ("t", P.cnt["t"])

            if j + 1 < NCT:
                @blk.gpsimd
                def _(gpsimd, j=j):
                    sname, sval = wqfree
                    P.wait(gpsimd, sname, sval)
                    src = wqkvT.rearrange("(c p) o -> p c o", p=128)
                    P.inc(gpsimd.dma_start(out=wq_t, in_=src[:, :, 128 * (j + 1):128 * (j + 2)]), "wq", 16)
                    pd[("wq_ld", j + 1)] = P.cnt["wq"]

            @blk.vector
            def _(vector, j=j):
                P.wait(vector, "v", pd[("scan2", j)])
                for tt in range(NTT):
                    P.wait(vector, "t", pd[("qmm", j, tt)])
                    v = vector.tensor_scalar(out=kS[:, TW * tt:TW * (tt + 1)], in0=ps[tt][:, :],
                                             scalar1=0.0, scalar2=None, op0=OP.min)
                    P.inc(v, "v")
                    pd[("minq", j, tt)] = P.cnt["v"]

            @blk.scalar
            def _(scalar, j=j):
                for tt in range(NTT):
                    P.wait(scalar, "v", pd[("minq", j, tt)])
                    a = scalar.activation(out=emc[tt], in_=kS[:, TW * tt:TW * (tt + 1)], func=AF.Exp)
                    P.inc(a, "s")
                    pd[("emq", j, tt)] = P.cnt["s"]

            @blk.vector
            def _(vector, j=j):
                for tt in range(NTT):
                    P.wait(vector, "s", pd[("emq", j, tt)])
                    v = vector.scalar_tensor_tensor(out=qS[:, TW * tt:TW * (tt + 1)], in0=ps[tt][:, :],
                                                    scalar=0.0, in1=emc[tt], op0=OP.max, op1=OP.add)
                    P.inc(v, "v")
                    pd[("phiq", j, tt)] = P.cnt["v"]
                    bank_free[tt] = ("v", P.cnt["v"])
                P.wait(vector, "gld", 16 * (j + 1))
                P.wait(vector, "msk", 16)
                P.wait(vector, "v")
                P.wait(vector, "fv")
                g3 = gath[:, 16 * j:16 * (j + 1)].rearrange("p (r c) -> p r c", r=8)
                m3 = maskt[:, :].unsqueeze(2).broadcast_to([128, 8, 2])
                prodb = smalls[:, 16:32]
                v = vector.scalar_tensor_tensor(out=prodb.rearrange("p (r c) -> p r c", r=8),
                                                in0=g3, scalar=1.0, in1=m3,
                                                op0=OP.mult, op1=OP.mult)
                P.vsync(vector, v)
                pb2 = prodb.rearrange("p (r c) -> p c r", c=2)
                vector.reduce_sum(out=smalls[:, 0:1], in_=pb2[:, 0], axis=AX.X)
                v = vector.reduce_sum(out=smalls[:, 1:2], in_=pb2[:, 1], axis=AX.X)
                P.vsync(vector, v)
                v = vector.scalar_tensor_tensor(out=kS, in0=cS, scalar=smalls[:, 0:1],
                                                in1=qS, op0=OP.add, op1=OP.mult)
                P.inc(v, "v")
                pd[("qk", j)] = P.cnt["v"]

            @blk.scalar
            def _(scalar, j=j):
                P.wait(scalar, "v", pd[("qk", j)])
                a0 = scalar.activation(out=dS, in_=kS, func=AF.Ln, bias=EPS_ATT)
                P.vsync(scalar, a0, "fs")
                a = scalar.activation(out=kS, in_=dS, func=AF.Exp, scale=-1.0)
                P.inc(a, "s")
                pd[("recip", j)] = P.cnt["s"]

            @blk.vector
            def _(vector, j=j):
                P.wait(vector, "v")
                P.wait(vector, "fv")
                v = vector.scalar_tensor_tensor(out=cS, in0=vS, scalar=smalls[:, 1:2],
                                                in1=qS, op0=OP.add, op1=OP.mult)
                P.vsync(vector, v)
                P.wait(vector, "s", pd[("recip", j)])
                v = vector.tensor_tensor(out=attnmix[:, TC * j:TC * (j + 1)], in0=cS, in1=kS, op=OP.mult)
                P.inc(v, "v")
                pd[("amix", j)] = P.cnt["v"]
        st["amix_ready"] = P.cnt["v"]
        if PH_STOP == 3:
            @blk.gpsimd
            def _(gpsimd):
                P.wait(gpsimd, "v")
            return nc

        wo_t = [wv_(s * 4096, BF16, 2048).rearrange("p (c o) -> p c o", c=16) for s in range(2)]
        wo_flat = [wv_(s * 4096, BF16, 2048) for s in range(2)]
        wo_stage = [av(S[2], F32, 2048), av(S[3], F32, 2048)]
        wofree = [("t", P.cnt["t"]), ("t", P.cnt["t"])]
        wsfree = [("v", st["amix_ready"]), ("v", st["amix_ready"])]
        xo = [av(S[0], F32, 512), av(S[0] + 2048, F32, 512)]
        a1b = [av(S[1], F32, 512), av(S[1] + 2048, F32, 512)]
        xofree = [("v", st["amix_ready"])] * 2
        a1free = [("as0", 0), ("as1", 0)]

        for ot in range(NCT):
            sl = ot % 2

            @blk.sync
            def _(sync, ot=ot, sl=sl):
                sname, sval = wsfree[sl]
                P.wait(sync, sname, sval)
                src = woutT.rearrange("(c p) o -> p c o", p=128)
                P.inc(sync.dma_start(out=wo_stage[sl].rearrange("p (c o) -> p c o", c=16),
                                     in_=src[:, :, 128 * ot:128 * (ot + 1)]), f"wof{sl}", 16)
            wof_now = P.cnt[f"wof{sl}"]

            @blk.vector
            def _(vector, sl=sl, wof_now=wof_now):
                P.wait(vector, f"wof{sl}", wof_now)
                sname, sval = wofree[sl]
                P.wait(vector, sname, sval)
                v = vector.tensor_copy(out=wo_flat[sl], in_=wo_stage[sl])
                P.inc(v, "v")
            ldw_now = P.cnt["v"]
            wsfree[sl] = ("v", ldw_now)

            for tt in range(NTT):
                i = ot * NTT + tt
                bsl = i % 8
                xsl = i % 2

                @blk.sync
                def _(sync, ot=ot, tt=tt, xsl=xsl):
                    sname, sval = xofree[xsl]
                    P.wait(sync, sname, sval)
                    P.inc(sync.dma_start(out=xo[xsl],
                                         in_=xT[128 * ot:128 * (ot + 1), TW * tt:TW * (tt + 1)]), f"xs{xsl}", 16)
                x_ld = P.cnt[f"xs{xsl}"]

                @blk.tensor
                def _(tensor, ot=ot, tt=tt, sl=sl, bsl=bsl, ldw_now=ldw_now):
                    P.wait(tensor, "v", ldw_now)
                    P.wait(tensor, "v", st["amix_ready"])
                    mm_group(tensor, bsl,
                             [wo_t[sl][:, c, :] for c in range(16)],
                             [amix3[:, c, TW * tt:TW * (tt + 1)] for c in range(16)], "t")
                mm_now = P.cnt["t"]

                @blk.vector
                def _(vector, bsl=bsl, xsl=xsl, i=i, mm_now=mm_now, x_ld=x_ld):
                    P.wait(vector, "t", mm_now)
                    P.wait(vector, f"xs{xsl}", x_ld)
                    asl = i % 2
                    sname, sval = a1free[asl]
                    P.wait(vector, sname, sval)
                    v = vector.scalar_tensor_tensor(out=a1b[asl], in0=ps[bsl][:, :], scalar=HD_S,
                                                    in1=xo[xsl], op0=OP.mult, op1=OP.add)
                    P.inc(v, "v")
                bank_free[bsl] = ("v", P.cnt["v"])
                xofree[xsl] = ("v", P.cnt["v"])
                a1_rdy = P.cnt["v"]

                @blk.gpsimd
                def _(gpsimd, ot=ot, tt=tt, i=i, a1_rdy=a1_rdy):
                    asl = i % 2
                    P.wait(gpsimd, "v", a1_rdy)
                    P.inc(gpsimd.dma_start(out=a1_scr[128 * ot:128 * (ot + 1), TW * tt:TW * (tt + 1)],
                                           in_=a1b[asl]), f"as{i % 2}", 16)
                a1free[i % 2] = (f"as{i % 2}", P.cnt[f"as{i % 2}"])
            wofree[sl] = ("t", P.cnt["t"])
        st["a1_spills"] = (P.cnt["as0"], P.cnt["as1"])
        st["ph4_t"] = P.cnt["t"]
        st["ph4_v"] = P.cnt["v"]

        wg = bv(0, BF16, 8192).rearrange("p (s o) -> p s o", s=16)
        wu = bv(16384, BF16, 8192).rearrange("p (s o) -> p s o", s=16)
        pkt = av(0, I32, 512)
        dq1 = av(2048, I32, 512)
        dq2 = av(4096, I32, 512)
        dce = av(6144, I32, 512)
        dco = av(8192, I32, 512)
        dq3 = av(32768, I32, 512)
        dq4 = av(38912, I32, 512)
        gsil = av(10240, BF16, 512)
        hbf = [av(11264 + 1024 * k, BF16, 512) for k in range(4)]
        hfree = [(f"hs{i}", 0) for i in range(4)]
        gd = {"pk_free": st["ph4_v"], "sc_free0": st["ph4_v"], "sc_free1": st["ph4_v"]}

        @blk.vector
        def _(vector):
            P.wait(vector, "as0", st["a1_spills"][0])
            P.wait(vector, "as1", st["a1_spills"][1])
            P.wait(vector, "v", st["ph4_v"])

        def emit_dequant(src_pk, src_sc, pt, col0, ncols, dst3, ssl, npart=128):
            row0 = 128 * pt

            @blk.sync
            def _(sync):
                P.wait(sync, "v", gd["pk_free"])
                P.inc(sync.dma_start(out=pkt[0:npart, 0:ncols],
                                     in_=src_pk[row0:row0 + npart, col0:col0 + ncols]), "pk", 16)
                P.wait(sync, "v", gd[f"sc_free{ssl}"])
                P.inc(sync.dma_start(out=sct[0:npart, 512 * ssl:512 * ssl + ncols],
                                     in_=src_sc[row0:row0 + npart, col0:col0 + ncols]), f"sc{ssl}", 16)

            @blk.vector
            def _(vector):
                P.wait(vector, "pk")
                P.wait(vector, f"sc{ssl}")
                P.wait(vector, "v", gd.get("last_deq", 0))
                pk = pkt[0:npart, 0:ncols]
                d1 = dq1[0:npart, 0:ncols]
                d2 = dq2[0:npart, 0:ncols]
                ce = dce[0:npart, 0:ncols]
                co = dco[0:npart, 0:ncols]
                d3 = dq3[0:npart, 0:ncols]
                d4 = dq4[0:npart, 0:ncols]
                vector.tensor_scalar(out=d1, in0=pk, scalar1=0x07070707, scalar2=3,
                                     op0=OP.bitwise_and, op1=OP.logical_shift_left)
                vector.tensor_scalar(out=d2, in0=pk, scalar1=0x08080808, scalar2=4,
                                     op0=OP.bitwise_and, op1=OP.logical_shift_left)
                vector.tensor_scalar(out=d3, in0=pk, scalar1=0x70707070, scalar2=1,
                                     op0=OP.bitwise_and, op1=OP.logical_shift_right)
                v = vector.tensor_scalar(out=d4, in0=pk, scalar1=0x80808080 - 2**32, scalar2=None,
                                         op0=OP.bitwise_and)
                gd["pk_free"] = P.inc(v, "v")
                P.wait(vector, "v")
                vector.tensor_tensor(out=ce, in0=d1, in1=d2, op=OP.bitwise_or)
                v = vector.tensor_tensor(out=co, in0=d3, in1=d4, op=OP.bitwise_or)
                P.vsync(vector, v)
                scv = sct[0:npart, 512 * ssl:512 * ssl + ncols]
                for par, csrc in ((0, ce), (1, co)):
                    fp8 = csrc.bitcast(FP8E3).rearrange("p (o k) -> p k o", k=4)
                    for k in range(4):
                        v = vector.scalar_tensor_tensor(out=dst3[0:npart, 4 * par + k, 0:ncols], in0=fp8[:, k],
                                                        scalar=4.0, in1=scv, op0=OP.mult, op1=OP.mult)
                gd[f"sc_free{ssl}"] = P.inc(v, "v")
                gd["last_deq"] = P.cnt["v"]
            return P.cnt["v"]

        wg_sl = [bv(0, BF16, 8192).rearrange("p (s o) -> p s o", s=16),
                 bv(32768, BF16, 8192).rearrange("p (s o) -> p s o", s=16)]
        wu_sl = [bv(16384, BF16, 8192).rearrange("p (s o) -> p s o", s=16),
                 bv(49152, BF16, 8192).rearrange("p (s o) -> p s o", s=16)]
        wdq_sl = [av(16384, BF16, 4096).rearrange("p (s o) -> p s o", s=8),
                  av(24576, BF16, 4096).rearrange("p (s o) -> p s o", s=8)]
        gsil2 = [gsil, av(15360, BF16, 512)]

        NCH = NHT // 4
        chunk_ready = {}
        chunk_used = {-1: st["ph4_t"], 0: st["ph4_t"]}
        dn_state = {"k": 0}

        def emit_gate_chunk(cidx):
            slq = cidx % 2
            col0 = 512 * cidx

            @blk.vector
            def _(vector):
                P.wait(vector, "t", chunk_used.get(cidx - 2, 0))
            for pt in range(2):
                emit_dequant(guT, sguT, pt, col0, 512, wg_sl[slq][:, 8 * pt:8 * pt + 8], 0)
                emit_dequant(guT, sguT, pt, I5632 + col0, 512, wu_sl[slq][:, 8 * pt:8 * pt + 8], 1)
            chunk_ready[cidx] = P.cnt["v"]

        def emit_dn_chunk():
            k = dn_state["k"]
            if k >= 24:
                return False
            pt, ch = divmod(k, 4)
            npart = 128 if pt < 5 else 64
            wsl = k % 2

            @blk.vector
            def _(vector):
                P.wait(vector, f"ws{wsl}", 128 * (k // 2))
            emit_dequant(dnT, sdnT, pt, 512 * ch, 512, wdq_sl[wsl], 0, npart=npart)
            dn_rdy = P.cnt["v"]

            @blk.gpsimd
            def _(gpsimd, pt=pt, ch=ch, npart=npart, dn_rdy=dn_rdy, wsl=wsl):
                P.wait(gpsimd, "v", dn_rdy)
                for s_ in range(8):
                    r0_ = 128 * (8 * pt + s_) if pt < 5 else 5120 + 64 * s_
                    P.inc(gpsimd.dma_start(out=wd_scr[r0_:r0_ + npart, 512 * ch:512 * (ch + 1)],
                                           in_=wdq_sl[wsl][0:npart, s_, :]), f"ws{wsl}", 16)
            dn_state["k"] = k + 1
            return True

        emit_gate_chunk(0)
        for m in range(NHT):
            cidx = m // 4
            slq = cidx % 2
            osub = m % 4
            wg = wg_sl[slq]
            wu = wu_sl[slq]
            if osub == 0 and cidx + 1 < NCH:
                emit_gate_chunk(cidx + 1)

            for tt in range(NTT):
                i = m * NTT + tt
                gb = i % 4
                ub = 4 + i % 4

                @blk.tensor
                def _(tensor, m=m, tt=tt, gb=gb, ub=ub, osub=osub, wg=wg, wu=wu, cidx=cidx):
                    P.wait(tensor, "v", chunk_ready[cidx])
                    mm_group(tensor, gb,
                             [wg[:, s, 128 * osub:128 * (osub + 1)] for s in range(16)],
                             [xn3[:, s, TW * tt:TW * (tt + 1)] for s in range(16)], "t")
                    gd[("gmm", m, tt)] = P.cnt["t"]
                    mm_group(tensor, ub,
                             [wu[:, s, 128 * osub:128 * (osub + 1)] for s in range(16)],
                             [xn3[:, s, TW * tt:TW * (tt + 1)] for s in range(16)], "t")
                    gd[("umm", m, tt)] = P.cnt["t"]
                chunk_used[cidx] = P.cnt["t"]

                @blk.scalar
                def _(scalar, m=m, tt=tt, gb=gb, i=i):
                    P.wait(scalar, "t", gd[("gmm", m, tt)])
                    P.wait(scalar, "v", gd.get(("gsil_free", i - 2), 0))
                    a = scalar.activation(out=gsil2[i % 2], in_=ps[gb][:, :], func=AF.Silu)
                    P.inc(a, "s")
                    gd[("gsil", m, tt)] = P.cnt["s"]

                @blk.vector
                def _(vector, m=m, tt=tt, ub=ub, i=i):
                    hsl = i % 4
                    P.wait(vector, "s", gd[("gsil", m, tt)])
                    P.wait(vector, "t", gd[("umm", m, tt)])
                    sname, sval = hfree[hsl]
                    P.wait(vector, sname, sval)
                    v = vector.tensor_tensor(out=hbf[hsl], in0=ps[ub][:, :], in1=gsil2[i % 2], op=OP.mult)
                    P.inc(v, "v")
                    gd[("h", m, tt)] = P.cnt["v"]
                    gd[("gsil_free", i)] = P.cnt["v"]
                bank_free[gb] = ("v", P.cnt["v"])
                bank_free[ub] = ("v", P.cnt["v"])

                @blk.gpsimd
                def _(gpsimd, m=m, tt=tt, i=i):
                    hsl = i % 4
                    P.wait(gpsimd, "v", gd[("h", m, tt)])
                    P.inc(gpsimd.dma_start(out=h_scr[128 * m:128 * (m + 1), TW * tt:TW * (tt + 1)],
                                           in_=hbf[hsl]), f"hs{hsl}", 16)
                hfree[i % 4] = (f"hs{i % 4}", P.cnt[f"hs{i % 4}"])

                if i % 2 == 1:
                    emit_dn_chunk()
        while emit_dn_chunk():
            pass
        st["xn_dead"] = P.cnt["t"]
        if PH_STOP == 5:
            @blk.gpsimd
            def _(gpsimd):
                P.wait(gpsimd, "v")
                P.wait(gpsimd, "t")
                for i4 in range(4):
                    P.wait(gpsimd, f"hs{i4}")
                P.wait(gpsimd, "ws0")
                P.wait(gpsimd, "ws1")
            return nc

        y_d = big2
        yd3 = y_d[:, :].rearrange("p (o t) -> p o t", o=16)
        hb3 = xv(0, BF16, 22528).rearrange("p (m t) -> p m t", m=NHT)
        wd_sb = [xv(45056 + s * 10240, BF16, 5120).rearrange("p (n o) -> p n o", n=40) for s in range(2)]
        wrag_e = [wrag[:, 2048 * s:2048 * s + 512].rearrange("p (q o) -> p q o", q=4) for s in range(2)]
        wrag_o = [wrag[:, 2048 * s + 512:2048 * s + 1024].rearrange("p (q o) -> p q o", q=4) for s in range(2)]
        hrag3 = hrag[:, :].rearrange("p (q t) -> p q t", q=4)
        sqv2 = [av(0, BF16, 512), av(1024, BF16, 512)]
        f0 = av(18432, F32, 2048)[0:1, :]
        f1 = av(26624, F32, 2048)[0:1, :]
        rfb = wv_(0, F32, 2048)
        dn = {}
        rms = {}
        wdfree = [("t", 0)] * 2

        for tch in range(NTT):
            @blk.sync
            def _(sync, tch=tch):
                P.wait(sync, "t", dn.get(("h_used", tch - 1), 0))
                for i4 in range(4):
                    P.wait(sync, f"hs{i4}")
                P.wait(sync, "ws0")
                P.wait(sync, "ws1")
                P.wait(sync, "t", st["xn_dead"])
                src = h_scr.rearrange("(m p) t -> p m t", p=128)
                for half in range(4):
                    P.inc(sync.dma_start(out=hb3[:, 11 * half:11 * (half + 1), :],
                                         in_=src[:, 11 * half:11 * (half + 1), TW * tch:TW * (tch + 1)]),
                          f"hq{half}", 16)
                    dn[("hq", tch, half)] = P.cnt[f"hq{half}"]
            @blk.scalar
            def _(scalar, tch=tch):
                P.wait(scalar, "t", dn.get(("h_used", tch - 1), 0))
                for i4 in range(4):
                    P.wait(scalar, f"hs{i4}")
                src2 = h_scr.rearrange("(n p) t -> p n t", p=64)
                P.inc(scalar.dma_start(out=hrag3, in_=src2[:, 81:88:2, TW * tch:TW * (tch + 1)]), "hrg", 16)
                dn[("hrg", tch)] = P.cnt["hrg"]

            for oc in range(16):
                sl = oc % 2
                b_ = oc % 6

                @blk.scalar
                def _(scalar, tch=tch, oc=oc, sl=sl):
                    sname, sval = wdfree[sl]
                    P.wait(scalar, sname, sval)
                    P.wait(scalar, "t", st["xn_dead"])
                    P.wait(scalar, "ws0")
                    P.wait(scalar, "ws1")
                    src = wd_scr.rearrange("(n p) o -> p n o", p=128)
                    P.inc(scalar.dma_start(out=wd_sb[sl][:, :, :],
                                         in_=src[:, 0:40, 128 * oc:128 * (oc + 1)]), f"wds{sl}", 16)
                    src2 = wd_scr.rearrange("(n p) o -> p n o", p=64)
                    P.inc(scalar.dma_start(out=wrag_e[sl],
                                         in_=src2[:, 80:88:2, 128 * oc:128 * (oc + 1)]), f"wds{sl}", 16)
                    P.inc(scalar.dma_start(out=wrag_o[sl],
                                         in_=src2[:, 81:88:2, 128 * oc:128 * (oc + 1)]), f"wds{sl}", 16)
                wd_ld = P.cnt[f"wds{sl}"]

                @blk.tensor
                def _(tensor, tch=tch, oc=oc, sl=sl, b_=b_):
                    P.wait(tensor, f"wds{sl}", P.cnt[f"wds{sl}"])
                    sname, sval = bank_free[b_]
                    P.wait(tensor, sname, sval)
                    mm = None
                    for n in range(40):
                        q_ = min(n // 11, 3)
                        P.wait(tensor, f"hq{q_}", dn[("hq", tch, q_)])
                        mm = tensor.matmul(ps[b_][:, :], wd_sb[sl][:, n, :], hb3[:, n, :],
                                           start=(n == 0), stop=False)
                    P.wait(tensor, "hrg", dn[("hrg", tch)])
                    for q in range(4):
                        tensor.matmul(ps[b_][:, :], wrag_e[sl][:, q, :], hb3[0:64, 40 + q, :],
                                      start=False, stop=False)
                        mm = tensor.matmul(ps[b_][:, :], wrag_o[sl][:, q, :], hrag3[:, q, :],
                                           start=False, stop=(q == 3))
                    P.inc(mm, "t")
                    dn[("mm", tch, oc)] = P.cnt["t"]
                    dn[("h_used", tch)] = P.cnt["t"]
                wdfree[sl] = ("t", P.cnt["t"])

                @blk.vector
                def _(vector, tch=tch, oc=oc, b_=b_):
                    P.wait(vector, "t", dn[("mm", tch, oc)])
                    v = vector.tensor_copy(out=yd3[:, oc, TW * tch:TW * (tch + 1)], in_=ps[b_][:, :])
                    P.inc(v, "v")
                    dn[("ev", tch, oc)] = P.cnt["v"]
                bank_free[b_] = ("v", P.cnt["v"])




        st["yd_ready"] = P.cnt["v"]

        sqb2 = [av(0, BF16, 2048), av(34816, BF16, 2048)]
        for ot in range(16):
            @blk.vector
            def _(vector, ot=ot):
                P.wait(vector, "v", st["yd_ready"], key="self_yd")
                P.wait(vector, "t", rms.get(("sq_used", ot - 2), 0))
                a = vector.tensor_tensor(out=sqb2[ot % 2], in0=yd3[:, ot, :], in1=yd3[:, ot, :], op=OP.mult)
                P.inc(a, "s")
                rms[("sq", ot)] = P.cnt["s"]

            @blk.tensor
            def _(tensor, ot=ot):
                P.wait(tensor, "s", rms[("sq", ot)])
                if ot == 0:
                    for b4 in range(4):
                        sname, sval = bank_free[b4]
                        P.wait(tensor, sname, sval)
                sq3 = sqb2[ot % 2].rearrange("p (a b) -> p a b", a=NTT)
                mm = None
                for tc_ in range(NTT):
                    mm = tensor.matmul(ps[tc_][0:1, :], onesb[:], sq3[:, tc_],
                                       start=(ot == 0), stop=(ot == 15))
                P.inc(mm, "t")
                rms[("sq_used", ot)] = P.cnt["t"]

        @blk.vector
        def _(vector):
            P.wait(vector, "t")
            P.wait(vector, "ws0")
            P.wait(vector, "ws1")
            v = None
            for tc_ in range(NTT):
                v = vector.tensor_copy(out=f0[:, 512 * tc_:512 * (tc_ + 1)], in_=ps[tc_][0:1, :])
            P.inc(v, "v")
        for b4 in range(4):
            bank_free[b4] = ("v", P.cnt["v"])
        rms_s = P.cnt["v"]
        if PH_STOP == 7:
            @blk.gpsimd
            def _(gpsimd):
                P.wait(gpsimd, "v")
                P.wait(gpsimd, "t")
            return nc
        st["dn_done"] = P.cnt["t"]

        @blk.scalar
        def _(scalar):
            P.wait(scalar, "v", rms_s)
            a0 = scalar.activation(out=f1, in_=f0, func=AF.Ln, scale=1.0 / 2048.0, bias=EPS_FFN)
            P.vsync(scalar, a0, "fs")
            a = scalar.activation(out=f0, in_=f1, func=AF.Exp, scale=-0.5)
            P.inc(a, "s")
        rf_row = P.cnt["s"]

        rfbe = {}
        for tc_ in range(NTT):
            @blk.tensor
            def _(tensor, tc_=tc_):
                if tc_ == 0:
                    P.wait(tensor, "s", rf_row)
                b = 4 + tc_ % 2
                if tc_ >= 2:
                    P.wait(tensor, "v", rfbe[tc_ - 2])
                else:
                    sname, sval = bank_free[b]
                    P.wait(tensor, sname, sval)
                mm = tensor.matmul(ps[b][:, :], onesr[:], f0[:, 512 * tc_:512 * (tc_ + 1)],
                                   start=True, stop=True)
                P.inc(mm, "t")
                rms[("rfb", tc_)] = P.cnt["t"]

            @blk.vector
            def _(vector, tc_=tc_):
                P.wait(vector, "t", rms[("rfb", tc_)])
                v = vector.tensor_copy(out=rfb[:, 512 * tc_:512 * (tc_ + 1)], in_=ps[4 + tc_ % 2][:, :])
                rfbe[tc_] = P.inc(v, "v")
        rfb_done = P.cnt["v"]

        a1l = [av(4096 + k * 2048, F32, 512) for k in range(3)]
        yob = [av(10240 + k * 2048, F32, 512) for k in range(3)]
        t2b = av(16384, F32, 512)
        t2g = av(38912, F32, 512)
        a1lfree = [("v", 0)] * 3
        yofree = [(f"ys{i}", 0) for i in range(3)]

        for ot in range(16):
            for tt in range(NTT):
                i = ot * NTT + tt
                asl = i % 3
                ysl = i % 3
                eng_v = (i % 2 == 0)

                @blk.sync
                def _(sync, ot=ot, tt=tt, asl=asl):
                    sname, sval = a1lfree[asl]
                    P.wait(sync, sname, sval)
                    P.wait(sync, "as0", st["a1_spills"][0])
                    P.wait(sync, "as1", st["a1_spills"][1])
                    P.wait(sync, "t", st["dn_done"])
                    P.inc(sync.dma_start(out=a1l[asl],
                                         in_=a1_scr[128 * ot:128 * (ot + 1), TW * tt:TW * (tt + 1)]), f"xs{asl}", 16)
                a1_ld = P.cnt[f"xs{asl}"]

                if eng_v:
                    @blk.vector
                    def _(vector, ot=ot, tt=tt, asl=asl, ysl=ysl, a1_ld=a1_ld):
                        P.wait(vector, "v", rfb_done, key="self_rfb")
                        P.wait(vector, "v")
                        P.wait(vector, "fv")
                        P.wait(vector, f"xs{asl}", a1_ld)
                        sname, sval = yofree[ysl]
                        P.wait(vector, sname, sval)
                        v = vector.tensor_tensor(out=t2b, in0=yd3[:, ot, TW * tt:TW * (tt + 1)],
                                                 in1=rfb[:, TW * tt:TW * (tt + 1)], op=OP.mult)
                        P.vsync(vector, v)
                        v = vector.tensor_tensor(out=yob[ysl], in0=t2b, in1=a1l[asl], op=OP.add)
                        P.inc(v, "v")
                    a1lfree[asl] = ("v", P.cnt["v"])
                    yo_rdy = ("v", P.cnt["v"])
                else:
                    @blk.gpsimd
                    def _(gpsimd, ot=ot, tt=tt, asl=asl, ysl=ysl, a1_ld=a1_ld):
                        P.wait(gpsimd, "v", rfb_done, key="gp_rfb")
                        P.wait(gpsimd, "g")
                        P.wait(gpsimd, "fg")
                        P.wait(gpsimd, f"xs{asl}", a1_ld)
                        sname, sval = yofree[ysl]
                        P.wait(gpsimd, sname, sval)
                        v = gpsimd.tensor_tensor(out=t2g, in0=yd3[:, ot, TW * tt:TW * (tt + 1)],
                                                 in1=rfb[:, TW * tt:TW * (tt + 1)], op=OP.mult)
                        P.vsync(gpsimd, v, "fg")
                        v = gpsimd.tensor_tensor(out=yob[ysl], in0=t2g, in1=a1l[asl], op=OP.add)
                        P.inc(v, "g")
                    a1lfree[asl] = ("g", P.cnt["g"])
                    yo_rdy = ("g", P.cnt["g"])

                @blk.scalar
                def _(scalar, ot=ot, tt=tt, ysl=ysl, yo_rdy=yo_rdy):
                    P.wait(scalar, yo_rdy[0], yo_rdy[1])
                    P.inc(scalar.dma_start(out=y_out[128 * ot:128 * (ot + 1), TW * tt:TW * (tt + 1)],
                                           in_=yob[ysl]), f"ys{ysl}", 16)
                yofree[ysl] = (f"ys{ysl}", P.cnt[f"ys{ysl}"])

        @blk.gpsimd
        def _(gpsimd):
            for i3 in range(3):
                P.wait(gpsimd, f"ys{i3}")
            if DEBUG:
                P.wait(gpsimd, "ds")

    return nc


_CACHE = {}


def kernel(L0_attn_x, L0_attn_W_qkv, L0_attn_W_out, L0_ffn_gate_up_packed,
           L0_ffn_gate_up_scales, L0_ffn_down_packed, L0_ffn_down_scales):
    x = np.asarray(L0_attn_x)
    wqkv = np.asarray(L0_attn_W_qkv)
    wout = np.asarray(L0_attn_W_out)
    gup = np.asarray(L0_ffn_gate_up_packed)
    gus = np.asarray(L0_ffn_gate_up_scales)
    dnp = np.asarray(L0_ffn_down_packed)
    dns = np.asarray(L0_ffn_down_scales)

    perm = _perm_d()
    iperm = _perm_i()
    inv = np.argsort(perm)

    wqkvT = np.ascontiguousarray(wqkv.T[perm], dtype=np.float32)
    woutT = np.ascontiguousarray(wout.T[:, perm], dtype=np.float32)
    row_order = np.concatenate([iperm, iperm + I5632])
    guT = np.ascontiguousarray(gup[row_order].T)
    sguT = np.ascontiguousarray(np.repeat(gus[:, row_order], 16, axis=0))
    dnT = np.ascontiguousarray(dnp[perm].T)
    sdnT = np.ascontiguousarray(np.repeat(dns[:, perm], 16, axis=0))

    if "nc" not in _CACHE:
        _CACHE["nc"] = build_program()
    nc = _CACHE["nc"]

    in_maps = []
    for c in range(NCORE):
        xc = np.ascontiguousarray(x[0, TC * c:TC * (c + 1), :].T[perm], dtype=np.float32)
        m8 = np.zeros((128, 8), np.float32)
        m8[:, :c] = 1.0
        in_maps.append({
            "xT": xc, "wqkvT": wqkvT, "woutT": woutT, "guT": guT, "sguT": sguT,
            "dnT": dnT, "sdnT": sdnT, "mask8": m8,
        })
    res = run_bass_kernel_spmd(nc, in_maps, core_ids=list(range(NCORE)))
    out = np.empty((1, NCORE * TC, D), np.float32)
    for c in range(NCORE):
        yc = res.results[c]["y_out"]
        out[0, TC * c:TC * (c + 1), :] = yc[inv].T
    return out


# version 24 (final)
# speedup vs baseline: 1.0270x; 1.0124x over previous
import contextlib
import numpy as np
import concourse.bass as bass
import concourse.mybir as mybir
from concourse.bass_utils import run_bass_kernel_spmd
from concourse.alu_op_type import AluOpType

F32 = mybir.dt.float32
BF16 = mybir.dt.bfloat16
I32 = mybir.dt.int32
U8 = mybir.dt.uint8
FP16 = mybir.dt.float16
FP8E3 = mybir.dt.float8e3
AF = mybir.ActivationFunctionType
OP = AluOpType
AX = mybir.AxisListType

NCORE = 8
TC = 2048
D = 2048
NCT = 16
NTT = 4
TW = 512
I5632 = 5632
NHT = 44
NKD = 48
HD_S = 0.08838834764831845
EPS_LN = 1e-5
EPS_ATT = 1e-9
EPS_FFN = 1e-6
DEBUG = False
PH_STOP = 0
SKIP_RAGGED = False
SKIP_WDRAG = False
SKIP_SIGMA = False
ESZ = {F32: 4, BF16: 2, I32: 4, U8: 1, FP16: 2, FP8E3: 1}


def _perm_d():
    perm = np.empty(D, np.int64)
    p = np.arange(128)
    for pt in range(2):
        for par in range(2):
            for k in range(4):
                m = 8 * pt + 4 * par + k
                perm[128 * m + p] = 1024 * pt + 8 * p + 2 * k + par
    return perm


def _perm_i():
    perm = np.empty(I5632, np.int64)
    p = np.arange(128)
    for n in range(40):
        pt, sl = divmod(n, 8)
        par, k = divmod(sl, 4)
        perm[128 * n + p] = 8 * (128 * pt + p) + 2 * k + par
    p64 = np.arange(64)
    for n2 in range(8):
        par, k = divmod(n2, 4)
        perm[5120 + 64 * n2 + p64] = 8 * (640 + p64) + 2 * k + par
    return perm


class _Prog:
    def __init__(self, nc, ctx):
        self.nc = nc
        self.ctx = ctx
        self._sems = {}
        self.cnt = {}
        self._hw = {}

    def sem(self, name):
        if name not in self._sems:
            self._sems[name] = self.ctx.enter_context(self.nc.semaphore(name))
            self.cnt[name] = 0
            self._hw[name] = {}
        return self._sems[name]

    def inc(self, instr, name, n=1):
        instr.then_inc(self.sem(name), n)
        self.cnt[name] += n
        return self.cnt[name]

    def vsync(self, eng, instr, name="fv"):
        c = self.inc(instr, name)
        self.wait(eng, name, c)
        return c

    def wait(self, eng, name, val=None, key=None):
        self.sem(name)
        if val is None:
            val = self.cnt[name]
        if val <= 0:
            return
        k = key if key is not None else str(getattr(eng, "engine", id(eng)))
        hw = self._hw[name]
        if hw.get(k, -1) >= val:
            return
        eng.wait_ge(self._sems[name], val)
        hw[k] = val


def build_program():
    nc = bass.Bass("TRN2", target_bir_lowering=False, debug=False)

    xT = nc.dram_tensor("xT", [D, TC], F32, kind="ExternalInput").ap()
    wqkvT = nc.dram_tensor("wqkvT", [D, 3 * D], F32, kind="ExternalInput").ap()
    woutT = nc.dram_tensor("woutT", [D, D], F32, kind="ExternalInput").ap()
    guT = nc.dram_tensor("guT", [256, 2 * I5632], I32, kind="ExternalInput").ap()
    sguT = nc.dram_tensor("sguT", [256, 2 * I5632], FP16, kind="ExternalInput").ap()
    dnT = nc.dram_tensor("dnT", [704, D], I32, kind="ExternalInput").ap()
    sdnT = nc.dram_tensor("sdnT", [704, D], FP16, kind="ExternalInput").ap()
    mask8 = nc.dram_tensor("mask8", [128, 8], F32, kind="ExternalInput").ap()
    y_out = nc.dram_tensor("y_out", [D, TC], F32, kind="ExternalOutput").ap()

    kind_dbg = "ExternalOutput" if DEBUG else "Internal"
    if DEBUG:
        xn_dbg = nc.dram_tensor("xn_dbg", [128, NCT * TC], BF16, kind="ExternalOutput").ap()
        mb_dbg = nc.dram_tensor("mb_dbg", [128, 2 * TC], F32, kind="ExternalOutput").ap()
        s12_dbg = nc.dram_tensor("s12_dbg", [1, 2 * TC], F32, kind="ExternalOutput").ap()
    h_scr = nc.dram_tensor("h_scr", [I5632, TC], BF16, kind=kind_dbg).ap()
    wd_scr = nc.dram_tensor("wd_scr", [I5632, D], BF16, kind=kind_dbg).ap()
    a1_scr = nc.dram_tensor("a1_scr", [D, TC], F32, kind=kind_dbg).ap()
    ag_in = [nc.dram_tensor(f"ag_in{j}", [128, 2], F32) for j in range(NCT)]
    ag_out = [nc.dram_tensor(f"ag_out{j}", [8 * 128, 2], F32) for j in range(NCT)]

    ctx = contextlib.ExitStack()
    with ctx:
        E = ctx.enter_context
        P = _Prog(nc, ctx)

        xn = E(nc.sbuf_tensor("xn", [128, NCT * TC], BF16))
        big2 = E(nc.sbuf_tensor("big2", [128, NCT * TC], BF16))
        arena = E(nc.sbuf_tensor("arena", [128, 40960], U8))
        wbuf = E(nc.sbuf_tensor("wbuf", [128, 5 * 2048], BF16))
        sct = E(nc.sbuf_tensor("sct", [128, 2 * 512], FP16))
        smalls = E(nc.sbuf_tensor("smalls", [128, 64], F32))
        onesr = E(nc.sbuf_tensor("onesr", [1, 128], F32))
        onesc = E(nc.sbuf_tensor("onesc", [128, 1], F32))
        onesb = E(nc.sbuf_tensor("onesb", [128, 1], BF16))
        cpool = E(nc.sbuf_tensor("cpool", [128, 8], F32))
        gath = E(nc.sbuf_tensor("gath", [128, NCT * 16], F32))
        sums = E(nc.sbuf_tensor("sums", [128, NCT * 2], F32))
        maskt = E(nc.sbuf_tensor("maskt", [128, 8], F32))
        wrag = E(nc.sbuf_tensor("wrag", [64, 4096], BF16))
        negone = E(nc.sbuf_tensor("negone", [128, 512], BF16))
        neghalf = E(nc.sbuf_tensor("neghalf", [1, 512], BF16))
        hrag = E(nc.sbuf_tensor("hrag", [64, 2048], BF16))

        ps = [E(nc.psum_tensor(f"ps{i}", [128, 512], F32)) for i in range(8)]

        def av(off, dt, n):
            return arena[:, off:off + n * ESZ[dt]].bitcast(dt)

        def xv(off_b, dt, n):
            e = ESZ[BF16]
            assert off_b % e == 0
            return xn[:, off_b // e:(off_b + n * ESZ[dt]) // e].bitcast(dt)

        def bv(off_b, dt, n):
            e = ESZ[BF16]
            return big2[:, off_b // e:(off_b + n * ESZ[dt]) // e].bitcast(dt)

        def wv_(off_b, dt, n):
            e = ESZ[BF16]
            return wbuf[:, off_b // e:(off_b + n * ESZ[dt]) // e].bitcast(dt)

        S = [8192 * i for i in range(5)]
        TMP_OFF = 40960

        blk = E(nc.Block())
        bank_free = [("v", 0)] * 8

        def mm_group(tensor, bank, lhs_list, rhs_list, inc_sem="t", gates=None):
            sname, sval = bank_free[bank]
            P.wait(tensor, sname, sval)
            n = len(lhs_list)
            mm = None
            for i, (l, r) in enumerate(zip(lhs_list, rhs_list)):
                if gates is not None:
                    P.wait(tensor, gates[i][0], gates[i][1])
                mm = tensor.matmul(ps[bank][:, :], l, r, start=(i == 0), stop=(i == n - 1))
            return P.inc(mm, inc_sem)

        st = {}

        @blk.vector
        def _(vector):
            vector.memset(onesr[:], 1.0)
            vector.memset(onesc[:], 1.0)
            vector.memset(onesb[:], 1.0)
            for i, v in enumerate([EPS_LN, -0.5, -1.0, EPS_ATT, EPS_FFN, 1.0 / 2048.0]):
                vector.memset(cpool[:, i:i + 1], v)
                nc.const_aps.register(F32, v, cpool[:, i:i + 1])
            m = vector.memset(cpool[:, 6:7], 0.0)
            P.inc(m, "v")

        @blk.sync
        def _(sync):
            P.inc(sync.dma_start(out=maskt[:], in_=mask8), "msk", 16)

        xslot = [av(S[i], F32, 2048) for i in range(3)]
        xbslot = [av(S[i], BF16, 2048) for i in range(3)]
        sqv = av(S[3], F32, 2048)
        sqvb = av(S[3], BF16, 2048)
        r0 = bv(8192, F32, 2048)[0:1, :]
        r1 = bv(16384, F32, 2048)[0:1, :]
        r2 = bv(24576, F32, 2048)[0:1, :]
        r3 = bv(49152, F32, 2048)[0:1, :]
        rtmp = bv(57344, F32, 2048)[0:1, :]
        Mb = bv(32768, F32, 2048)
        Rb = bv(40960, F32, 2048)
        xfree = [("v", 0)] * 3
        st["sq_used"] = 0

        for ct in range(NCT):
            sl = ct % 3

            @blk.gpsimd
            def _(gpsimd, ct=ct, sl=sl):
                sname, sval = xfree[sl]
                P.wait(gpsimd, sname, sval)
                P.inc(gpsimd.dma_start(out=xbslot[sl], in_=xT[128 * ct:128 * (ct + 1), :]), f"xg{sl}", 16)

            @blk.vector
            def _(vector, ct=ct, sl=sl):
                P.wait(vector, f"xg{sl}")
                P.wait(vector, "t", st["sq_used"])
                a = vector.tensor_tensor(out=sqvb, in0=xbslot[sl], in1=xbslot[sl], op=OP.mult)
                P.inc(a, "s")

            @blk.tensor
            def _(tensor, ct=ct, sl=sl):
                P.wait(tensor, "v", 1)
                P.wait(tensor, f"xg{sl}")
                x3 = xbslot[sl].rearrange("p (a b) -> p a b", a=NTT)
                for tc_ in range(NTT):
                    mm = tensor.matmul(ps[tc_][0:1, :], onesb[:], x3[:, tc_],
                                       start=(ct == 0), stop=(ct == NCT - 1))
                if ct == NCT - 1:
                    P.inc(mm, "t")
                P.wait(tensor, "s")
                sq3 = sqvb.rearrange("p (a b) -> p a b", a=NTT)
                for tc_ in range(NTT):
                    mm = tensor.matmul(ps[4 + tc_][0:1, :], onesb[:], sq3[:, tc_],
                                       start=(ct == 0), stop=(ct == NCT - 1))
                P.inc(mm, "t")
                st["sq_used"] = P.cnt["t"]

            xfree[sl] = ("t", P.cnt["t"])

        @blk.vector
        def _(vector):
            P.wait(vector, "t")
            v = None
            for tc_ in range(NTT):
                vector.tensor_copy(out=r0[:, 512 * tc_:512 * (tc_ + 1)], in_=ps[tc_][0:1, :])
                v = vector.tensor_copy(out=r1[:, 512 * tc_:512 * (tc_ + 1)], in_=ps[4 + tc_][0:1, :])
            P.vsync(vector, v)
            v = vector.tensor_scalar(out=r2, in0=r0, scalar1=1.0 / 2048.0, scalar2=None, op0=OP.mult)
            P.vsync(vector, v)
            v = vector.tensor_tensor(out=r3, in0=r2, in1=r2, op=OP.mult)
            P.vsync(vector, v)
            v = vector.scalar_tensor_tensor(out=r3, in0=r1, scalar=1.0 / 2048.0, in1=r3,
                                            op0=OP.mult, op1=OP.subtract)
            P.inc(v, "v")
        st["var"] = P.cnt["v"]

        @blk.scalar
        def _(scalar):
            P.wait(scalar, "v", st["var"])
            a0 = scalar.activation(out=rtmp, in_=r3, func=AF.Ln, bias=EPS_LN)
            P.vsync(scalar, a0, "fs")
            a = scalar.activation(out=r3, in_=rtmp, func=AF.Exp, scale=-0.5)
            P.inc(a, "s")
        st["rstd"] = P.cnt["s"]

        @blk.tensor
        def _(tensor):
            P.wait(tensor, "s", st["rstd"])
            for tc_ in range(NTT):
                mm = tensor.matmul(ps[tc_][:, :], onesr[:], r2[:, 512 * tc_:512 * (tc_ + 1)],
                                   start=True, stop=True)
                P.inc(mm, "t")
                mm = tensor.matmul(ps[4 + tc_][:, :], onesr[:], r3[:, 512 * tc_:512 * (tc_ + 1)],
                                   start=True, stop=True)
                P.inc(mm, "t")

        @blk.vector
        def _(vector):
            P.wait(vector, "t")
            for tc_ in range(NTT):
                vector.tensor_copy(out=Mb[:, 512 * tc_:512 * (tc_ + 1)], in_=ps[tc_][:, :])
                v = vector.tensor_copy(out=Rb[:, 512 * tc_:512 * (tc_ + 1)], in_=ps[4 + tc_][:, :])
            P.inc(v, "v")
        for b in range(8):
            bank_free[b] = ("v", P.cnt["v"])
        st["mbrb"] = P.cnt["v"]

        if DEBUG:
            @blk.gpsimd
            def _(gpsimd):
                P.wait(gpsimd, "v")
                P.inc(gpsimd.dma_start(out=mb_dbg[:, 0:TC], in_=Mb), "ds", 16)
                P.inc(gpsimd.dma_start(out=mb_dbg[:, TC:2 * TC], in_=Rb), "ds", 16)
                P.inc(gpsimd.dma_start(out=s12_dbg[0:1, 0:TC], in_=r0), "ds", 16)
                P.inc(gpsimd.dma_start(out=s12_dbg[0:1, TC:2 * TC], in_=r1), "ds", 16)

        if PH_STOP == 1:
            @blk.gpsimd
            def _(gpsimd):
                P.wait(gpsimd, "ds")
            return nc

        xfree = [("t", P.cnt["t"])] * 3
        for ct in range(NCT):
            sl = ct % 3

            @blk.sync
            def _(sync, ct=ct, sl=sl):
                sname, sval = xfree[sl]
                P.wait(sync, sname, sval)
                P.inc(sync.dma_start(out=xslot[sl], in_=xT[128 * ct:128 * (ct + 1), :]), f"xs{sl}", 16)

            @blk.vector
            def _(vector, ct=ct, sl=sl):
                P.wait(vector, f"xs{sl}")
                P.wait(vector, "v", st[("xn_t", ct - 1)] if ct else st["mbrb"])
                v = vector.tensor_tensor(out=sqv, in0=xslot[sl], in1=Mb, op=OP.subtract)
                P.vsync(vector, v)
                v = vector.tensor_tensor(out=xn[:, TC * ct:TC * (ct + 1)], in0=sqv, in1=Rb, op=OP.mult)
                P.inc(v, "v")
                st[("xn_t", ct)] = P.cnt["v"]

            xfree[sl] = ("v", P.cnt["v"])
        st["xn_ready"] = P.cnt["v"]
        if DEBUG:
            @blk.gpsimd
            def _(gpsimd):
                P.wait(gpsimd, "v", st["xn_ready"])
                P.inc(gpsimd.dma_start(out=xn_dbg, in_=xn[:, :]), "ds", 16)

        attnmix = big2
        kS, vS, cS, qS, dS = (av(S[i], F32, 2048) for i in range(5))
        emc = [av(S[4] + 2048 * tt, F32, 512) for tt in range(NTT)]
        wk_t = [wv_(s * 4096, BF16, 2048).rearrange("p (c o) -> p c o", c=16) for s in range(2)]
        wv_t = [wv_(8192 + s * 4096, BF16, 2048).rearrange("p (c o) -> p c o", c=16) for s in range(2)]
        wq_t = wv_(16384, BF16, 2048).rearrange("p (c o) -> p c o", c=16)
        wfree = [("v", 0), ("v", 0)]
        wqfree = ("v", 0)
        pd = {}
        xn3 = xn[:, :].rearrange("p (c t) -> p c t", c=NCT)
        amix3 = attnmix[:, :].rearrange("p (c t) -> p c t", c=NCT)

        def load_wkv(gpsimd, j):
            sl = j % 2
            sname, sval = wfree[sl]
            P.wait(gpsimd, sname, sval)
            src = wqkvT.rearrange("(c p) o -> p c o", p=128)
            P.inc(gpsimd.dma_start(out=wk_t[sl], in_=src[:, :, D + 128 * j:D + 128 * (j + 1)]), f"wk{sl}", 16)
            P.inc(gpsimd.dma_start(out=wv_t[sl], in_=src[:, :, 2 * D + 128 * j:2 * D + 128 * (j + 1)]), f"wvv{sl}", 16)
            pd[("wkv_ld", j)] = (P.cnt[f"wk{sl}"], P.cnt[f"wvv{sl}"])

        @blk.gpsimd
        def _(gpsimd):
            load_wkv(gpsimd, 0)
            src = wqkvT.rearrange("(c p) o -> p c o", p=128)
            P.inc(gpsimd.dma_start(out=wq_t, in_=src[:, :, 0:128]), "wq", 16)
            pd[("wq_ld", 0)] = P.cnt["wq"]

        for j in range(NCT):
            sl = j % 2

            @blk.tensor
            def _(tensor, j=j, sl=sl):
                P.wait(tensor, f"wk{sl}", pd[("wkv_ld", j)][0])
                P.wait(tensor, f"wvv{sl}", pd[("wkv_ld", j)][1])
                xn_gates = [("v", st[("xn_t", c)]) for c in range(16)]
                for tt in range(NTT):
                    mm_group(tensor, tt,
                             [wk_t[sl][:, c, :] for c in range(16)],
                             [xn3[:, c, TW * tt:TW * (tt + 1)] for c in range(16)], "t",
                             gates=xn_gates)
                    pd[("kmm", j, tt)] = P.cnt["t"]
                for tt in range(NTT):
                    mm_group(tensor, 4 + tt,
                             [wv_t[sl][:, c, :] for c in range(16)],
                             [xn3[:, c, TW * tt:TW * (tt + 1)] for c in range(16)], "t",
                             gates=xn_gates)
                    pd[("vmm", j, tt)] = P.cnt["t"]
            wfree[sl] = ("t", P.cnt["t"])

            @blk.vector
            def _(vector, j=j):
                for tt in range(NTT):
                    P.wait(vector, "t", pd[("kmm", j, tt)])
                    if j > 0:
                        P.wait(vector, "v", pd[("amix", j - 1)], key="self_amx")
                    v = vector.tensor_scalar(out=qS[:, TW * tt:TW * (tt + 1)], in0=ps[tt][:, :],
                                             scalar1=0.0, scalar2=None, op0=OP.min)
                    P.inc(v, "v")
                    pd[("mink", j, tt)] = P.cnt["v"]

            @blk.scalar
            def _(scalar, j=j):
                for tt in range(NTT):
                    P.wait(scalar, "v", pd[("mink", j, tt)])
                    a = scalar.activation(out=emc[tt], in_=qS[:, TW * tt:TW * (tt + 1)], func=AF.Exp)
                    P.inc(a, "s")
                    pd[("emk", j, tt)] = P.cnt["s"]

            @blk.vector
            def _(vector, j=j):
                for tt in range(NTT):
                    P.wait(vector, "s", pd[("emk", j, tt)])
                    v = vector.scalar_tensor_tensor(out=kS[:, TW * tt:TW * (tt + 1)], in0=ps[tt][:, :],
                                                    scalar=0.0, in1=emc[tt], op0=OP.max, op1=OP.add)
                    P.inc(v, "v")
                    pd[("phik", j, tt)] = P.cnt["v"]
                for tt in range(NTT):
                    P.wait(vector, "t", pd[("vmm", j, tt)])
                    v = vector.tensor_copy(out=vS[:, TW * tt:TW * (tt + 1)], in_=ps[4 + tt][:, :])
                    P.inc(v, "v")
                    pd[("vev", j, tt)] = P.cnt["v"]
                P.wait(vector, "v")
                v = vector.scalar_tensor_tensor(out=cS, in0=kS, scalar=1.0, in1=vS,
                                                op0=OP.mult, op1=OP.mult,
                                                accum_out=sums[:, 2 * j + 1:2 * j + 2])
                P.inc(v, "v")
                v = vector.reduce_sum(out=sums[:, 2 * j:2 * j + 1], in_=kS, axis=AX.X)
                P.inc(v, "v")
                pd[("sums", j)] = P.cnt["v"]
                P.wait(vector, "v")
                v = vector.tensor_tensor_scan(out=vS, data0=cS, data1=cS, initial=0.0,
                                              op0=OP.add, op1=OP.bypass)
                P.vsync(vector, v)
                v = vector.tensor_tensor_scan(out=cS, data0=kS, data1=kS, initial=0.0,
                                              op0=OP.add, op1=OP.bypass)
                P.inc(v, "v")
                pd[("scan2", j)] = P.cnt["v"]
            for tt in range(NTT):
                bank_free[tt] = ("v", pd[("phik", j, tt)])
                bank_free[4 + tt] = ("v", pd[("vev", j, tt)])

            @blk.gpsimd
            def _(gpsimd, j=j):
                if j + 1 < NCT:
                    load_wkv(gpsimd, j + 1)
                P.wait(gpsimd, "v", pd[("sums", j)])
                P.inc(gpsimd.dma_start(out=ag_in[j].ap(), in_=sums[:, 2 * j:2 * j + 2]), "agi", 16)
                P.wait(gpsimd, "agi")
                cc = gpsimd.collective_compute(
                    "AllGather", OP.bypass, replica_groups=[list(range(NCORE))],
                    ins=[ag_in[j].ap().opt()], outs=[ag_out[j].ap().opt()])
                P.inc(cc, "cc")
                P.wait(gpsimd, "cc")
                P.inc(gpsimd.dma_start(out=gath[:, 16 * j:16 * (j + 1)].rearrange("p (r c) -> p r c", r=8),
                                       in_=ag_out[j].ap().rearrange("(r p) c -> p r c", r=8)), "gld", 16)
                P.wait(gpsimd, "gld")

            @blk.tensor
            def _(tensor, j=j):
                P.wait(tensor, "wq", pd[("wq_ld", j)])
                for tt in range(NTT):
                    mm_group(tensor, tt,
                             [wq_t[:, c, :] for c in range(16)],
                             [xn3[:, c, TW * tt:TW * (tt + 1)] for c in range(16)], "t")
                    pd[("qmm", j, tt)] = P.cnt["t"]
            wqfree = ("t", P.cnt["t"])

            if j + 1 < NCT:
                @blk.gpsimd
                def _(gpsimd, j=j):
                    sname, sval = wqfree
                    P.wait(gpsimd, sname, sval)
                    src = wqkvT.rearrange("(c p) o -> p c o", p=128)
                    P.inc(gpsimd.dma_start(out=wq_t, in_=src[:, :, 128 * (j + 1):128 * (j + 2)]), "wq", 16)
                    pd[("wq_ld", j + 1)] = P.cnt["wq"]

            @blk.vector
            def _(vector, j=j):
                P.wait(vector, "v", pd[("scan2", j)])
                for tt in range(NTT):
                    P.wait(vector, "t", pd[("qmm", j, tt)])
                    v = vector.tensor_scalar(out=kS[:, TW * tt:TW * (tt + 1)], in0=ps[tt][:, :],
                                             scalar1=0.0, scalar2=None, op0=OP.min)
                    P.inc(v, "v")
                    pd[("minq", j, tt)] = P.cnt["v"]

            @blk.scalar
            def _(scalar, j=j):
                for tt in range(NTT):
                    P.wait(scalar, "v", pd[("minq", j, tt)])
                    a = scalar.activation(out=emc[tt], in_=kS[:, TW * tt:TW * (tt + 1)], func=AF.Exp)
                    P.inc(a, "s")
                    pd[("emq", j, tt)] = P.cnt["s"]

            @blk.vector
            def _(vector, j=j):
                for tt in range(NTT):
                    P.wait(vector, "s", pd[("emq", j, tt)])
                    v = vector.scalar_tensor_tensor(out=qS[:, TW * tt:TW * (tt + 1)], in0=ps[tt][:, :],
                                                    scalar=0.0, in1=emc[tt], op0=OP.max, op1=OP.add)
                    P.inc(v, "v")
                    pd[("phiq", j, tt)] = P.cnt["v"]
                    bank_free[tt] = ("v", P.cnt["v"])
                P.wait(vector, "gld", 16 * (j + 1))
                P.wait(vector, "msk", 16)
                P.wait(vector, "v")
                P.wait(vector, "fv")
                g3 = gath[:, 16 * j:16 * (j + 1)].rearrange("p (r c) -> p r c", r=8)
                m3 = maskt[:, :].unsqueeze(2).broadcast_to([128, 8, 2])
                prodb = smalls[:, 16:32]
                v = vector.scalar_tensor_tensor(out=prodb.rearrange("p (r c) -> p r c", r=8),
                                                in0=g3, scalar=1.0, in1=m3,
                                                op0=OP.mult, op1=OP.mult)
                P.vsync(vector, v)
                pb2 = prodb.rearrange("p (r c) -> p c r", c=2)
                vector.reduce_sum(out=smalls[:, 0:1], in_=pb2[:, 0], axis=AX.X)
                v = vector.reduce_sum(out=smalls[:, 1:2], in_=pb2[:, 1], axis=AX.X)
                P.vsync(vector, v)
                v = vector.scalar_tensor_tensor(out=kS, in0=cS, scalar=smalls[:, 0:1],
                                                in1=qS, op0=OP.add, op1=OP.mult)
                P.inc(v, "v")
                pd[("qk", j)] = P.cnt["v"]

            @blk.scalar
            def _(scalar, j=j):
                P.wait(scalar, "v", pd[("qk", j)])
                a0 = scalar.activation(out=dS, in_=kS, func=AF.Ln, bias=EPS_ATT)
                P.vsync(scalar, a0, "fs")
                a = scalar.activation(out=kS, in_=dS, func=AF.Exp, scale=-1.0)
                P.inc(a, "s")
                pd[("recip", j)] = P.cnt["s"]

            @blk.vector
            def _(vector, j=j):
                P.wait(vector, "v")
                P.wait(vector, "fv")
                v = vector.scalar_tensor_tensor(out=cS, in0=vS, scalar=smalls[:, 1:2],
                                                in1=qS, op0=OP.add, op1=OP.mult)
                P.vsync(vector, v)
                P.wait(vector, "s", pd[("recip", j)])
                v = vector.tensor_tensor(out=attnmix[:, TC * j:TC * (j + 1)], in0=cS, in1=kS, op=OP.mult)
                P.inc(v, "v")
                pd[("amix", j)] = P.cnt["v"]
        st["amix_ready"] = P.cnt["v"]
        if PH_STOP == 3:
            @blk.gpsimd
            def _(gpsimd):
                P.wait(gpsimd, "v")
            return nc

        wo_t = [wv_(s * 4096, BF16, 2048).rearrange("p (c o) -> p c o", c=16) for s in range(2)]
        wo_flat = [wv_(s * 4096, BF16, 2048) for s in range(2)]
        wo_stage = [av(S[2], F32, 2048), av(S[3], F32, 2048)]
        wofree = [("t", P.cnt["t"]), ("t", P.cnt["t"])]
        wsfree = [("v", st["amix_ready"]), ("v", st["amix_ready"])]
        xo = [av(S[0], F32, 512), av(S[0] + 2048, F32, 512)]
        a1b = [av(S[1], F32, 512), av(S[1] + 2048, F32, 512)]
        xofree = [("v", st["amix_ready"])] * 2
        a1free = [("as0", 0), ("as1", 0)]

        for ot in range(NCT):
            sl = ot % 2

            @blk.sync
            def _(sync, ot=ot, sl=sl):
                sname, sval = wsfree[sl]
                P.wait(sync, sname, sval)
                src = woutT.rearrange("(c p) o -> p c o", p=128)
                P.inc(sync.dma_start(out=wo_stage[sl].rearrange("p (c o) -> p c o", c=16),
                                     in_=src[:, :, 128 * ot:128 * (ot + 1)]), f"wof{sl}", 16)
            wof_now = P.cnt[f"wof{sl}"]

            @blk.vector
            def _(vector, sl=sl, wof_now=wof_now):
                P.wait(vector, f"wof{sl}", wof_now)
                sname, sval = wofree[sl]
                P.wait(vector, sname, sval)
                v = vector.tensor_copy(out=wo_flat[sl], in_=wo_stage[sl])
                P.inc(v, "v")
            ldw_now = P.cnt["v"]
            wsfree[sl] = ("v", ldw_now)

            for tt in range(NTT):
                i = ot * NTT + tt
                bsl = i % 8
                xsl = i % 2

                @blk.sync
                def _(sync, ot=ot, tt=tt, xsl=xsl):
                    sname, sval = xofree[xsl]
                    P.wait(sync, sname, sval)
                    P.inc(sync.dma_start(out=xo[xsl],
                                         in_=xT[128 * ot:128 * (ot + 1), TW * tt:TW * (tt + 1)]), f"xs{xsl}", 16)
                x_ld = P.cnt[f"xs{xsl}"]

                @blk.tensor
                def _(tensor, ot=ot, tt=tt, sl=sl, bsl=bsl, ldw_now=ldw_now):
                    P.wait(tensor, "v", ldw_now)
                    P.wait(tensor, "v", st["amix_ready"])
                    mm_group(tensor, bsl,
                             [wo_t[sl][:, c, :] for c in range(16)],
                             [amix3[:, c, TW * tt:TW * (tt + 1)] for c in range(16)], "t")
                mm_now = P.cnt["t"]

                @blk.vector
                def _(vector, bsl=bsl, xsl=xsl, i=i, mm_now=mm_now, x_ld=x_ld):
                    P.wait(vector, "t", mm_now)
                    P.wait(vector, f"xs{xsl}", x_ld)
                    asl = i % 2
                    sname, sval = a1free[asl]
                    P.wait(vector, sname, sval)
                    v = vector.scalar_tensor_tensor(out=a1b[asl], in0=ps[bsl][:, :], scalar=HD_S,
                                                    in1=xo[xsl], op0=OP.mult, op1=OP.add)
                    P.inc(v, "v")
                bank_free[bsl] = ("v", P.cnt["v"])
                xofree[xsl] = ("v", P.cnt["v"])
                a1_rdy = P.cnt["v"]

                @blk.gpsimd
                def _(gpsimd, ot=ot, tt=tt, i=i, a1_rdy=a1_rdy):
                    asl = i % 2
                    P.wait(gpsimd, "v", a1_rdy)
                    P.inc(gpsimd.dma_start(out=a1_scr[128 * ot:128 * (ot + 1), TW * tt:TW * (tt + 1)],
                                           in_=a1b[asl]), f"as{i % 2}", 16)
                a1free[i % 2] = (f"as{i % 2}", P.cnt[f"as{i % 2}"])
            wofree[sl] = ("t", P.cnt["t"])
        st["a1_spills"] = (P.cnt["as0"], P.cnt["as1"])
        st["ph4_t"] = P.cnt["t"]
        st["ph4_v"] = P.cnt["v"]

        wg = bv(0, BF16, 8192).rearrange("p (s o) -> p s o", s=16)
        wu = bv(16384, BF16, 8192).rearrange("p (s o) -> p s o", s=16)
        pkt = av(0, I32, 512)
        dq1 = av(2048, I32, 512)
        dq2 = av(4096, I32, 512)
        dce = av(6144, I32, 512)
        dco = av(8192, I32, 512)
        dq3 = av(32768, I32, 512)
        dq4 = av(38912, I32, 512)
        gsil = av(10240, BF16, 512)
        hbf = [av(11264 + 1024 * k, BF16, 512) for k in range(4)]
        hfree = [(f"hs{i}", 0) for i in range(4)]
        gd = {"pkfree0": st["ph4_v"], "pkfree1": st["ph4_v"], "deq_i": 0,
              "sc_free0": st["ph4_v"], "sc_free1": st["ph4_v"]}
        pkt2 = av(34816, I32, 512)
        pkt_sl = [pkt, pkt2]

        @blk.vector
        def _(vector):
            P.wait(vector, "as0", st["a1_spills"][0])
            P.wait(vector, "as1", st["a1_spills"][1])
            P.wait(vector, "v", st["ph4_v"])

        def emit_dequant(src_pk, src_sc, pt, col0, ncols, dst3, ssl, npart=128):
            row0 = 128 * pt
            psl = gd["deq_i"] % 2
            gd["deq_i"] += 1
            pktb = pkt_sl[psl]

            @blk.sync
            def _(sync, psl=psl, pktb=pktb):
                P.wait(sync, "v", gd[f"pkfree{psl}"])
                P.inc(sync.dma_start(out=pktb[0:npart, 0:ncols],
                                     in_=src_pk[row0:row0 + npart, col0:col0 + ncols]), f"pk{psl}", 16)
                P.wait(sync, "v", gd[f"sc_free{ssl}"])
                P.inc(sync.dma_start(out=sct[0:npart, 512 * ssl:512 * ssl + ncols],
                                     in_=src_sc[row0:row0 + npart, col0:col0 + ncols]), f"sc{ssl}", 16)
            pk_ld = P.cnt[f"pk{psl}"]

            @blk.vector
            def _(vector, psl=psl, pktb=pktb, pk_ld=pk_ld):
                P.wait(vector, f"pk{psl}", pk_ld)
                P.wait(vector, f"sc{ssl}")
                P.wait(vector, "v", gd.get("last_deq", 0))
                pk = pktb[0:npart, 0:ncols]
                d1 = dq1[0:npart, 0:ncols]
                d2 = dq2[0:npart, 0:ncols]
                ce = dce[0:npart, 0:ncols]
                co = dco[0:npart, 0:ncols]
                d3 = dq3[0:npart, 0:ncols]
                d4 = dq4[0:npart, 0:ncols]
                vector.tensor_scalar(out=d1, in0=pk, scalar1=0x07070707, scalar2=3,
                                     op0=OP.bitwise_and, op1=OP.logical_shift_left)
                vector.tensor_scalar(out=d2, in0=pk, scalar1=0x08080808, scalar2=4,
                                     op0=OP.bitwise_and, op1=OP.logical_shift_left)
                vector.tensor_scalar(out=d3, in0=pk, scalar1=0x70707070, scalar2=1,
                                     op0=OP.bitwise_and, op1=OP.logical_shift_right)
                v = vector.tensor_scalar(out=d4, in0=pk, scalar1=0x80808080 - 2**32, scalar2=None,
                                         op0=OP.bitwise_and)
                gd[f"pkfree{psl}"] = P.inc(v, "v")
                P.wait(vector, "v")
                vector.tensor_tensor(out=ce, in0=d1, in1=d2, op=OP.bitwise_or)
                v = vector.tensor_tensor(out=co, in0=d3, in1=d4, op=OP.bitwise_or)
                P.vsync(vector, v)
                scv = sct[0:npart, 512 * ssl:512 * ssl + ncols]
                for par, csrc in ((0, ce), (1, co)):
                    fp8 = csrc.bitcast(FP8E3).rearrange("p (o k) -> p k o", k=4)
                    for k in range(4):
                        v = vector.scalar_tensor_tensor(out=dst3[0:npart, 4 * par + k, 0:ncols], in0=fp8[:, k],
                                                        scalar=4.0, in1=scv, op0=OP.mult, op1=OP.mult)
                gd[f"sc_free{ssl}"] = P.inc(v, "v")
                gd["last_deq"] = P.cnt["v"]
            return P.cnt["v"]

        wg_sl = [bv(0, BF16, 8192).rearrange("p (s o) -> p s o", s=16),
                 bv(32768, BF16, 8192).rearrange("p (s o) -> p s o", s=16)]
        wu_sl = [bv(16384, BF16, 8192).rearrange("p (s o) -> p s o", s=16),
                 bv(49152, BF16, 8192).rearrange("p (s o) -> p s o", s=16)]
        wdq_sl = [av(16384, BF16, 4096).rearrange("p (s o) -> p s o", s=8),
                  av(24576, BF16, 4096).rearrange("p (s o) -> p s o", s=8)]
        gsil2 = [gsil, av(15360, BF16, 512)]

        NCH = NHT // 4
        chunk_ready = {}
        chunk_used = {-1: st["ph4_t"], 0: st["ph4_t"]}
        dn_state = {"k": 0}

        def emit_gate_chunk(cidx):
            slq = cidx % 2
            col0 = 512 * cidx

            @blk.vector
            def _(vector):
                P.wait(vector, "t", chunk_used.get(cidx - 2, 0))
            for pt in range(2):
                emit_dequant(guT, sguT, pt, col0, 512, wg_sl[slq][:, 8 * pt:8 * pt + 8], 0)
                emit_dequant(guT, sguT, pt, I5632 + col0, 512, wu_sl[slq][:, 8 * pt:8 * pt + 8], 1)
            chunk_ready[cidx] = P.cnt["v"]

        def emit_dn_chunk():
            k = dn_state["k"]
            if k >= 24:
                return False
            pt, ch = divmod(k, 4)
            npart = 128 if pt < 5 else 64
            wsl = k % 2

            @blk.vector
            def _(vector):
                P.wait(vector, f"ws{wsl}", 128 * (k // 2))
            emit_dequant(dnT, sdnT, pt, 512 * ch, 512, wdq_sl[wsl], 0, npart=npart)
            dn_rdy = P.cnt["v"]

            @blk.gpsimd
            def _(gpsimd, pt=pt, ch=ch, npart=npart, dn_rdy=dn_rdy, wsl=wsl):
                P.wait(gpsimd, "v", dn_rdy)
                for s_ in range(8):
                    r0_ = 128 * (8 * pt + s_) if pt < 5 else 5120 + 64 * s_
                    P.inc(gpsimd.dma_start(out=wd_scr[r0_:r0_ + npart, 512 * ch:512 * (ch + 1)],
                                           in_=wdq_sl[wsl][0:npart, s_, :]), f"ws{wsl}", 16)
            dn_state["k"] = k + 1
            return True

        emit_gate_chunk(0)
        for m in range(NHT):
            cidx = m // 4
            slq = cidx % 2
            osub = m % 4
            wg = wg_sl[slq]
            wu = wu_sl[slq]
            if osub == 0 and cidx + 1 < NCH:
                emit_gate_chunk(cidx + 1)

            for tt in range(NTT):
                i = m * NTT + tt
                gb = i % 4
                ub = 4 + i % 4

                @blk.tensor
                def _(tensor, m=m, tt=tt, gb=gb, ub=ub, osub=osub, wg=wg, wu=wu, cidx=cidx):
                    P.wait(tensor, "v", chunk_ready[cidx])
                    mm_group(tensor, gb,
                             [wg[:, s, 128 * osub:128 * (osub + 1)] for s in range(16)],
                             [xn3[:, s, TW * tt:TW * (tt + 1)] for s in range(16)], "t")
                    gd[("gmm", m, tt)] = P.cnt["t"]
                    mm_group(tensor, ub,
                             [wu[:, s, 128 * osub:128 * (osub + 1)] for s in range(16)],
                             [xn3[:, s, TW * tt:TW * (tt + 1)] for s in range(16)], "t")
                    gd[("umm", m, tt)] = P.cnt["t"]
                chunk_used[cidx] = P.cnt["t"]

                @blk.scalar
                def _(scalar, m=m, tt=tt, gb=gb, i=i):
                    P.wait(scalar, "t", gd[("gmm", m, tt)])
                    P.wait(scalar, "v", gd.get(("gsil_free", i - 2), 0))
                    a = scalar.activation(out=gsil2[i % 2], in_=ps[gb][:, :], func=AF.Silu)
                    P.inc(a, "s")
                    gd[("gsil", m, tt)] = P.cnt["s"]

                @blk.vector
                def _(vector, m=m, tt=tt, ub=ub, i=i):
                    hsl = i % 4
                    P.wait(vector, "s", gd[("gsil", m, tt)])
                    P.wait(vector, "t", gd[("umm", m, tt)])
                    sname, sval = hfree[hsl]
                    P.wait(vector, sname, sval)
                    v = vector.tensor_tensor(out=hbf[hsl], in0=ps[ub][:, :], in1=gsil2[i % 2], op=OP.mult)
                    P.inc(v, "v")
                    gd[("h", m, tt)] = P.cnt["v"]
                    gd[("gsil_free", i)] = P.cnt["v"]
                bank_free[gb] = ("v", P.cnt["v"])
                bank_free[ub] = ("v", P.cnt["v"])

                @blk.gpsimd
                def _(gpsimd, m=m, tt=tt, i=i):
                    hsl = i % 4
                    P.wait(gpsimd, "v", gd[("h", m, tt)])
                    P.inc(gpsimd.dma_start(out=h_scr[128 * m:128 * (m + 1), TW * tt:TW * (tt + 1)],
                                           in_=hbf[hsl]), f"hs{hsl}", 16)
                hfree[i % 4] = (f"hs{i % 4}", P.cnt[f"hs{i % 4}"])

                if i % 2 == 1:
                    emit_dn_chunk()
        while emit_dn_chunk():
            pass
        st["xn_dead"] = P.cnt["t"]
        if PH_STOP == 5:
            @blk.gpsimd
            def _(gpsimd):
                P.wait(gpsimd, "v")
                P.wait(gpsimd, "t")
                for i4 in range(4):
                    P.wait(gpsimd, f"hs{i4}")
                P.wait(gpsimd, "ws0")
                P.wait(gpsimd, "ws1")
            return nc

        y_d = big2
        yd3 = y_d[:, :].rearrange("p (o t) -> p o t", o=16)
        hb3 = xv(0, BF16, 22528).rearrange("p (m t) -> p m t", m=NHT)
        wd_sb = [xv(45056 + s * 10240, BF16, 5120).rearrange("p (n o) -> p n o", n=40) for s in range(2)]
        wrag_e = [wrag[:, 2048 * s:2048 * s + 512].rearrange("p (q o) -> p q o", q=4) for s in range(2)]
        wrag_o = [wrag[:, 2048 * s + 512:2048 * s + 1024].rearrange("p (q o) -> p q o", q=4) for s in range(2)]
        hrag3 = hrag[:, :].rearrange("p (q t) -> p q t", q=4)
        sqv2 = [av(0, BF16, 512), av(1024, BF16, 512)]
        f0 = av(18432, F32, 2048)[0:1, :]
        f1 = av(26624, F32, 2048)[0:1, :]
        rfb = wv_(0, F32, 2048)
        dn = {}
        rms = {}
        wdfree = [("t", 0)] * 2

        for tch in range(NTT):
            @blk.sync
            def _(sync, tch=tch):
                P.wait(sync, "t", dn.get(("h_used", tch - 1), 0))
                for i4 in range(4):
                    P.wait(sync, f"hs{i4}")
                P.wait(sync, "ws0")
                P.wait(sync, "ws1")
                P.wait(sync, "t", st["xn_dead"])
                src = h_scr.rearrange("(m p) t -> p m t", p=128)
                for half in range(4):
                    P.inc(sync.dma_start(out=hb3[:, 11 * half:11 * (half + 1), :],
                                         in_=src[:, 11 * half:11 * (half + 1), TW * tch:TW * (tch + 1)]),
                          f"hq{half}", 16)
                    dn[("hq", tch, half)] = P.cnt[f"hq{half}"]
            @blk.scalar
            def _(scalar, tch=tch):
                P.wait(scalar, "t", dn.get(("h_used", tch - 1), 0))
                for i4 in range(4):
                    P.wait(scalar, f"hs{i4}")
                src2 = h_scr.rearrange("(n p) t -> p n t", p=64)
                P.inc(scalar.dma_start(out=hrag3, in_=src2[:, 81:88:2, TW * tch:TW * (tch + 1)]), "hrg", 16)
                dn[("hrg", tch)] = P.cnt["hrg"]

            for oc in range(16):
                sl = oc % 2
                b_ = oc % 6

                @blk.scalar
                def _(scalar, tch=tch, oc=oc, sl=sl):
                    sname, sval = wdfree[sl]
                    P.wait(scalar, sname, sval)
                    P.wait(scalar, "t", st["xn_dead"])
                    P.wait(scalar, "ws0")
                    P.wait(scalar, "ws1")
                    src = wd_scr.rearrange("(n p) o -> p n o", p=128)
                    P.inc(scalar.dma_start(out=wd_sb[sl][:, :, :],
                                         in_=src[:, 0:40, 128 * oc:128 * (oc + 1)]), f"wds{sl}", 16)
                    src2 = wd_scr.rearrange("(n p) o -> p n o", p=64)
                    P.inc(scalar.dma_start(out=wrag_e[sl],
                                         in_=src2[:, 80:88:2, 128 * oc:128 * (oc + 1)]), f"wds{sl}", 16)
                    P.inc(scalar.dma_start(out=wrag_o[sl],
                                         in_=src2[:, 81:88:2, 128 * oc:128 * (oc + 1)]), f"wds{sl}", 16)
                wd_ld = P.cnt[f"wds{sl}"]

                @blk.tensor
                def _(tensor, tch=tch, oc=oc, sl=sl, b_=b_):
                    P.wait(tensor, f"wds{sl}", P.cnt[f"wds{sl}"])
                    sname, sval = bank_free[b_]
                    P.wait(tensor, sname, sval)
                    mm = None
                    for n in range(40):
                        q_ = min(n // 11, 3)
                        P.wait(tensor, f"hq{q_}", dn[("hq", tch, q_)])
                        mm = tensor.matmul(ps[b_][:, :], wd_sb[sl][:, n, :], hb3[:, n, :],
                                           start=(n == 0), stop=False)
                    P.wait(tensor, "hrg", dn[("hrg", tch)])
                    for q in range(4):
                        tensor.matmul(ps[b_][:, :], wrag_e[sl][:, q, :], hb3[0:64, 40 + q, :],
                                      start=False, stop=False)
                        mm = tensor.matmul(ps[b_][:, :], wrag_o[sl][:, q, :], hrag3[:, q, :],
                                           start=False, stop=(q == 3))
                    P.inc(mm, "t")
                    dn[("mm", tch, oc)] = P.cnt["t"]
                    dn[("h_used", tch)] = P.cnt["t"]
                wdfree[sl] = ("t", P.cnt["t"])

                @blk.vector
                def _(vector, tch=tch, oc=oc, b_=b_):
                    P.wait(vector, "t", dn[("mm", tch, oc)])
                    v = vector.tensor_copy(out=yd3[:, oc, TW * tch:TW * (tch + 1)], in_=ps[b_][:, :])
                    P.inc(v, "v")
                    dn[("ev", tch, oc)] = P.cnt["v"]
                bank_free[b_] = ("v", P.cnt["v"])




        st["yd_ready"] = P.cnt["v"]

        sqb2 = [av(0, BF16, 2048), av(34816, BF16, 2048)]
        for ot in range(16):
            @blk.vector
            def _(vector, ot=ot):
                P.wait(vector, "v", st["yd_ready"], key="self_yd")
                P.wait(vector, "t", rms.get(("sq_used", ot - 2), 0))
                a = vector.tensor_tensor(out=sqb2[ot % 2], in0=yd3[:, ot, :], in1=yd3[:, ot, :], op=OP.mult)
                P.inc(a, "s")
                rms[("sq", ot)] = P.cnt["s"]

            @blk.tensor
            def _(tensor, ot=ot):
                P.wait(tensor, "s", rms[("sq", ot)])
                if ot == 0:
                    for b4 in range(4):
                        sname, sval = bank_free[b4]
                        P.wait(tensor, sname, sval)
                sq3 = sqb2[ot % 2].rearrange("p (a b) -> p a b", a=NTT)
                mm = None
                for tc_ in range(NTT):
                    mm = tensor.matmul(ps[tc_][0:1, :], onesb[:], sq3[:, tc_],
                                       start=(ot == 0), stop=(ot == 15))
                P.inc(mm, "t")
                rms[("sq_used", ot)] = P.cnt["t"]

        @blk.vector
        def _(vector):
            P.wait(vector, "t")
            P.wait(vector, "ws0")
            P.wait(vector, "ws1")
            v = None
            for tc_ in range(NTT):
                v = vector.tensor_copy(out=f0[:, 512 * tc_:512 * (tc_ + 1)], in_=ps[tc_][0:1, :])
            P.inc(v, "v")
        for b4 in range(4):
            bank_free[b4] = ("v", P.cnt["v"])
        rms_s = P.cnt["v"]
        if PH_STOP == 7:
            @blk.gpsimd
            def _(gpsimd):
                P.wait(gpsimd, "v")
                P.wait(gpsimd, "t")
            return nc
        st["dn_done"] = P.cnt["t"]

        @blk.scalar
        def _(scalar):
            P.wait(scalar, "v", rms_s)
            a0 = scalar.activation(out=f1, in_=f0, func=AF.Ln, scale=1.0 / 2048.0, bias=EPS_FFN)
            P.vsync(scalar, a0, "fs")
            a = scalar.activation(out=f0, in_=f1, func=AF.Exp, scale=-0.5)
            P.inc(a, "s")
        rf_row = P.cnt["s"]

        rfbe = {}
        for tc_ in range(NTT):
            @blk.tensor
            def _(tensor, tc_=tc_):
                if tc_ == 0:
                    P.wait(tensor, "s", rf_row)
                b = 4 + tc_ % 2
                if tc_ >= 2:
                    P.wait(tensor, "v", rfbe[tc_ - 2])
                else:
                    sname, sval = bank_free[b]
                    P.wait(tensor, sname, sval)
                mm = tensor.matmul(ps[b][:, :], onesr[:], f0[:, 512 * tc_:512 * (tc_ + 1)],
                                   start=True, stop=True)
                P.inc(mm, "t")
                rms[("rfb", tc_)] = P.cnt["t"]

            @blk.vector
            def _(vector, tc_=tc_):
                P.wait(vector, "t", rms[("rfb", tc_)])
                v = vector.tensor_copy(out=rfb[:, 512 * tc_:512 * (tc_ + 1)], in_=ps[4 + tc_ % 2][:, :])
                rfbe[tc_] = P.inc(v, "v")
        rfb_done = P.cnt["v"]

        a1l = [av(4096 + k * 2048, F32, 512) for k in range(3)]
        yob = [av(10240 + k * 2048, F32, 512) for k in range(3)]
        t2b = av(16384, F32, 512)
        t2g = av(38912, F32, 512)
        a1lfree = [("v", 0)] * 3
        yofree = [(f"ys{i}", 0) for i in range(3)]

        for ot in range(16):
            for tt in range(NTT):
                i = ot * NTT + tt
                asl = i % 3
                ysl = i % 3
                eng_v = (i % 2 == 0)

                @blk.sync
                def _(sync, ot=ot, tt=tt, asl=asl):
                    sname, sval = a1lfree[asl]
                    P.wait(sync, sname, sval)
                    P.wait(sync, "as0", st["a1_spills"][0])
                    P.wait(sync, "as1", st["a1_spills"][1])
                    P.wait(sync, "t", st["dn_done"])
                    P.inc(sync.dma_start(out=a1l[asl],
                                         in_=a1_scr[128 * ot:128 * (ot + 1), TW * tt:TW * (tt + 1)]), f"xs{asl}", 16)
                a1_ld = P.cnt[f"xs{asl}"]

                if eng_v:
                    @blk.vector
                    def _(vector, ot=ot, tt=tt, asl=asl, ysl=ysl, a1_ld=a1_ld):
                        P.wait(vector, "v", rfb_done, key="self_rfb")
                        P.wait(vector, "v")
                        P.wait(vector, "fv")
                        P.wait(vector, f"xs{asl}", a1_ld)
                        sname, sval = yofree[ysl]
                        P.wait(vector, sname, sval)
                        v = vector.tensor_tensor(out=t2b, in0=yd3[:, ot, TW * tt:TW * (tt + 1)],
                                                 in1=rfb[:, TW * tt:TW * (tt + 1)], op=OP.mult)
                        P.vsync(vector, v)
                        v = vector.tensor_tensor(out=yob[ysl], in0=t2b, in1=a1l[asl], op=OP.add)
                        P.inc(v, "v")
                    a1lfree[asl] = ("v", P.cnt["v"])
                    yo_rdy = ("v", P.cnt["v"])
                else:
                    @blk.gpsimd
                    def _(gpsimd, ot=ot, tt=tt, asl=asl, ysl=ysl, a1_ld=a1_ld):
                        P.wait(gpsimd, "v", rfb_done, key="gp_rfb")
                        P.wait(gpsimd, "g")
                        P.wait(gpsimd, "fg")
                        P.wait(gpsimd, f"xs{asl}", a1_ld)
                        sname, sval = yofree[ysl]
                        P.wait(gpsimd, sname, sval)
                        v = gpsimd.tensor_tensor(out=t2g, in0=yd3[:, ot, TW * tt:TW * (tt + 1)],
                                                 in1=rfb[:, TW * tt:TW * (tt + 1)], op=OP.mult)
                        P.vsync(gpsimd, v, "fg")
                        v = gpsimd.tensor_tensor(out=yob[ysl], in0=t2g, in1=a1l[asl], op=OP.add)
                        P.inc(v, "g")
                    a1lfree[asl] = ("g", P.cnt["g"])
                    yo_rdy = ("g", P.cnt["g"])

                @blk.scalar
                def _(scalar, ot=ot, tt=tt, ysl=ysl, yo_rdy=yo_rdy):
                    P.wait(scalar, yo_rdy[0], yo_rdy[1])
                    P.inc(scalar.dma_start(out=y_out[128 * ot:128 * (ot + 1), TW * tt:TW * (tt + 1)],
                                           in_=yob[ysl]), f"ys{ysl}", 16)
                yofree[ysl] = (f"ys{ysl}", P.cnt[f"ys{ysl}"])

        @blk.gpsimd
        def _(gpsimd):
            for i3 in range(3):
                P.wait(gpsimd, f"ys{i3}")
            if DEBUG:
                P.wait(gpsimd, "ds")

    return nc


_CACHE = {}


def kernel(L0_attn_x, L0_attn_W_qkv, L0_attn_W_out, L0_ffn_gate_up_packed,
           L0_ffn_gate_up_scales, L0_ffn_down_packed, L0_ffn_down_scales):
    x = np.asarray(L0_attn_x)
    wqkv = np.asarray(L0_attn_W_qkv)
    wout = np.asarray(L0_attn_W_out)
    gup = np.asarray(L0_ffn_gate_up_packed)
    gus = np.asarray(L0_ffn_gate_up_scales)
    dnp = np.asarray(L0_ffn_down_packed)
    dns = np.asarray(L0_ffn_down_scales)

    perm = _perm_d()
    iperm = _perm_i()
    inv = np.argsort(perm)

    wqkvT = np.ascontiguousarray(wqkv.T[perm], dtype=np.float32)
    woutT = np.ascontiguousarray(wout.T[:, perm], dtype=np.float32)
    row_order = np.concatenate([iperm, iperm + I5632])
    guT = np.ascontiguousarray(gup[row_order].T)
    sguT = np.ascontiguousarray(np.repeat(gus[:, row_order], 16, axis=0))
    dnT = np.ascontiguousarray(dnp[perm].T)
    sdnT = np.ascontiguousarray(np.repeat(dns[:, perm], 16, axis=0))

    if "nc" not in _CACHE:
        _CACHE["nc"] = build_program()
    nc = _CACHE["nc"]

    in_maps = []
    for c in range(NCORE):
        xc = np.ascontiguousarray(x[0, TC * c:TC * (c + 1), :].T[perm], dtype=np.float32)
        m8 = np.zeros((128, 8), np.float32)
        m8[:, :c] = 1.0
        in_maps.append({
            "xT": xc, "wqkvT": wqkvT, "woutT": woutT, "guT": guT, "sguT": sguT,
            "dnT": dnT, "sdnT": sdnT, "mask8": m8,
        })
    res = run_bass_kernel_spmd(nc, in_maps, core_ids=list(range(NCORE)))
    out = np.empty((1, NCORE * TC, D), np.float32)
    for c in range(NCORE):
        yc = res.results[c]["y_out"]
        out[0, TC * c:TC * (c + 1), :] = yc[inv].T
    return out
